# version 04
# baseline (speedup 1.0000x reference)
import numpy as np
import ml_dtypes
import contextlib
import concourse.bass as bass
import concourse.mybir as mybir
from concourse.bass_utils import run_bass_kernel_spmd

F32 = mybir.dt.float32
BF16 = mybir.dt.bfloat16
AF = mybir.ActivationFunctionType
ALU = mybir.AluOpType
AX = mybir.AxisListType

T = 2048
D = 1024
NT = T // 128
ND = D // 128
ZSTRIDE = 2080


def _bf(x):
    return np.asarray(x).astype(ml_dtypes.bfloat16)

def _cols(v):
    v = np.asarray(v, dtype=np.float32)
    k = v.size // 128
    return np.ascontiguousarray(v.reshape(k, 128).T)

def _t(w):
    return np.ascontiguousarray(np.asarray(w, dtype=np.float32).T)

def build_constants():
    C = {}
    C["c_ident"] = np.eye(128, dtype=np.float32)
    dh = 64
    inv = 1.0 / (10000.0 ** (np.arange(0, dh, 2, dtype=np.float64) / dh))
    fr = np.arange(T, dtype=np.float64)[:, None] * inv[None, :]
    emb = np.concatenate([fr, fr], axis=1)
    cos1 = np.cos(emb).T.astype(np.float32)
    sin1 = np.sin(emb).T.astype(np.float32)
    C["c_ropeL1"] = np.ascontiguousarray(
        np.stack([np.tile(cos1, (2, 1)), np.tile(sin1, (2, 1))]))
    R = np.zeros((64, 64), dtype=np.float32)
    for j in range(32):
        R[j, j + 32] = -1.0
        R[j + 32, j] = 1.0
    R2 = np.zeros((128, 128), dtype=np.float32)
    R2[:64, :64] = R
    R2[64:, 64:] = R
    C["c_rotH"] = _bf(R2.T)
    def rope_int(base):
        invf = 1.0 / (base ** (np.arange(0, dh, 2, dtype=np.float64) / dh))
        f = np.arange(T, dtype=np.float64)[:, None] * invf[None, :]
        c = np.cos(f).T
        s = np.sin(f).T
        c2 = np.zeros((64, T)); s2 = np.zeros((64, T))
        c2[0::2] = c; c2[1::2] = c
        s2[0::2] = -s; s2[1::2] = s
        return (np.ascontiguousarray(np.tile(c2, (2, 1)).astype(np.float32)),
                np.ascontiguousarray(np.tile(s2, (2, 1)).astype(np.float32)))
    cg, sg = rope_int(1000000.0)
    cl, sl = rope_int(10000.0)
    C["c_ropeG"] = np.stack([cg, sg])
    C["c_ropeL"] = np.stack([cl, sl])
    P = np.zeros((128, 128), dtype=np.float32)
    for i in range(64):
        P[2 * i, 2 * i + 1] = 1.0
        P[2 * i + 1, 2 * i] = 1.0
    C["c_rotP"] = _bf(P)
    pos = np.arange(T, dtype=np.float64)[:, None]
    div = np.exp(np.arange(0, dh, 2, dtype=np.float64) * -(np.log(10000.0) / dh))
    pe = np.zeros((T, dh))
    pe[:, 0::2] = np.sin(pos * div)
    pe[:, 1::2] = np.cos(pos * div)
    C["c_posL3"] = np.ascontiguousarray(np.tile(pe.T.astype(np.float32), (2, 1)))
    NEG = np.float32(-1e30)
    def band_mask(r, lo, hi):
        k = np.arange(128)[:, None]
        q = np.arange(512)[None, :]
        d = q - (r * 128 + k)
        return np.where((d >= lo) & (d <= hi), np.float32(0), NEG).astype(np.float32)
    C["c_maskL1"] = np.stack([band_mask(r, 0, 127) for r in range(-1, 4)])
    C["c_maskL2g"] = np.stack([band_mask(r, 0, 10 ** 9) for r in range(0, 4)])
    C["c_maskL2l"] = np.stack([band_mask(r, 0, 255) for r in range(-2, 4)])
    ind = np.zeros((8, 128, 16), dtype=np.float32)
    for dt in range(8):
        ind[dt, :64, 2 * dt] = 1.0
        ind[dt, 64:, 2 * dt + 1] = 1.0
    C["c_hind"] = ind
    C["c_hindT"] = np.ascontiguousarray(ind.transpose(0, 2, 1))
    ind2 = np.zeros((128, 2), dtype=np.float32)
    ind2[:64, 0] = 1.0
    ind2[64:, 1] = 1.0
    C["c_ind2"] = ind2
    C["c_ind2T"] = np.ascontiguousarray(ind2.T)
    return C

_CONSTS = build_constants()

VECCOLS = {}
def _register_veccols():
    off = 0
    for name, n in [
        ("l0a_Dskip", 8), ("l0a_ssmw", 8), ("l0a_ffnw", 8), ("l0a_loglam", 2),
        ("l1a_bo", 8), ("l2f_mur", 8), ("l2f_muk", 8), ("l2a_n1", 8),
        ("l2a_n2", 8), ("k_zero", 1), ("k_one", 1), ("k_shift", 1),
        ("k_eps5", 1), ("k_eps6", 1), ("k_neg1e30", 1), ("k_shift8", 1),
        ("l3a_bout", 8),
    ]:
        VECCOLS[name] = (off, n)
        off += n
    return off
NVC = _register_veccols()

def host_prepare(inputs):
    x = np.asarray(inputs["L0_attn_x"], dtype=np.float32)
    shared = {}
    shared["l0a_BwT"] = _t(inputs["L0_attn_B_w"])
    shared["l0a_CwT"] = _t(inputs["L0_attn_C_w"])
    shared["l0a_w1T"] = _t(inputs["L0_attn_w1"])
    w2t = _t(inputs["L0_attn_w2"])
    w2tp = np.zeros((2816, 1024), dtype=np.float32)
    w2tp[:2736] = w2t
    shared["l0a_w2T"] = w2tp
    shared["l0a_w3T"] = _t(inputs["L0_attn_w3"])
    shared["l0f_w12T"] = np.ascontiguousarray(
        np.asarray(inputs["L0_ffn_w12"], dtype=np.float32).transpose(0, 2, 1))
    shared["l0f_w3T"] = np.ascontiguousarray(
        np.asarray(inputs["L0_ffn_w3"], dtype=np.float32).transpose(0, 2, 1))
    shared["l0f_gwT"] = _t(inputs["L0_ffn_gate_w"])
    shared["l1a_WqT"] = _t(inputs["L1_attn_Wq"])
    shared["l1a_WkT"] = _t(inputs["L1_attn_Wk"])
    shared["l1a_WvT"] = _t(inputs["L1_attn_Wv"])
    shared["l1a_WoT"] = _t(inputs["L1_attn_Wo"])
    shared["l1f_Wg"] = np.ascontiguousarray(np.asarray(inputs["L1_ffn_Wg"], dtype=np.float32))
    shared["l1f_We"] = np.ascontiguousarray(np.asarray(inputs["L1_ffn_We"], dtype=np.float32))
    for k_src, k_dst in [("L2_attn_g_wq", "l2a_gwq"), ("L2_attn_g_kv_down", "l2a_gkvd"),
                         ("L2_attn_g_k_up", "l2a_gkup"), ("L2_attn_g_v_up", "l2a_gvup"),
                         ("L2_attn_g_wo", "l2a_gwo"), ("L2_attn_l_wq", "l2a_lwq"),
                         ("L2_attn_l_wk", "l2a_lwk"), ("L2_attn_l_wv", "l2a_lwv"),
                         ("L2_attn_l_wo", "l2a_lwo")]:
        shared[k_dst] = np.ascontiguousarray(np.asarray(inputs[k_src], dtype=np.float32))
    shared["l2f_WrT"] = _t(inputs["L2_ffn_W_r"])
    shared["l2f_WkT"] = _t(inputs["L2_ffn_W_k"])
    shared["l2f_WvT"] = _t(inputs["L2_ffn_W_v"])
    shared["l3a_Wqkv"] = np.ascontiguousarray(np.asarray(inputs["L3_attn_W_qkv"], dtype=np.float32))
    shared["l3a_Wout"] = np.ascontiguousarray(np.asarray(inputs["L3_attn_W_out"], dtype=np.float32))
    shared["l3f_w1"] = np.ascontiguousarray(np.asarray(inputs["L3_ffn_w1"], dtype=np.float32))
    shared["l3f_w2"] = np.ascontiguousarray(np.asarray(inputs["L3_ffn_w2"], dtype=np.float32))
    shared["l3f_rwT"] = _t(inputs["L3_ffn_router_w"])
    vcv = np.zeros((128, NVC), dtype=np.float32)
    def put(name, v):
        o, n = VECCOLS[name]
        vcv[:, o:o + n] = _cols(v)
    put("l0a_Dskip", inputs["L0_attn_D_skip"])
    put("l0a_ssmw", inputs["L0_attn_ssm_norm_w"])
    put("l0a_ffnw", inputs["L0_attn_ffn_norm_w"])
    put("l0a_loglam", inputs["L0_attn_log_lambda"])
    put("l1a_bo", inputs["L1_attn_bo"])
    put("l2f_mur", inputs["L2_ffn_mu_r"])
    put("l2f_muk", inputs["L2_ffn_mu_k"])
    put("l2a_n1", inputs["L2_attn_norm1_w"])
    put("l2a_n2", inputs["L2_attn_norm2_w"])
    vcv[:, VECCOLS["k_zero"][0]] = 0.0
    vcv[:, VECCOLS["k_one"][0]] = 1.0
    vcv[:, VECCOLS["k_shift"][0]] = -30.0
    vcv[:, VECCOLS["k_eps5"][0]] = 1e-5
    vcv[:, VECCOLS["k_eps6"][0]] = 1e-6
    vcv[:, VECCOLS["k_neg1e30"][0]] = -1e30
    vcv[:, VECCOLS["k_shift8"][0]] = -8.0
    put("l3a_bout", inputs["L3_attn_b_out"])
    shared["veccols"] = vcv
    vr = np.zeros((13, 1024), dtype=np.float32)
    vr[0] = np.asarray(inputs["L1_ffn_bg"], np.float32)
    vr[1:9] = np.asarray(inputs["L1_ffn_be"], np.float32)
    vr[9] = np.asarray(inputs["L3_attn_b_out"], np.float32)
    vr[10:13] = np.asarray(inputs["L3_attn_b_qkv"], np.float32).reshape(3, 1024)
    shared["vecrows"] = vr
    shared.update(_CONSTS)
    maps = []
    for c in range(8):
        m = dict(shared)
        m["xT"] = np.ascontiguousarray(x[c].T)
        maps.append(m)
    return maps



class Sem:
    __slots__ = ("h", "v", "hwm")
    def __init__(self, h):
        self.h = h
        self.v = 0
        self.hwm = {}

class B:
    def __init__(self, nc, blk, st):
        self.nc = nc
        self.blk = blk
        self.st = st
        self.sems = {}
        for name in ["dS", "dG", "dG0", "dG1", "pe", "act", "dve", "pool"]:
            self.sems[name] = Sem(st.enter_context(nc.semaphore(name)))

    def wait(self, eng, key, *evs):
        for ev in evs:
            if ev is None:
                continue
            s, v = ev
            if s.hwm.get(key, 0) < v:
                eng.wait_ge(s.h, v)
                s.hwm[key] = v

    def bump(self, sem_name, instr, n=1):
        s = self.sems[sem_name]
        instr.then_inc(s.h, n)
        s.v += n
        return (s, s.v)

    def ev(self, sem_name):
        s = self.sems[sem_name]
        return (s, s.v)

    def fence(self, eng, sem_name="dve"):
        s = self.sems[sem_name]
        eng.wait_ge(s.h, s.v)

    def all_compute_evs(self):
        return [self.ev(k) for k in ["pe", "act", "dve", "pool"]]

    def all_evs(self):
        return [self.ev(k) for k in ["pe", "act", "dve", "pool", "dS", "dG"]]


def sbuf(st, nc, name, shape, dtype=F32):
    return st.enter_context(nc.sbuf_tensor(name, shape, dtype))



STEP_SEQ = ["l0f", "l0a", "l0f", "l1f", "l1a", "l1f",
            "l2f", "l2a", "l2f", "l3f", "l3a", "l3f"]

def build_program(steps):
    nc = bass.Bass("TRN2", target_bir_lowering=False, debug=False)
    IN = {}
    def di(name, shape, dtype=F32):
        IN[name] = nc.dram_tensor(name, list(shape), dtype, kind="ExternalInput").ap()
        return IN[name]
    di("xT", (D, T))
    di("veccols", (128, NVC)); di("vecrows", (13, 1024))
    di("c_ident", (128, 128)); di("c_ropeL1", (2, 128, T)); di("c_rotH", (128, 128), BF16)
    di("c_ropeG", (2, 128, T)); di("c_ropeL", (2, 128, T)); di("c_rotP", (128, 128), BF16)
    di("c_posL3", (128, T)); di("c_maskL1", (5, 128, 512)); di("c_maskL2g", (4, 128, 512))
    di("c_maskL2l", (6, 128, 512)); di("c_hind", (8, 128, 16)); di("c_hindT", (8, 16, 128))
    di("c_ind2", (128, 2)); di("c_ind2T", (2, 128))
    di("l0a_BwT", (D, 256)); di("l0a_CwT", (256, D)); di("l0a_w1T", (D, 2736))
    di("l0a_w2T", (2816, D)); di("l0a_w3T", (D, 2736))
    di("l0f_w12T", (8, D, 4096)); di("l0f_w3T", (8, 2048, D)); di("l0f_gwT", (D, 8))
    for n in ["l1a_WqT", "l1a_WkT", "l1a_WvT", "l1a_WoT", "l1f_Wg",
              "l2a_gwq", "l2a_gwo", "l2a_lwq", "l2a_lwo", "l2f_WrT",
              "l2f_WkT", "l2f_WvT", "l3a_Wout"]:
        di(n, (D, D))
    di("l1f_We", (8, D, D))
    di("l2a_gkvd", (D, 256)); di("l2a_gkup", (256, 256)); di("l2a_gvup", (256, 256))
    di("l2a_lwk", (D, 256)); di("l2a_lwv", (D, 256))
    di("l3a_Wqkv", (D, 3072)); di("l3f_w1", (8, D, 4096)); di("l3f_w2", (8, 4096, D))
    di("l3f_rwT", (D, 8))
    yT = nc.dram_tensor("yT", [D, T], F32, kind="ExternalOutput").ap()
    hx = [nc.dram_tensor(f"hx{i}", [D, T], F32, kind="Internal").ap() for i in range(2)]
    hxi = nc.dram_tensor("hxi", [D, T], F32, kind="Internal").ap()

    st = contextlib.ExitStack()
    with st:
        ctx = dict(nc=nc, IN=IN, yT=yT, hx=hx, hxi=hxi)
        ctx["SA"] = sbuf(st, nc, "SA", [128, NT * T], BF16)
        ctx["SAf"] = sbuf(st, nc, "SAf", [128, ND * T], F32)
        ctx["zT"] = sbuf(st, nc, "zTb", [128, ND * ZSTRIDE], BF16)
        ctx["wb"] = [sbuf(st, nc, f"wb{i}", [128, 4096], BF16) for i in range(2)]
        ctx["arena"] = sbuf(st, nc, "AR", [128, 6144], F32)
        ctx["vcol"] = sbuf(st, nc, "vcolb", [128, NVC], F32)
        ctx["ident"] = sbuf(st, nc, "identb_", [128, 128], F32)
        ctx["identb"] = sbuf(st, nc, "identbb", [128, 128], BF16)
        ctx["onesf"] = sbuf(st, nc, "onesf", [1, 512], F32)
        ctx["onesb"] = sbuf(st, nc, "onesb", [1, 512], BF16)
        ctx["onescf"] = sbuf(st, nc, "onescf", [128, 1], F32)
        ctx["onescb"] = sbuf(st, nc, "onescb", [128, 1], BF16)
        ctx["touch"] = sbuf(st, nc, "tchb", [1, 8], F32)
        ctx["SAb"] = ctx["SAf"].ap().bitcast(BF16)
        ctx["pQ"] = st.enter_context(nc.psum_tensor("pQ", [128, 2048], F32))
        ctx["pA"] = ctx["pQ"][:, 0:512]
        ctx["pB"] = ctx["pQ"][:, 512:1024]
        ctx["pD"] = ctx["pQ"][:, 1536:2048]
        for p in ["pC", "pE", "pF"]:
            ctx[p] = st.enter_context(nc.psum_tensor(p, [128, 512], F32))
        ctx["pStat"] = st.enter_context(nc.psum_tensor("pStat", [64, 512], F32))
        blk = st.enter_context(nc.Block())
        b = B(nc, blk, st)
        ctx["b"] = b
        ctx["wbev"] = [None, None]
        ctx["pkev"] = {}
        emit_prologue(ctx)
        n = len(steps)
        for i, stp in enumerate(steps):
            name, scale = stp if isinstance(stp, tuple) else (stp, 0.5 if stp.endswith("f") else 1.0)
            src = IN["xT"] if i == 0 else hx[(i - 1) % 2]
            dst = yT if i == n - 1 else hx[i % 2]
            EMITTERS[name](ctx, src, dst, scale)
        emit_epilogue(ctx)
    return nc


def emit_prologue(ctx):
    b = ctx["b"]
    @b.blk.sync
    def _(sy):
        b.bump("dS", sy.dma_start(out=ctx["vcol"][:], in_=ctx["IN"]["veccols"]), 16)
        b.bump("dS", sy.dma_start(out=ctx["ident"][:], in_=ctx["IN"]["c_ident"]), 16)
    @b.blk.vector
    def _(v):
        b.wait(v, "dve", b.ev("dS"))
        b.bump("dve", v.tensor_copy(ctx["identb"][:], ctx["ident"][:]))
        b.bump("dve", v.memset(ctx["onesf"][:], 1.0))
        b.bump("dve", v.memset(ctx["onesb"][:], 1.0))
        b.bump("dve", v.memset(ctx["onescf"][:], 1.0))
        b.bump("dve", v.memset(ctx["onescb"][:], 1.0))
        b.fence(v)
        b.bump("dve", v.tensor_copy(ctx["touch"][:], ctx["identb"][0:1, 0:8]))
    ctx["prologue_ev"] = b.ev("dve")


def emit_epilogue(ctx):
    b = ctx["b"]
    @b.blk.scalar
    def _(s):
        b.wait(s, "act", *b.all_evs())


def pk_wait(b, t, ctx, bank):
    evs = ctx["pkev"].get(bank)
    if evs:
        b.wait(t, "pe", *evs)

def pk_set(ctx, b, bank, *sems):
    ctx["pkev"][bank] = tuple(b.ev(s) for s in sems)


def vc(ctx, name, j=0):
    o, _n = VECCOLS[name]
    return ctx["vcol"][:, o + j:o + j + 1]


def zslc(dt, tc=None, w=512):
    if tc is None:
        return slice(dt * ZSTRIDE + 1, dt * ZSTRIDE + 1 + T)
    return slice(dt * ZSTRIDE + 1 + tc * w, dt * ZSTRIDE + 1 + tc * w + w)



def emit_ln(ctx, src_hbm, rms=False, w_col_name=None, eps_name="k_eps5",
            barrier=True, want_zlo=False, resident=False):
    b = ctx["b"]
    SAf, arena = ctx["SAf"], ctx["arena"]
    SA = ctx["SA"]
    pStat, zT = ctx["pStat"], ctx["zT"]
    m_row = lambda c0, c1: arena[0:1, 4096 + c0:4096 + c1]
    v_row = lambda c0, c1: arena[0:1, c0:c1]
    s_row = lambda c0, c1: arena[0:1, 2048 + c0:2048 + c1]
    xevs = []
    if not resident:
        @b.blk.sync
        def _(sy):
            if barrier:
                b.wait(sy, "dS", *b.all_compute_evs())
            b.wait(sy, "dS", b.ev("dS"))
            for dt in range(ND):
                xevs.append(b.bump("dS", sy.dma_start(
                    out=SAf[:, dt * T:(dt + 1) * T],
                    in_=src_hbm[dt * 128:(dt + 1) * 128, :]), 16))
    else:
        xevs.append(b.ev("dve"))
    for tc in range(4):
        sq_evs = []
        @b.blk.scalar
        def _(s, tc=tc):
            b.wait(s, "act", xevs[-1], b.ev("pe"))
            for dt in range(ND):
                sq_evs.append(b.bump("act", s.activation(
                    SA[:, dt * 512:dt * 512 + 512],
                    SAf[:, dt * T + tc * 512: dt * T + tc * 512 + 512],
                    AF.Square, bias=vc(ctx, "k_zero"), scale=1.0)))
        @b.blk.tensor
        def _(t, tc=tc):
            b.wait(t, "pe", xevs[-1], ctx["prologue_ev"], b.ev("dve"))
            for dt in range(ND):
                t.matmul(pStat[0:1, :], ctx["onescf"][:, :],
                         SAf[:, dt * T + tc * 512: dt * T + tc * 512 + 512],
                         start=(dt == 0), stop=(dt == ND - 1))
                b.wait(t, "pe", sq_evs[dt])
                ee = t.matmul(pStat[32:33, :], ctx["onescb"][:, :],
                              SA[:, dt * 512:dt * 512 + 512],
                              start=(dt == 0), stop=(dt == ND - 1))
            b.bump("pe", ee)
        @b.blk.vector
        def _(v, tc=tc):
            b.wait(v, "dve", b.ev("pe"))
            v.tensor_copy(m_row(tc * 512, (tc + 1) * 512), pStat[0:1, :])
            b.bump("dve", v.tensor_copy(v_row(tc * 512, (tc + 1) * 512), pStat[32:33, :]))
    @b.blk.vector
    def _(v):
        b.wait(v, "dve", b.ev("dve"))
        if not rms:
            b.bump("dve", v.tensor_scalar(m_row(0, T), m_row(0, T), 1.0 / D, None, op0=ALU.mult))
            b.fence(v)
            b.bump("dve", v.tensor_tensor(s_row(0, T), m_row(0, T), m_row(0, T), op=ALU.mult))
            b.bump("dve", v.tensor_scalar(v_row(0, T), v_row(0, T), 1.0 / D, None, op0=ALU.mult))
            b.fence(v)
            b.bump("dve", v.tensor_tensor(v_row(0, T), v_row(0, T), s_row(0, T), op=ALU.subtract))
        else:
            b.bump("dve", v.memset(m_row(0, T), 0.0))
            b.bump("dve", v.tensor_scalar(v_row(0, T), v_row(0, T), 1.0 / D, None, op0=ALU.mult))
        b.fence(v)
        b.bump("dve", v.tensor_copy(ctx["touch"][:], v_row(0, 8)))
    @b.blk.scalar
    def _(s):
        b.wait(s, "act", b.ev("dve"))
        o_eps = VECCOLS[eps_name][0]
        b.bump("act", s.activation(v_row(0, T), v_row(0, T), AF.Sqrt,
                                   bias=ctx["vcol"][0:1, o_eps:o_eps + 1], scale=1.0))
    @b.blk.vector
    def _(v):
        b.wait(v, "dve", b.ev("act"))
        b.bump("dve", v.reciprocal(v_row(0, T), v_row(0, T)))
        b.fence(v)
        b.bump("dve", v.tensor_copy(ctx["touch"][:], v_row(0, 8)))
    rowev = b.ev("dve")
    for tc in range(4):
        @b.blk.tensor
        def _(t, tc=tc):
            b.wait(t, "pe", rowev, b.ev("dve"))
            t.matmul(ctx["pD"][:, :], ctx["onesf"][:, 0:128], m_row(tc * 512, (tc + 1) * 512),
                     start=True, stop=True)
            b.bump("pe", t.matmul(ctx["pE"][:, :], ctx["onesf"][:, 0:128],
                                  v_row(tc * 512, (tc + 1) * 512), start=True, stop=True))
        @b.blk.vector
        def _(v, tc=tc):
            b.wait(v, "dve", b.ev("pe"))
            v.tensor_copy(arena[:, tc * 512:(tc + 1) * 512], ctx["pD"][:])
            b.bump("dve", v.tensor_copy(arena[:, 2048 + tc * 512:2048 + (tc + 1) * 512],
                                        ctx["pE"][:]))
    @b.blk.vector
    def _(v):
        b.wait(v, "dve", b.ev("dve"))
        for dt in range(ND):
            if not rms:
                b.bump("dve", v.tensor_tensor(SAf[:, dt * T:(dt + 1) * T],
                                              SAf[:, dt * T:(dt + 1) * T],
                                              arena[:, 0:T], op=ALU.subtract))
                b.fence(v)
                if want_zlo:
                    b.bump("dve", v.tensor_tensor(SAf[:, dt * T:(dt + 1) * T],
                                                  SAf[:, dt * T:(dt + 1) * T],
                                                  arena[:, 2048:2048 + T], op=ALU.mult))
                    b.fence(v)
                    b.bump("dve", v.tensor_copy(zT[:, zslc(dt)], SAf[:, dt * T:(dt + 1) * T]))
                    b.fence(v)
                    b.bump("dve", v.tensor_tensor(SA[:, dt * T:(dt + 1) * T], SAf[:, dt * T:(dt + 1) * T],
                                    zT[:, zslc(dt)], op=ALU.subtract))
                else:
                    b.bump("dve", v.tensor_tensor(zT[:, zslc(dt)], SAf[:, dt * T:(dt + 1) * T],
                                    arena[:, 2048:2048 + T], op=ALU.mult))
            else:
                b.bump("dve", v.scalar_tensor_tensor(zT[:, zslc(dt)], SAf[:, dt * T:(dt + 1) * T],
                                       vc(ctx, w_col_name, dt), arena[:, 2048:2048 + T],
                                       op0=ALU.mult, op1=ALU.mult))
        b.fence(v)
        b.bump("dve", v.tensor_copy(ctx["touch"][:], zT[0:1, 1:9]))
    return b.ev("dve")



def emit_l2f(ctx, src, dst, scale):
    b = ctx["b"]
    SA, SAf, arena, zT = ctx["SA"], ctx["SAf"], ctx["arena"], ctx["zT"]
    wb, pA, pB = ctx["wb"], ctx["pA"], ctx["pB"]
    zev = emit_ln(ctx, src, rms=False)
    @b.blk.vector
    def _(v):
        b.wait(v, "dve", zev)
        for dt in range(ND):
            b.bump("dve", v.memset(zT[:, dt * ZSTRIDE:dt * ZSTRIDE + 1], 0.0))
        b.fence(v)
        for dt in range(ND):
            zc = zT[:, dt * ZSTRIDE + 1: dt * ZSTRIDE + 1 + T]
            zp = zT[:, dt * ZSTRIDE: dt * ZSTRIDE + T]
            b.fence(v)
            b.bump("dve", v.tensor_tensor(arena[:, 0:T], zc, zp, op=ALU.subtract))
            b.fence(v)
            v.scalar_tensor_tensor(SA[:, dt * T:(dt + 1) * T], arena[:, 0:T],
                                   vc(ctx, "l2f_mur", dt), zp, op0=ALU.mult, op1=ALU.add)
            b.bump("dve", v.scalar_tensor_tensor(SA[:, (8 + dt) * T:(9 + dt) * T], arena[:, 0:T],
                                   vc(ctx, "l2f_muk", dt), zp, op0=ALU.mult, op1=ALU.add))
        b.bump("dve", v.tensor_copy(ctx["touch"][:], SA[0:1, 0:8]))
    mixev = b.ev("dve")
    for wname, src_off in [("l2f_WrT", 0), ("l2f_WkT", 8 * T)]:
        for do in range(ND):
            wev = [None]
            @b.blk.gpsimd
            def _(gp, wname=wname, do=do):
                b.wait(gp, "pool", ctx["wbev"][do % 2])
                wev[0] = b.bump("dG0" if do % 2 == 0 else "dG1", gp.dma_start(
                    out=wb[do % 2][:, 0:1024].rearrange("p (a c) -> p a c", a=8),
                    in_=ctx["IN"][wname].rearrange("(a p) f -> p a f", p=128)[:, :, do * 128:(do + 1) * 128]), 16)
            for tc in range(4):
                p = pA if tc % 2 == 0 else pB
                @b.blk.tensor
                def _(t, wname=wname, src_off=src_off, do=do, tc=tc, p=p):
                    b.wait(t, "pe", wev[0], mixev, b.ev("act"), b.ev("dve"))
                    for dt in range(ND):
                        ee = t.matmul(p[:, :], wb[do % 2][:, dt * 128:(dt + 1) * 128],
                                      SA[:, src_off + dt * T + tc * 512: src_off + dt * T + tc * 512 + 512],
                                      start=(dt == 0), stop=(dt == ND - 1))
                    b.bump("pe", ee)
                ctx["wbev"][do % 2] = b.ev("pe")
                @b.blk.scalar
                def _(s, wname=wname, do=do, tc=tc, p=p):
                    b.wait(s, "act", b.ev("pe"))
                    if wname == "l2f_WrT":
                        ee = s.activation(SAf[:, do * T + tc * 512: do * T + tc * 512 + 512],
                                          p[:], AF.Sigmoid, bias=vc(ctx, "k_zero"), scale=1.0)
                    else:
                        ee = s.activation(arena[:, (tc % 2) * 512:(tc % 2) * 512 + 512],
                                          p[:], AF.Relu, bias=vc(ctx, "k_zero"), scale=1.0)
                    b.bump("act", ee)
                if wname == "l2f_WkT":
                    @b.blk.vector
                    def _(v, do=do, tc=tc):
                        b.wait(v, "dve", b.ev("act"))
                        sq = arena[:, (tc % 2) * 512:(tc % 2) * 512 + 512]
                        b.bump("dve", v.tensor_tensor(zT[:, zslc(do, tc)], sq, sq, op=ALU.mult))
    k2ev = b.ev("dve")
    rev = b.ev("act")
    for do in range(ND):
        wev = [None]
        xev = [None]
        guard = b.ev("pe")
        @b.blk.gpsimd
        def _(gp, do=do, guard=guard):
            b.wait(gp, "pool", guard)
            wev[0] = b.bump("dG0" if do % 2 == 0 else "dG1", gp.dma_start(
                out=wb[do % 2][:, 0:1024].rearrange("p (a c) -> p a c", a=8),
                in_=ctx["IN"]["l2f_WvT"].rearrange("(a p) f -> p a f", p=128)[:, :, do * 128:(do + 1) * 128]), 16)
        dguard = b.ev("dve")
        @b.blk.sync
        def _(sy, do=do, dguard=dguard):
            b.wait(sy, "dS", dguard, b.ev("dS"))
            xev[0] = b.bump("dS", sy.dma_start(
                out=arena[:, 2048 + (do % 2) * 2048: 2048 + (do % 2) * 2048 + T],
                in_=src[do * 128:(do + 1) * 128, :]), 16)
        for tc in range(4):
            p = pA if tc % 2 == 0 else pB
            @b.blk.tensor
            def _(t, do=do, tc=tc, p=p):
                b.wait(t, "pe", wev[0], k2ev, b.ev("dve"))
                for dt in range(ND):
                    ee = t.matmul(p[:, :], wb[do % 2][:, dt * 128:(dt + 1) * 128],
                                  zT[:, zslc(dt, tc)], start=(dt == 0), stop=(dt == ND - 1))
                b.bump("pe", ee)
            @b.blk.vector
            def _(v, do=do, tc=tc, p=p):
                b.wait(v, "dve", b.ev("pe"), xev[0], rev)
                b.bump("dve", v.tensor_tensor(arena[:, 1024:1536], p[:],
                                SAf[:, do * T + tc * 512: do * T + tc * 512 + 512], op=ALU.mult))
                b.fence(v)
                xs = arena[:, 2048 + (do % 2) * 2048 + tc * 512: 2048 + (do % 2) * 2048 + tc * 512 + 512]
                b.bump("dve", v.scalar_tensor_tensor(xs, arena[:, 1024:1536], scale, xs,
                                                     op0=ALU.mult, op1=ALU.add))
                b.fence(v)
                b.bump("dve", v.tensor_copy(ctx["touch"][:], xs[0:1, 0:8]))
        @b.blk.sync
        def _(sy, do=do):
            b.wait(sy, "dS", b.ev("dve"))
            b.bump("dS", sy.dma_start(out=dst[do * 128:(do + 1) * 128, :],
                                      in_=arena[:, 2048 + (do % 2) * 2048: 2048 + (do % 2) * 2048 + T]), 16)



def emit_l1f(ctx, src, dst, scale):
    b = ctx["b"]
    SA, SAf, arena, zT = ctx["SA"], ctx["SAf"], ctx["arena"], ctx["zT"]
    wb, pA, pB, pC = ctx["wb"], ctx["pA"], ctx["pB"], ctx["pC"]
    zev = emit_ln(ctx, src, rms=False, want_zlo=True)
    @b.blk.gpsimd
    def _(gp):
        b.wait(gp, "pool", zev)
        b.bump("dG", gp.dma_start(out=SA[0:1, 20480:21504],
                                  in_=ctx["IN"]["vecrows"][0:1, :]), 16)
        for e in range(8):
            b.bump("dG", gp.dma_start(out=SA[0:1, 22528 + e * 1024: 22528 + (e + 1) * 1024],
                                      in_=ctx["IN"]["vecrows"][1 + e:2 + e, :]), 16)
    @b.blk.sync
    def _(sy):
        b.wait(sy, "dS", zev)
        b.bump("dS", sy.dma_start(out=arena[0:1, 0:1024],
                                  in_=ctx["IN"]["vecrows"][0:1, :]), 16)
    @b.blk.vector
    def _(v):
        b.wait(v, "dve", b.ev("dS"), b.ev("dG"))
        b.bump("dve", v.tensor_tensor(SA[0:1, 21504:22528], arena[0:1, 0:1024],
                                      SA[0:1, 20480:21504], op=ALU.subtract))
        b.fence(v)
        b.bump("dve", v.tensor_copy(ctx["touch"][:], SA[0:1, 21504:21512]))
    rowev = b.ev("dve")
    for fc in range(2):
        hlev = [None]
        for dt in range(ND):
            wev = [None]
            dguard = b.ev("dve")
            @b.blk.sync
            def _(sy, fc=fc, dt=dt, dguard=dguard):
                b.wait(sy, "dS", dguard, b.ev("pe"))
                wev[0] = b.bump("dS", sy.dma_start(
                    out=arena[:, 1536 + (dt % 2) * 512:1536 + (dt % 2) * 512 + 512],
                    in_=ctx["IN"]["l1f_Wg"][dt * 128:(dt + 1) * 128, fc * 512:(fc + 1) * 512]), 16)
            @b.blk.vector
            def _(v, fc=fc, dt=dt):
                b.wait(v, "dve", wev[0], ctx["wbev"][0], ctx["wbev"][1])
                wsl = arena[:, 1536 + (dt % 2) * 512:1536 + (dt % 2) * 512 + 512]
                b.bump("dve", v.tensor_copy(wb[0][:, dt * 512:(dt + 1) * 512], wsl))
                b.fence(v)
                b.bump("dve", v.tensor_tensor(wb[1][:, dt * 512:(dt + 1) * 512], wsl,
                                               wb[0][:, dt * 512:(dt + 1) * 512], op=ALU.subtract))
                if dt == ND - 1:
                    b.fence(v)
                    hlev[0] = b.bump("dve", v.tensor_copy(ctx["touch"][:], wb[1][0:1, 0:8]))
        for tt in range(NT):
            bk = ["pA", "pB", "pC", "pD"][tt % 4]
            @b.blk.tensor
            def _(t, fc=fc, tt=tt, bk=bk):
                b.wait(t, "pe", hlev[0])
                pk_wait(b, t, ctx, bk)
                p = ctx[bk]
                for dt in range(ND):
                    zhi = zT[:, dt * ZSTRIDE + 1 + tt * 128: dt * ZSTRIDE + 1 + tt * 128 + 128]
                    zlo = SA[:, dt * T + tt * 128: dt * T + tt * 128 + 128]
                    t.matmul(p[:], zhi, wb[0][:, dt * 512:(dt + 1) * 512],
                             start=(dt == 0), stop=False)
                    t.matmul(p[:], zhi, wb[1][:, dt * 512:(dt + 1) * 512],
                             start=False, stop=False)
                    t.matmul(p[:], zlo, wb[0][:, dt * 512:(dt + 1) * 512],
                             start=False, stop=False)
                t.matmul(p[:], ctx["onesb"][:, 0:128], SA[0:1, 20480 + fc * 512:20480 + fc * 512 + 512],
                         start=False, stop=False)
                b.bump("pe", t.matmul(p[:], ctx["onesb"][:, 0:128],
                                      SA[0:1, 21504 + fc * 512:21504 + fc * 512 + 512],
                                      start=False, stop=True))
            @b.blk.vector
            def _(v, fc=fc, tt=tt, bk=bk):
                b.wait(v, "dve", b.ev("pe"))
                p = ctx[bk]
                b.bump("dve", v.tensor_copy(SAf[:, tt * 1024 + fc * 512: tt * 1024 + fc * 512 + 512], p[:]))
            pk_set(ctx, b, bk, "dve")
            ctx["wbev"][0] = b.ev("pe")
            ctx["wbev"][1] = b.ev("pe")
    AXX = AX.X
    for tt in range(NT):
        @b.blk.vector
        def _(v, tt=tt):
            b.wait(v, "dve", b.ev("dve"))
            sc = SAf[:, tt * 1024: (tt + 1) * 1024]
            scr = sc.rearrange("p (a b) -> p a b", b=8)
            for e in range(8):
                b.bump("dve", v.reduce_max(arena[:, 1280 + e:1281 + e], scr[:, :, e], axis=AXX))
            b.fence(v)
            b.bump("dve", v.reduce_max(arena[:, 1288:1289], arena[:, 1280:1288], axis=AXX))
            b.fence(v)
            b.bump("dve", v.tensor_scalar(arena[:, 1289:1297], arena[:, 1280:1288], arena[:, 1288:1289],
                                          None, op0=ALU.is_equal))
            ee = v.tensor_scalar(arena[:, 1297:1298], arena[:, 1288:1289], -1.0, None, op0=ALU.mult)
            b.bump("dve", ee)
        @b.blk.scalar
        def _(s, tt=tt):
            b.wait(s, "act", b.ev("dve"))
            b.bump("act", s.activation(SA[:, 16384:17408], SAf[:, tt * 1024:(tt + 1) * 1024],
                                       AF.Exp, bias=arena[:, 1297:1298], scale=1.0,
                                       accum_out=arena[:, 1298:1299]))
        @b.blk.vector
        def _(v, tt=tt):
            b.wait(v, "dve", b.ev("act"))
            b.bump("dve", v.reciprocal(arena[:, 1299:1300], arena[:, 1298:1299]))
            b.fence(v)
            ee = v.tensor_scalar(arena[:, 1024 + tt * 8:1024 + tt * 8 + 8],
                                 arena[:, 1289:1297], arena[:, 1299:1300], None, op0=ALU.mult)
            b.bump("dve", ee)
    gatev = b.ev("dve")
    BANKS = ["pA", "pB", "pC", "pD"]
    for e in range(8):
        for fc in range(2):
            ci = e * 2 + fc
            wev = [None]
            @b.blk.gpsimd
            def _(gp, e=e, fc=fc, ci=ci):
                b.wait(gp, "pool", ctx["wbev"][ci % 2])
                wev[0] = b.bump("dG0" if ci % 2 == 0 else "dG1", gp.dma_start(
                    out=wb[ci % 2][:, :].rearrange("p (a c) -> p a c", a=8),
                    in_=ctx["IN"]["l1f_We"].rearrange("e (a p) f -> e p a f", p=128)[e][:, :, fc * 512:(fc + 1) * 512]), 16)
            for tt in range(NT):
                bk = BANKS[tt % 4]
                p = ctx[bk]
                @b.blk.tensor
                def _(t, e=e, ci=ci, tt=tt, p=p, bk=bk):
                    b.wait(t, "pe", wev[0], gatev)
                    pk_wait(b, t, ctx, bk)
                    for dt in range(ND):
                        zhi = zT[:, dt * ZSTRIDE + 1 + tt * 128: dt * ZSTRIDE + 1 + tt * 128 + 128]
                        t.matmul(p[:], zhi, wb[ci % 2][:, dt * 512:(dt + 1) * 512],
                                 start=(dt == 0), stop=False)
                    b.bump("pe", t.matmul(p[:], ctx["onesb"][:, 0:128],
                                          SA[0:1, 22528 + e * 1024 + fc * 512: 22528 + e * 1024 + fc * 512 + 512],
                                          start=False, stop=True))
                ctx["wbev"][ci % 2] = b.ev("pe")
                @b.blk.vector
                def _(v, e=e, fc=fc, tt=tt, p=p):
                    b.wait(v, "dve", b.ev("pe"))
                    ysl = SAf[:, tt * 1024 + fc * 512: tt * 1024 + fc * 512 + 512]
                    ccol = arena[:, 1024 + tt * 8 + e:1024 + tt * 8 + e + 1]
                    if e == 0:
                        ee = v.tensor_scalar(ysl, p[:], 0.0, ccol, op0=ALU.max, op1=ALU.mult)
                    else:
                        tmp = SA[:, 16384 + (tt % 2) * 512:16384 + (tt % 2) * 512 + 512]
                        b.bump("dve", v.tensor_scalar(tmp, p[:], 0.0, ccol,
                                                      op0=ALU.max, op1=ALU.mult))
                        b.fence(v)
                        ee = v.tensor_tensor(ysl, ysl, tmp, op=ALU.add)
                    b.bump("dve", ee)
                pk_set(ctx, b, bk, "dve")
    emit_ytok_residual(ctx, src, dst, scale)


def emit_ytok_residual(ctx, src, dst, scale, extra_z=False):
    b = ctx["b"]
    SAf, arena = ctx["SAf"], ctx["arena"]
    pC = ctx["pC"]
    for dt in range(ND):
        xev = [None]
        dguard = b.ev("dve")
        @b.blk.sync
        def _(sy, dt=dt, dguard=dguard):
            b.wait(sy, "dS", dguard, b.ev("dS"))
            xev[0] = b.bump("dS", sy.dma_start(
                out=arena[:, (dt % 2) * 2048:(dt % 2) * 2048 + T],
                in_=src[dt * 128:(dt + 1) * 128, :]), 16)
        for tc in range(4):
            @b.blk.tensor
            def _(t, dt=dt, tc=tc):
                b.wait(t, "pe", b.ev("dve"))
                for j in range(4):
                    tt = tc * 4 + j
                    ee = t.transpose(pC[:, j * 128:(j + 1) * 128],
                                     SAf[:, tt * 1024 + dt * 128: tt * 1024 + dt * 128 + 128],
                                     ctx["ident"][:])
                b.bump("pe", ee)
            @b.blk.vector
            def _(v, dt=dt, tc=tc):
                b.wait(v, "dve", b.ev("pe"), xev[0])
                xs = arena[:, (dt % 2) * 2048 + tc * 512:(dt % 2) * 2048 + tc * 512 + 512]
                b.bump("dve", v.scalar_tensor_tensor(xs, pC[:], scale, xs,
                                                     op0=ALU.mult, op1=ALU.add))
                b.fence(v)
                if extra_z:
                    b.bump("dve", v.scalar_tensor_tensor(xs, ctx["zT"][:, zslc(dt, tc)],
                                                         scale, xs, op0=ALU.mult, op1=ALU.add))
                    b.fence(v)
                b.bump("dve", v.tensor_copy(ctx["touch"][:], xs[0:1, 0:8]))
        @b.blk.sync
        def _(sy, dt=dt):
            b.wait(sy, "dS", b.ev("dve"))
            b.bump("dS", sy.dma_start(out=dst[dt * 128:(dt + 1) * 128, :],
                                      in_=arena[:, (dt % 2) * 2048:(dt % 2) * 2048 + T]), 16)







def v65_col(cfg, tt, g):
    return cfg["v65_off"] + tt * cfg["kvh"] * 65 + g * 65

def emit_attn_core(ctx, cfg):
    b = ctx["b"]
    SA, SAb, arena, zT = ctx["SA"], ctx["SAb"], ctx["arena"], ctx["zT"]
    pC, pD, pE, pF = ctx["pC"], ctx["pD"], ctx["pE"], ctx["pF"]
    nh = cfg["nh"]
    for h in cfg.get("h_order", range(nh)):
        hb = (h % 2) * 64
        hp = h // 2
        for qc in cfg.get("qc_order", range(4)):
            kts = cfg["kt_list"](qc)
            for i, (kt, mi) in enumerate(kts):
                bk = "pE" if i % 2 == 0 else "pF"
                p = ctx[bk]
                @b.blk.tensor
                def _(t, h=h, hb=hb, qc=qc, kt=kt, p=p, bk=bk):
                    b.wait(t, "pe", cfg["ready_ev"])
                    pk_wait(b, t, ctx, bk)
                    b.bump("pe", t.matmul(
                        p[:], SA[hb:hb + 64, cfg["k_off"] + (h // 2) * T + kt * 128:
                                 cfg["k_off"] + (h // 2) * T + kt * 128 + 128],
                        SA[hb:hb + 64, cfg["q_off"] + (h // 2) * T + qc * 512:
                           cfg["q_off"] + (h // 2) * T + qc * 512 + 512],
                        start=True, stop=True))
                if mi is not None:
                    @b.blk.vector
                    def _(v, mi=mi, p=p):
                        b.wait(v, "dve", b.ev("pe"), b.ev("act"))
                        b.bump("dve", v.tensor_tensor(
                            arena[:, 5632:6144], p[:],
                            arena[:, cfg["mask_arena_off"] + mi * 512:
                                  cfg["mask_arena_off"] + (mi + 1) * 512], op=ALU.add))
                    pk_set(ctx, b, bk, "dve")
                    @b.blk.scalar
                    def _(s, i=i, kt=kt):
                        b.wait(s, "act", b.ev("dve"))
                        b.bump("act", s.activation(
                            SAb[:, cfg["pT_off"] + i * 512: cfg["pT_off"] + (i + 1) * 512],
                            arena[:, 5632:6144], AF.Exp,
                            bias=vc(ctx, cfg["shift"]), scale=1.0))
                else:
                    @b.blk.scalar
                    def _(s, i=i, p=p):
                        b.wait(s, "act", b.ev("pe"))
                        b.bump("act", s.activation(
                            SAb[:, cfg["pT_off"] + i * 512: cfg["pT_off"] + (i + 1) * 512],
                            p[:], AF.Exp, bias=vc(ctx, cfg["shift"]), scale=1.0))
                    pk_set(ctx, b, bk, "act")
            @b.blk.tensor
            def _(t, h=h, qc=qc, kts=kts):
                b.wait(t, "pe", b.ev("act"), b.ev("dve"))
                g = h // (cfg["nh"] // cfg["kvh"]) if cfg["kvh"] < cfg["nh"] else h
                for i, (kt, mi) in enumerate(kts):
                    ee = t.matmul(pC[0:65, :],
                                  SAb[:, v65_col(cfg, kt, g):v65_col(cfg, kt, g) + 65],
                                  SAb[:, cfg["pT_off"] + i * 512: cfg["pT_off"] + (i + 1) * 512],
                                  start=(i == 0), stop=(i == len(kts) - 1))
                b.bump("pe", ee)
            @b.blk.vector
            def _(v):
                b.wait(v, "dve", b.ev("pe"))
                b.bump("dve", v.reciprocal(arena[0:1, 5632:6144], pC[64:65, :]))
                b.fence(v)
                b.bump("dve", v.tensor_copy(ctx["touch"][:], arena[0:1, 5632:5640]))
            @b.blk.tensor
            def _(t):
                b.wait(t, "pe", b.ev("dve"))
                b.bump("pe", t.matmul(pD[0:64, :], ctx["onesf"][:, 0:64],
                                      arena[0:1, 5632:6144], start=True, stop=True))
            @b.blk.vector
            def _(v, h=h, hb=hb, hp=hp, qc=qc):
                b.wait(v, "dve", b.ev("pe"))
                b.bump("dve", v.tensor_copy(arena[0:64, 5120:5632], pD[0:64, :]))
                b.fence(v)
                b.bump("dve", v.tensor_tensor(
                    zT[hb:hb + 64, hp * ZSTRIDE + 1 + qc * 512: hp * ZSTRIDE + 1 + qc * 512 + 512],
                    pC[0:64, :], arena[0:64, 5120:5632], op=ALU.mult))


def emit_outproj_residual(ctx, srcs, dst, w_name, bias_col_name=None):
    b = ctx["b"]
    arena, zT, wb = ctx["arena"], ctx["zT"], ctx["wb"]
    pA, pB = ctx["pA"], ctx["pB"]
    for do in range(ND):
        wev = [None]
        xev = [None]
        @b.blk.gpsimd
        def _(gp, do=do):
            b.wait(gp, "pool", ctx["wbev"][do % 2])
            wev[0] = b.bump("dG0" if do % 2 == 0 else "dG1", gp.dma_start(
                out=wb[do % 2][:, 0:1024].rearrange("p (a c) -> p a c", a=8),
                in_=ctx["IN"][w_name].rearrange("(a p) f -> p a f", p=128)[:, :, do * 128:(do + 1) * 128]), 16)
        dguard = b.ev("dve")
        @b.blk.sync
        def _(sy, do=do, dguard=dguard):
            b.wait(sy, "dS", dguard, b.ev("dS"))
            for si, s_ in enumerate(srcs):
                xev[0] = b.bump("dS", sy.dma_start(
                    out=arena[:, si * 2048:si * 2048 + T],
                    in_=s_[do * 128:(do + 1) * 128, :]), 16)
        for tc in range(4):
            p = pA if tc % 2 == 0 else pB
            @b.blk.tensor
            def _(t, do=do, tc=tc, p=p):
                b.wait(t, "pe", wev[0], b.ev("dve"))
                for ft in range(ND):
                    ee = t.matmul(p[:], wb[do % 2][:, ft * 128:(ft + 1) * 128],
                                  zT[:, zslc(ft, tc)], start=(ft == 0), stop=(ft == ND - 1))
                b.bump("pe", ee)
            ctx["wbev"][do % 2] = b.ev("pe")
            @b.blk.vector
            def _(v, do=do, tc=tc, p=p, nsrc=len(srcs)):
                b.wait(v, "dve", b.ev("pe"), xev[0])
                xs = arena[:, tc * 512:tc * 512 + 512]
                if nsrc > 1:
                    b.bump("dve", v.tensor_tensor(xs, xs,
                                                  arena[:, 2048 + tc * 512:2048 + tc * 512 + 512],
                                                  op=ALU.add))
                    b.fence(v)
                if bias_col_name is not None:
                    b.bump("dve", v.scalar_tensor_tensor(xs, p[:], vc(ctx, bias_col_name, do),
                                                         xs, op0=ALU.add, op1=ALU.add))
                else:
                    b.bump("dve", v.scalar_tensor_tensor(xs, p[:], 0.0, xs,
                                                         op0=ALU.add, op1=ALU.add))
                b.fence(v)
                b.bump("dve", v.tensor_copy(ctx["touch"][:], xs[0:1, 0:8]))
        @b.blk.sync
        def _(sy, do=do):
            b.wait(sy, "dS", b.ev("dve"))
            b.bump("dS", sy.dma_start(out=dst[do * 128:(do + 1) * 128, :],
                                      in_=arena[:, 0:T]), 16)


def proj_fm(ctx, w_name, fcols, out_sa_off, n_f, ready_evs=(), bias_row=None,
            add_tile=None, evac_scale=None, src_tiles=None):
    b = ctx["b"]
    SA, SAb, arena, zT, wb = ctx["SA"], ctx["SAb"], ctx["arena"], ctx["zT"], ctx["wb"]
    pA, pB = ctx["pA"], ctx["pB"]
    if src_tiles is None:
        srcbuf, sbase, sstride, nk = zT, 1, ZSTRIDE, ND
    else:
        srcbuf, sbase, sstride, nk = src_tiles
    ready_evs = tuple(ready_evs) + (b.ev("dve"),)
    nchunk = (n_f + 511) // 512
    BANKS = ["pA", "pB", "pC", "pD"]
    for c in range(nchunk):
        cw = min(512, n_f - c * 512)
        wev = [None]
        ci = ctx.get("wb_ci", 0); ctx["wb_ci"] = ci + 1
        @b.blk.gpsimd
        def _(gp, c=c, cw=cw, ci=ci):
            b.wait(gp, "pool", ctx["wbev"][ci % 2])
            wev[0] = b.bump("dG0" if ci % 2 == 0 else "dG1", gp.dma_start(
                out=wb[ci % 2][:, 0:nk * cw].rearrange("p (a q) -> p a q", a=nk),
                in_=ctx["IN"][w_name].rearrange("(a p) f -> p a f", p=128)[:, :, fcols + c * 512:fcols + c * 512 + cw]), 16)
        nf_t = (cw + 127) // 128
        for fi in range(nf_t):
            fw = min(128, cw - fi * 128)
            ft = c * 4 + fi
            for tc in range(4):
                bk = BANKS[(fi * 4 + tc) % 4]
                p = ctx[bk]
                @b.blk.tensor
                def _(t, c=c, ci=ci, cw=cw, fi=fi, fw=fw, ft=ft, tc=tc, p=p, bk=bk):
                    b.wait(t, "pe", wev[0], *ready_evs)
                    pk_wait(b, t, ctx, bk)
                    for dt in range(nk):
                        ee = t.matmul(p[0:fw, 0:512],
                                      wb[ci % 2][:, dt * cw + fi * 128: dt * cw + fi * 128 + fw],
                                      srcbuf[:, sbase + dt * sstride + tc * 512:
                                             sbase + dt * sstride + tc * 512 + 512],
                                      start=(dt == 0), stop=(dt == nk - 1 and bias_row is None))
                    if bias_row is not None:
                        ee = t.matmul(p[0:fw, 0:512],
                                      SAb[0:1, bias_row + c * 512 + fi * 128:
                                          bias_row + c * 512 + fi * 128 + fw],
                                      ctx["onesb"][:, 0:512], start=False, stop=True)
                    b.bump("pe", ee)
                ctx["wbev"][ci % 2] = b.ev("pe")
                @b.blk.vector
                def _(v, c=c, fi=fi, fw=fw, ft=ft, tc=tc, p=p):
                    b.wait(v, "dve", b.ev("pe"))
                    dstsl = SA[0:fw, out_sa_off + ft * T + tc * 512: out_sa_off + ft * T + tc * 512 + 512]
                    if add_tile is not None:
                        b.bump("dve", v.tensor_tensor(dstsl, p[0:fw, 0:512],
                                                      arena[0:fw, add_tile + tc * 512:add_tile + tc * 512 + 512],
                                                      op=ALU.add))
                    elif evac_scale is not None:
                        b.bump("dve", v.tensor_scalar(dstsl, p[0:fw, 0:512], evac_scale, None,
                                                      op0=ALU.mult))
                    else:
                        b.bump("dve", v.tensor_copy(dstsl, p[0:fw, 0:512]))
                pk_set(ctx, b, bk, "dve")




def proj_v65(ctx, w_name, fcols, nh_v, cfg, ready_evs=(), bias_row=None,
             src_tiles=None):
    b = ctx["b"]
    SA, SAb, zT, wb = ctx["SA"], ctx["SAb"], ctx["zT"], ctx["wb"]
    pA, pB = ctx["pA"], ctx["pB"]
    if src_tiles is None:
        srcbuf, sbase, sstride, nk = zT, 1, ZSTRIDE, ND
    else:
        srcbuf, sbase, sstride, nk = src_tiles
    ready_evs = tuple(ready_evs) + (b.ev("dve"),)
    @b.blk.vector
    def _(v):
        b.wait(v, "dve", *ready_evs)
        for tt in range(NT):
            for h in range(nh_v):
                ee = v.memset(SAb[:, v65_col(cfg, tt, h) + 64: v65_col(cfg, tt, h) + 65], 1.0)
        b.bump("dve", ee)
        b.fence(v)
        b.bump("dve", v.tensor_copy(ctx["touch"][:], SAb[0:1, v65_col(cfg, 0, 0) + 64:v65_col(cfg, 0, 0) + 72]))
    nchunk = (nh_v * 64 + 511) // 512
    BANKS = ["pA", "pB", "pC", "pD"]
    for c in range(nchunk):
        cw = min(512, nh_v * 64 - c * 512)
        wev = [None]
        ci = ctx.get("wb_ci", 0); ctx["wb_ci"] = ci + 1
        @b.blk.gpsimd
        def _(gp, c=c, cw=cw, ci=ci):
            b.wait(gp, "pool", ctx["wbev"][ci % 2])
            wev[0] = b.bump("dG0" if ci % 2 == 0 else "dG1", gp.dma_start(
                out=wb[ci % 2][:, 0:nk * cw].rearrange("p (a q) -> p a q", a=nk),
                in_=ctx["IN"][w_name].rearrange("(a p) f -> p a f", p=128)[:, :, fcols + c * 512:fcols + c * 512 + cw]), 16)
        for tt in range(NT):
            bk = BANKS[tt % 4]
            p = ctx[bk]
            @b.blk.tensor
            def _(t, c=c, ci=ci, cw=cw, tt=tt, p=p, bk=bk):
                b.wait(t, "pe", wev[0], b.ev("dve") if tt == 0 and c == 0 else None,
                       *ready_evs)
                pk_wait(b, t, ctx, bk)
                for dt in range(nk):
                    ee = t.matmul(p[:, 0:cw],
                                  srcbuf[:, sbase + dt * sstride + tt * 128: sbase + dt * sstride + tt * 128 + 128],
                                  wb[ci % 2][:, dt * cw: dt * cw + cw],
                                  start=(dt == 0), stop=(dt == nk - 1 and bias_row is None))
                if bias_row is not None:
                    ee = t.matmul(p[:, 0:cw], ctx["onesb"][:, 0:128],
                                  SAb[0:1, bias_row + c * 512: bias_row + c * 512 + cw],
                                  start=False, stop=True)
                b.bump("pe", ee)
            ctx["wbev"][ci % 2] = b.ev("pe")
            @b.blk.vector
            def _(v, c=c, cw=cw, tt=tt, p=p):
                b.wait(v, "dve", b.ev("pe"))
                for hh in range(cw // 64):
                    h = c * 8 + hh
                    ee = v.tensor_copy(SAb[:, v65_col(cfg, tt, h):v65_col(cfg, tt, h) + 64],
                                       p[:, hh * 64:(hh + 1) * 64])
                b.bump("dve", ee)
            pk_set(ctx, b, bk, "dve")



def emit_l3a(ctx, src, dst, scale):
    b = ctx["b"]
    SA, SAb, arena = ctx["SA"], ctx["SAb"], ctx["arena"]
    zev = emit_ln(ctx, src, rms=False)
    BROW = 24832
    @b.blk.gpsimd
    def _(gp):
        b.wait(gp, "pool", zev)
        for r in range(3):
            b.bump("dG", gp.dma_start(out=SAb[0:1, BROW + r * 1024: BROW + (r + 1) * 1024],
                                      in_=ctx["IN"]["vecrows"][10 + r:11 + r, :]), 16)
    @b.blk.sync
    def _(sy):
        b.wait(sy, "dS", zev)
        b.bump("dS", sy.dma_start(out=arena[:, 2048:4096], in_=ctx["IN"]["c_posL3"]), 16)
    rdy = (b.ev("dG"), b.ev("dS"))
    cfg = dict(nh=16, kvh=16, q_off=0, k_off=16384, v65_off=0, pT_off=16640,
               shift="k_shift", mask_arena_off=None,
               kt_list=lambda qc: [(kt, None) for kt in range(16)])
    proj_fm(ctx, "l3a_Wqkv", 0, 0, 1024, ready_evs=rdy, bias_row=BROW,
            evac_scale=0.125)
    proj_fm(ctx, "l3a_Wqkv", 1024, 16384, 1024, ready_evs=rdy, bias_row=BROW + 1024,
            add_tile=2048)
    proj_v65(ctx, "l3a_Wqkv", 2048, 16, cfg, ready_evs=rdy, bias_row=BROW + 2048)
    cfg["ready_ev"] = b.ev("dve")
    emit_attn_core(ctx, cfg)
    emit_outproj_residual(ctx, [src], dst, "l3a_Wout", bias_col_name="l3a_bout")




def emit_rope(ctx, targets, cos_off, sin_off, rot_wb_off, ready_evs=()):
    b = ctx["b"]
    SA, SAb, arena, wb = ctx["SA"], ctx["SAb"], ctx["arena"], ctx["wb"]
    pD = ctx["pD"]
    TMP1, TMP2 = 24832, 25344
    for (off, ntiles) in targets:
        for tl in range(ntiles):
            for tc in range(4):
                sl = SA[:, off + tl * T + tc * 512: off + tl * T + tc * 512 + 512]
                @b.blk.tensor
                def _(t, sl=sl):
                    b.wait(t, "pe", *ready_evs)
                    b.wait(t, "pe", b.ev("dve"))
                    b.bump("pe", t.matmul(pD[:], wb[0][:, rot_wb_off:rot_wb_off + 128],
                                          sl, start=True, stop=True))
                @b.blk.vector
                def _(v, sl=sl, tc=tc):
                    b.wait(v, "dve", b.ev("pe"))
                    b.bump("dve", v.tensor_tensor(
                        SAb[:, TMP1:TMP1 + 512], sl,
                        arena[:, cos_off + tc * 512:cos_off + tc * 512 + 512], op=ALU.mult))
                    b.bump("dve", v.tensor_tensor(
                        SAb[:, TMP2:TMP2 + 512], pD[:],
                        arena[:, sin_off + tc * 512:sin_off + tc * 512 + 512], op=ALU.mult))
                    b.fence(v)
                    b.bump("dve", v.tensor_tensor(sl, SAb[:, TMP1:TMP1 + 512],
                                                  SAb[:, TMP2:TMP2 + 512], op=ALU.add))



def emit_l1a(ctx, src, dst, scale):
    b = ctx["b"]
    SA, SAb, arena, wb = ctx["SA"], ctx["SAb"], ctx["arena"], ctx["wb"]
    zev = emit_ln(ctx, src, rms=False)
    @b.blk.sync
    def _(sy):
        b.wait(sy, "dS", zev)
        b.bump("dS", sy.dma_start(out=arena[:, 2048:4096], in_=ctx["IN"]["c_ropeL1"][0]), 16)
        b.bump("dS", sy.dma_start(out=arena[:, 4096:6144], in_=ctx["IN"]["c_ropeL1"][1]), 16)
    rdy = (b.ev("dS"),)
    cfg = dict(nh=16, kvh=16, q_off=0, k_off=16384, v65_off=0, pT_off=16640,
               shift="k_shift", mask_arena_off=2048,
               kt_list=lambda qc: [(kt, kt - qc * 4 + 1)
                                   for kt in range(max(0, qc * 4 - 1), qc * 4 + 4)])
    proj_fm(ctx, "l1a_WqT", 0, 0, 1024, ready_evs=rdy, evac_scale=0.125)
    proj_fm(ctx, "l1a_WkT", 0, 16384, 1024, ready_evs=rdy)
    proj_v65(ctx, "l1a_WvT", 0, 16, cfg, ready_evs=rdy)
    rev = [None]
    @b.blk.gpsimd
    def _(gp):
        b.wait(gp, "pool", ctx["wbev"][0])
        rev[0] = b.bump("dG0", gp.dma_start(out=wb[0][:, 0:128], in_=ctx["IN"]["c_rotH"]), 16)
    emit_rope(ctx, [(0, 8), (16384, 8)], 2048, 4096, 0, ready_evs=(rev[0],))
    ctx["wbev"][0] = b.ev("pe")
    mev = [None]
    dguard = b.ev("dve")
    @b.blk.sync
    def _(sy, dguard=dguard):
        b.wait(sy, "dS", dguard)
        mev[0] = b.bump("dS", sy.dma_start(
            out=arena[:, 2048:2048 + 5 * 512].rearrange("p (a c) -> p a c", a=5),
            in_=ctx["IN"]["c_maskL1"].rearrange("a p c -> p a c")), 16)
    cfg["ready_ev"] = mev[0]
    emit_attn_core(ctx, cfg)
    emit_outproj_residual(ctx, [src], dst, "l1a_WoT", bias_col_name="l1a_bo")




def emit_qknorm(ctx, targets, qscale=None, ready_evs=()):
    b = ctx["b"]
    SA, arena = ctx["SA"], ctx["arena"]
    pD, pE = ctx["pD"], ctx["pE"]
    for (off, ntiles) in targets:
        for tl in range(ntiles):
            for tc in range(4):
                sl = SA[:, off + tl * T + tc * 512: off + tl * T + tc * 512 + 512]
                @b.blk.scalar
                def _(s, sl=sl):
                    b.wait(s, "act", b.ev("dve"), b.ev("pe"))
                    b.bump("act", s.activation(arena[:, 2048:2560], sl, AF.Square,
                                               bias=vc(ctx, "k_zero"), scale=1.0))
                @b.blk.tensor
                def _(t):
                    b.wait(t, "pe", b.ev("act"), *ready_evs)
                    b.bump("pe", t.matmul(pD[0:2, :], arena[:, 2600:2602],
                                          arena[:, 2048:2560], start=True, stop=True))
                @b.blk.vector
                def _(v):
                    b.wait(v, "dve", b.ev("pe"))
                    b.bump("dve", v.tensor_scalar(arena[0:2, 5120:5632], pD[0:2, :],
                                                  1.0 / 64, None, op0=ALU.mult))
                    b.fence(v)
                    b.bump("dve", v.tensor_copy(ctx["touch"][:], arena[0:2, 5120:5128][0:1, :]))
                @b.blk.scalar
                def _(s):
                    b.wait(s, "act", b.ev("dve"))
                    b.bump("act", s.activation(arena[0:2, 5120:5632], arena[0:2, 5120:5632],
                                               AF.Sqrt, bias=ctx["vcol"][0:2, VECCOLS["k_eps6"][0]:VECCOLS["k_eps6"][0] + 1],
                                               scale=1.0))
                @b.blk.vector
                def _(v, qscale=qscale):
                    b.wait(v, "dve", b.ev("act"))
                    b.bump("dve", v.reciprocal(arena[0:2, 5120:5632], arena[0:2, 5120:5632]))
                    b.fence(v)
                    if qscale is not None:
                        b.bump("dve", v.tensor_scalar(arena[0:2, 5120:5632], arena[0:2, 5120:5632],
                                                      qscale, None, op0=ALU.mult))
                        b.fence(v)
                    b.bump("dve", v.tensor_copy(ctx["touch"][:], arena[0:2, 5120:5128][0:1, :]))
                @b.blk.tensor
                def _(t):
                    b.wait(t, "pe", b.ev("dve"))
                    b.bump("pe", t.matmul(pE[:, :], arena[0:2, 2604:2732],
                                          arena[0:2, 5120:5632], start=True, stop=True))
                @b.blk.vector
                def _(v, sl=sl):
                    b.wait(v, "dve", b.ev("pe"))
                    b.bump("dve", v.tensor_tensor(sl, sl, pE[:], op=ALU.mult))



def emit_l2a(ctx, src, dst, scale, halves=(0, 1)):
    b = ctx["b"]
    SA, SAb, SAf, arena, zT, wb = (ctx["SA"], ctx["SAb"], ctx["SAf"],
                                   ctx["arena"], ctx["zT"], ctx["wb"])
    hxi = ctx["hxi"]
    cfgs = [("l2a_gwq", "l2a_gkvd", "l2a_gkup", "l2a_gvup", "l2a_gwo", "l2a_n1",
             "c_ropeG", None, None),
            ("l2a_lwq", "l2a_lwk", None, "l2a_lwv", "l2a_lwo", "l2a_n2",
             "c_ropeL", None, None)]
    for half in halves:
        (wq, wkd, wku, wvu, wo, nw, ropec, src_h, dsts) = cfgs[half]
        glob = (half == 0)
        if glob:
            emit_ln(ctx, src, rms=False, want_zlo=True)
            @b.blk.sync
            def _(sy):
                b.wait(sy, "dS", b.ev("dve"))
                for dt in range(ND):
                    b.bump("dS", sy.dma_start(out=hxi[dt * 128:(dt + 1) * 128, :],
                                              in_=SAf[:, dt * T:(dt + 1) * T]), 16)
            emit_ln(ctx, None, rms=True, w_col_name="l2a_n1", eps_name="k_eps6",
                    resident=True)
        else:
            emit_ln(ctx, dst, rms=True, w_col_name="l2a_n2", eps_name="k_eps6")
        cev = [None]
        @b.blk.sync
        def _(sy):
            b.wait(sy, "dS", b.ev("dve"), b.ev("pe"))
            b.bump("dS", sy.dma_start(out=arena[:, 2048:4096], in_=ctx["IN"][ropec][0]), 16)
            cev[0] = b.bump("dS", sy.dma_start(out=arena[:, 4096:6144],
                                               in_=ctx["IN"][ropec][1]), 16)
        rdy = (cev[0],)
        cfg = dict(nh=16, kvh=4, q_off=0, k_off=16384, v65_off=0, pT_off=16640,
                   shift="k_shift8", mask_arena_off=2048)
        if glob:
            cfg["kt_list"] = lambda qc: [(kt, kt - qc * 4 if kt >= qc * 4 else None)
                                         for kt in range(0, qc * 4 + 4)]
        else:
            cfg["kt_list"] = lambda qc: [(kt, kt - qc * 4 + 2)
                                         for kt in range(max(0, qc * 4 - 2), qc * 4 + 4)]
        proj_fm(ctx, wq, 0, 0, 1024, ready_evs=rdy)
        if glob:
            proj_fm(ctx, wkd, 0, 16384, 256, ready_evs=rdy)
            lat_src = (SA, 16384, T, 2)
            proj_fm(ctx, wku, 0, 20480, 256, ready_evs=rdy, src_tiles=lat_src)
            proj_v65(ctx, wvu, 0, 4, cfg, ready_evs=rdy, src_tiles=lat_src)
        else:
            proj_fm(ctx, wkd, 0, 20480, 256, ready_evs=rdy)
            proj_v65(ctx, wvu, 0, 4, cfg, ready_evs=rdy)
        rev = [None]
        @b.blk.gpsimd
        def _(gp):
            b.wait(gp, "pool", ctx["wbev"][0])
            rev[0] = b.bump("dG0", gp.dma_start(out=wb[0][:, 0:128], in_=ctx["IN"]["c_rotP"]), 16)
        emit_rope(ctx, [(0, 8), (20480, 2)], 2048, 4096, 0, ready_evs=(rev[0],))
        ctx["wbev"][0] = b.ev("pe")
        iev = [None]
        rguard = b.ev("dve")
        @b.blk.sync
        def _(sy, rguard=rguard):
            b.wait(sy, "dS", rguard)
            b.bump("dS", sy.dma_start(out=arena[:, 2600:2602], in_=ctx["IN"]["c_ind2"]), 16)
            iev[0] = b.bump("dS", sy.dma_start(out=arena[0:2, 2604:2732],
                                               in_=ctx["IN"]["c_ind2T"]), 16)
        emit_qknorm(ctx, [(0, 8)], qscale=0.125, ready_evs=(iev[0],))
        emit_qknorm(ctx, [(20480, 2)], ready_evs=(iev[0],))
        @b.blk.vector
        def _(v):
            b.wait(v, "dve", b.ev("dve"), b.ev("pe"), b.ev("act"))
            b.bump("dve", v.tensor_copy(SAb[:, 4160:6208], SA[:, 20480:22528]))
            b.bump("dve", v.tensor_copy(SAb[:, 6208:8256], SA[:, 22528:24576]))
            b.fence(v)
            for h in range(16):
                g = h // 4
                ee = v.tensor_copy(SA[(h % 2) * 64:(h % 2) * 64 + 64,
                                      16384 + (h // 2) * T:16384 + (h // 2) * T + T],
                                   SAb[(g % 2) * 64:(g % 2) * 64 + 64,
                                       4160 + (g // 2) * T:4160 + (g // 2) * T + T])
            b.bump("dve", ee)
        mev = [None]
        dguard = b.ev("dve")
        @b.blk.sync
        def _(sy, glob=glob, dguard=dguard):
            b.wait(sy, "dS", dguard)
            if glob:
                mev[0] = b.bump("dS", sy.dma_start(
                    out=arena[:, 2048:2048 + 4 * 512].rearrange("p (a c) -> p a c", a=4),
                    in_=ctx["IN"]["c_maskL2g"].rearrange("a p c -> p a c")), 16)
            else:
                mev[0] = b.bump("dS", sy.dma_start(
                    out=arena[:, 2048:2048 + 6 * 512].rearrange("p (a c) -> p a c", a=6),
                    in_=ctx["IN"]["c_maskL2l"].rearrange("a p c -> p a c")), 16)
        cfg["ready_ev"] = mev[0]
        emit_attn_core(ctx, cfg)
        if glob:
            emit_outproj_residual(ctx, [hxi], dst, wo)
        else:
            emit_outproj_residual(ctx, [src, dst], dst, wo)





def emit_l0a(ctx, src, dst, scale):
    b = ctx["b"]
    SA, SAf, arena, zT, wb = ctx["SA"], ctx["SAf"], ctx["arena"], ctx["zT"], ctx["wb"]
    pA, pB, pStat = ctx["pA"], ctx["pB"], ctx["pStat"]
    hxi = ctx["hxi"]
    zev = emit_ln(ctx, src, rms=False)
    for tc in range(4):
        sq_evs = []
        @b.blk.scalar
        def _(s, tc=tc):
            b.wait(s, "act", zev, b.ev("pe"))
            for dt in range(ND):
                sq_evs.append(b.bump("act", s.activation(
                    SA[:, 16384 + dt * 512:16384 + dt * 512 + 512],
                    zT[:, zslc(dt, tc)], AF.Square, bias=vc(ctx, "k_zero"), scale=1.0)))
        @b.blk.tensor
        def _(t, tc=tc):
            b.wait(t, "pe", b.ev("dve"))
            for dt in range(ND):
                b.wait(t, "pe", sq_evs[dt])
                ee = t.matmul(pStat[32:33, :], ctx["onescb"][:, :],
                              SA[:, 16384 + dt * 512:16384 + dt * 512 + 512],
                              start=(dt == 0), stop=(dt == ND - 1))
            b.bump("pe", ee)
        @b.blk.vector
        def _(v, tc=tc):
            b.wait(v, "dve", b.ev("pe"))
            b.bump("dve", v.tensor_copy(arena[0:1, tc * 512:(tc + 1) * 512], pStat[32:33, :]))
    @b.blk.vector
    def _(v):
        b.wait(v, "dve", b.ev("dve"))
        b.bump("dve", v.tensor_scalar(arena[0:1, 0:T], arena[0:1, 0:T], 1.0 / D, None,
                                      op0=ALU.mult))
        b.fence(v)
        b.bump("dve", v.tensor_copy(ctx["touch"][:], arena[0:1, 0:8]))
    @b.blk.scalar
    def _(s):
        b.wait(s, "act", b.ev("dve"))
        o6 = VECCOLS["k_eps6"][0]
        b.bump("act", s.activation(arena[0:1, 0:T], arena[0:1, 0:T], AF.Sqrt,
                                   bias=ctx["vcol"][0:1, o6:o6 + 1], scale=1.0))
    @b.blk.vector
    def _(v):
        b.wait(v, "dve", b.ev("act"))
        b.bump("dve", v.reciprocal(arena[0:1, 0:T], arena[0:1, 0:T]))
        b.fence(v)
        b.bump("dve", v.tensor_copy(ctx["touch"][:], arena[0:1, 0:8]))
    for tc in range(4):
        @b.blk.tensor
        def _(t, tc=tc):
            b.wait(t, "pe", b.ev("dve"))
            b.bump("pe", t.matmul(ctx["pE"][:, :], ctx["onesf"][:, 0:128],
                                  arena[0:1, tc * 512:(tc + 1) * 512], start=True, stop=True))
        @b.blk.vector
        def _(v, tc=tc):
            b.wait(v, "dve", b.ev("pe"))
            b.bump("dve", v.tensor_copy(arena[:, 2048 + tc * 512:2048 + (tc + 1) * 512],
                                        ctx["pE"][:]))
    @b.blk.vector
    def _(v):
        b.wait(v, "dve", b.ev("dve"))
        for dt in range(ND):
            ee = v.scalar_tensor_tensor(SA[:, dt * T:(dt + 1) * T], zT[:, zslc(dt)],
                                        vc(ctx, "l0a_ssmw", dt), arena[:, 2048:2048 + T],
                                        op0=ALU.mult, op1=ALU.mult)
        b.bump("dve", ee)
        b.fence(v)
        b.bump("dve", v.tensor_copy(ctx["touch"][:], SA[0:1, 0:8]))
    uev = b.ev("dve")
    wev = [None]
    @b.blk.gpsimd
    def _(gp):
        b.wait(gp, "pool", ctx["wbev"][0])
        wev[0] = b.bump("dG0", gp.dma_start(
            out=wb[0][:, 0:2048].rearrange("p (a q) -> p a q", a=8),
            in_=ctx["IN"]["l0a_BwT"].rearrange("(a p) f -> p a f", p=128)), 16)
    for fi in range(2):
        for tc in range(4):
            p = pA if tc % 2 == 0 else pB
            @b.blk.tensor
            def _(t, fi=fi, tc=tc, p=p):
                b.wait(t, "pe", wev[0], uev, b.ev("dve"))
                for dt in range(ND):
                    ee = t.matmul(p[:], wb[0][:, dt * 256 + fi * 128: dt * 256 + fi * 128 + 128],
                                  SA[:, dt * T + tc * 512: dt * T + tc * 512 + 512],
                                  start=(dt == 0), stop=(dt == ND - 1))
                b.bump("pe", ee)
            @b.blk.vector
            def _(v, fi=fi, tc=tc, p=p):
                b.wait(v, "dve", b.ev("pe"))
                b.bump("dve", v.tensor_copy(SAf[:, fi * T + tc * 512: fi * T + tc * 512 + 512],
                                            p[:]))
            ctx["wbev"][0] = b.ev("pe")
    @b.blk.scalar
    def _(s):
        b.wait(s, "act", b.ev("dve"))
        o = VECCOLS["l0a_loglam"][0]
        b.bump("act", s.activation(arena[:, 0:2], ctx["vcol"][:, o:o + 2], AF.Sigmoid,
                                   bias=vc(ctx, "k_zero"), scale=1.0))
    for nt in range(2):
        @b.blk.vector
        def _(v, nt=nt):
            b.wait(v, "dve", b.ev("act"), b.ev("pe"))
            reg = arena[:, 2048 + nt * 2048:2048 + nt * 2048 + T]
            b.bump("dve", v.memset(reg, 0.0))
            b.fence(v)
            b.bump("dve", v.tensor_scalar(reg, reg, arena[:, nt:nt + 1], None, op0=ALU.add))
            b.fence(v)
            b.bump("dve", v.tensor_tensor_scan(SA[:, 20480 + nt * T:20480 + (nt + 1) * T],
                                               reg, SAf[:, nt * T:(nt + 1) * T], 0.0,
                                               op0=ALU.mult, op1=ALU.add))
    scanev = b.ev("dve")
    for c in range(2):
        wev2 = [None]
        @b.blk.gpsimd
        def _(gp, c=c):
            b.wait(gp, "pool", ctx["wbev"][1])
            wev2[0] = b.bump("dG1", gp.dma_start(
                out=wb[1][:, 0:1024].rearrange("p (a q) -> p a q", a=2),
                in_=ctx["IN"]["l0a_CwT"].rearrange("(a p) f -> p a f", p=128)[:, :, c * 512:(c + 1) * 512]), 16)
        for fi in range(4):
            ft = c * 4 + fi
            for tc in range(4):
                p = pA if (fi * 4 + tc) % 2 == 0 else pB
                @b.blk.tensor
                def _(t, c=c, fi=fi, ft=ft, tc=tc, p=p):
                    b.wait(t, "pe", wev2[0], scanev, b.ev("dve"))
                    for a in range(2):
                        ee = t.matmul(p[:], wb[1][:, a * 512 + fi * 128: a * 512 + fi * 128 + 128],
                                      SA[:, 20480 + a * T + tc * 512: 20480 + a * T + tc * 512 + 512],
                                      start=(a == 0), stop=(a == 1))
                    b.bump("pe", ee)
                @b.blk.vector
                def _(v, ft=ft, tc=tc, p=p):
                    b.wait(v, "dve", b.ev("pe"))
                    xs = SAf[:, ft * T + tc * 512: ft * T + tc * 512 + 512]
                    b.bump("dve", v.scalar_tensor_tensor(xs, SA[:, ft * T + tc * 512: ft * T + tc * 512 + 512],
                                                         vc(ctx, "l0a_Dskip", ft), p[:],
                                                         op0=ALU.mult, op1=ALU.add))
                    b.fence(v)
                    b.bump("dve", v.tensor_tensor(xs, xs, zT[:, zslc(ft, tc)], op=ALU.add))
                    b.fence(v)
                    b.bump("dve", v.tensor_copy(ctx["touch"][:], xs[0:1, 0:8]))
                ctx["wbev"][1] = b.ev("pe")
    @b.blk.sync
    def _(sy):
        b.wait(sy, "dS", b.ev("dve"), b.ev("dS"))
        for dt in range(ND):
            b.bump("dS", sy.dma_start(out=hxi[dt * 128:(dt + 1) * 128, :],
                                      in_=SAf[:, dt * T:(dt + 1) * T]), 16)
    z2ev = emit_ln(ctx, None, rms=True, w_col_name="l0a_ffnw", eps_name="k_eps6", resident=True)
    NFT = 22
    for th in range(2):
        tbase = th * 1024
        for cch in range(6):
            cw = min(512, 2736 - cch * 512)
            nf_t = (cw + 127) // 128
            wev3 = [None, None]
            @b.blk.gpsimd
            def _(gp, cch=cch, cw=cw):
                b.wait(gp, "pool", ctx["wbev"][0], ctx["wbev"][1])
                wev3[1] = b.bump("dG0", gp.dma_start(
                    out=wb[0][:, 0:8 * cw].rearrange("p (a q) -> p a q", a=8),
                    in_=ctx["IN"]["l0a_w1T"].rearrange("(a p) f -> p a f", p=128)[:, :, cch * 512:cch * 512 + cw]), 16)
                wev3[0] = b.bump("dG1", gp.dma_start(
                    out=wb[1][:, 0:8 * cw].rearrange("p (a q) -> p a q", a=8),
                    in_=ctx["IN"]["l0a_w3T"].rearrange("(a p) f -> p a f", p=128)[:, :, cch * 512:cch * 512 + cw]), 16)
            for fi in range(nf_t):
                fw = min(128, cw - fi * 128)
                ft = cch * 4 + fi
                pQ = ctx["pQ"]
                @b.blk.tensor
                def _(t, cch=cch, cw=cw, fi=fi, fw=fw, tbase=tbase):
                    b.wait(t, "pe", wev3[0], wev3[1], z2ev)
                    for _bk in ("pQ", "pA", "pB", "pC", "pD"):
                        pk_wait(b, t, ctx, _bk)
                    for dt in range(ND):
                        for tc2 in range(2):
                            t.matmul(pQ[0:fw, tc2 * 512:(tc2 + 1) * 512],
                                     wb[0][:, dt * cw + fi * 128: dt * cw + fi * 128 + fw],
                                     zT[:, dt * ZSTRIDE + 1 + tbase + tc2 * 512: dt * ZSTRIDE + 1 + tbase + tc2 * 512 + 512],
                                     start=(dt == 0), stop=(dt == ND - 1))
                        for tc2 in range(2):
                            ee = t.matmul(pQ[0:fw, 1024 + tc2 * 512:1024 + (tc2 + 1) * 512],
                                          wb[1][:, dt * cw + fi * 128: dt * cw + fi * 128 + fw],
                                          zT[:, dt * ZSTRIDE + 1 + tbase + tc2 * 512: dt * ZSTRIDE + 1 + tbase + tc2 * 512 + 512],
                                          start=(dt == 0), stop=(dt == ND - 1))
                    b.bump("pe", ee)
                ctx["wbev"][0] = b.ev("pe")
                ctx["wbev"][1] = b.ev("pe")
                @b.blk.scalar
                def _(s, fw=fw):
                    b.wait(s, "act", b.ev("pe"), b.ev("dve"))
                    b.bump("act", s.activation(arena[0:fw, 2048:3072], pQ[0:fw, 0:1024],
                                               AF.Silu,
                                               bias=ctx["vcol"][0:fw, VECCOLS["k_zero"][0]:VECCOLS["k_zero"][0] + 1],
                                               scale=1.0))
                @b.blk.vector
                def _(v, ft=ft, fw=fw):
                    b.wait(v, "dve", b.ev("act"))
                    b.bump("dve", v.tensor_tensor(
                        SA[0:fw, ft * 1024: ft * 1024 + 1024],
                        arena[0:fw, 2048:3072], pQ[0:fw, 1024:2048], op=ALU.mult))
                pk_set(ctx, b, "pQ", "dve")
        for do in range(ND):
            wev4 = [None]
            xev = [None]
            guard = (b.ev("pe"),)
            @b.blk.gpsimd
            def _(gp, do=do):
                b.wait(gp, "pool", ctx["wbev"][do % 2])
                wev4[0] = b.bump("dG0" if do % 2 == 0 else "dG1", gp.dma_start(
                    out=wb[do % 2][:, 0:2816].rearrange("p (a q) -> p a q", a=22),
                    in_=ctx["IN"]["l0a_w2T"].rearrange("(a p) f -> p a f", p=128)[:, :, do * 128:(do + 1) * 128]), 16)
            dguard = b.ev("dve")
            @b.blk.sync
            def _(sy, do=do, th=th, dguard=dguard):
                b.wait(sy, "dS", dguard, b.ev("dS"))
                xev[0] = b.bump("dS", sy.dma_start(
                    out=arena[:, 4096 + (do % 2) * 1024:4096 + (do % 2) * 1024 + 1024],
                    in_=src[do * 128:(do + 1) * 128, th * 1024:(th + 1) * 1024]), 16)
            for tc2 in range(2):
                bk = ["pA", "pB", "pC", "pD"][(do * 2 + tc2) % 4]
                p = ctx[bk]
                @b.blk.tensor
                def _(t, do=do, tc2=tc2, p=p, bk=bk):
                    b.wait(t, "pe", wev4[0], b.ev("dve") if do == 0 and tc2 == 0 else None)
                    pk_wait(b, t, ctx, bk)
                    for a in range(22):
                        ee = t.matmul(p[:], wb[do % 2][:, a * 128:(a + 1) * 128],
                                      SA[:, a * 1024 + tc2 * 512: a * 1024 + tc2 * 512 + 512],
                                      start=(a == 0), stop=(a == 21))
                    b.bump("pe", ee)
                ctx["wbev"][do % 2] = b.ev("pe")
                @b.blk.vector
                def _(v, do=do, th=th, tc2=tc2, p=p):
                    b.wait(v, "dve", b.ev("pe"), xev[0])
                    tmp = arena[:, 2048:2560]
                    b.bump("dve", v.tensor_tensor(tmp, p[:],
                                                  SAf[:, do * T + th * 1024 + tc2 * 512: do * T + th * 1024 + tc2 * 512 + 512],
                                                  op=ALU.add))
                    b.fence(v)
                    xs = arena[:, 4096 + (do % 2) * 1024 + tc2 * 512: 4096 + (do % 2) * 1024 + tc2 * 512 + 512]
                    b.bump("dve", v.tensor_tensor(xs, xs, tmp, op=ALU.add))
                    b.fence(v)
                    b.bump("dve", v.tensor_copy(ctx["touch"][:], xs[0:1, 0:8]))
                pk_set(ctx, b, bk, "dve")
            @b.blk.sync
            def _(sy, do=do, th=th):
                b.wait(sy, "dS", b.ev("dve"))
                b.bump("dS", sy.dma_start(
                    out=dst[do * 128:(do + 1) * 128, th * 1024:(th + 1) * 1024],
                    in_=arena[:, 4096 + (do % 2) * 1024:4096 + (do % 2) * 1024 + 1024]), 16)



def emit_top2_gate(ctx, gw_name):
    b = ctx["b"]
    SA, arena, zT, wb = ctx["SA"], ctx["arena"], ctx["zT"], ctx["wb"]
    pD = ctx["pD"]
    gev = [None]
    guard = (b.ev("pe"), b.ev("dve"))
    @b.blk.sync
    def _(sy):
        b.wait(sy, "dS", *guard)
        gev[0] = b.bump("dS", sy.dma_start(
            out=arena[:, 1536:1600].rearrange("p (a c) -> p a c", a=8),
            in_=ctx["IN"][gw_name].rearrange("(a p) f -> p a f", p=128)), 16)
    hlev = [None]
    @b.blk.vector
    def _(v):
        b.wait(v, "dve", gev[0], ctx["wbev"][0], ctx["wbev"][1])
        b.bump("dve", v.tensor_copy(wb[0][:, 0:64], arena[:, 1536:1600]))
        b.fence(v)
        b.bump("dve", v.tensor_tensor(wb[1][:, 0:64], arena[:, 1536:1600], wb[0][:, 0:64],
                                      op=ALU.subtract))
        b.fence(v)
        hlev[0] = b.bump("dve", v.tensor_copy(ctx["touch"][:], wb[1][0:1, 0:8]))
    for tt in range(NT):
        @b.blk.tensor
        def _(t, tt=tt):
            b.wait(t, "pe", hlev[0], b.ev("dve"))
            for dt in range(ND):
                zhi = zT[:, dt * ZSTRIDE + 1 + tt * 128: dt * ZSTRIDE + 1 + tt * 128 + 128]
                zlo = SA[:, dt * T + tt * 128: dt * T + tt * 128 + 128]
                t.matmul(pD[:, 0:8], zhi, wb[0][:, dt * 8:(dt + 1) * 8],
                         start=(dt == 0), stop=False)
                t.matmul(pD[:, 0:8], zhi, wb[1][:, dt * 8:(dt + 1) * 8],
                         start=False, stop=False)
                ee = t.matmul(pD[:, 0:8], zlo, wb[0][:, dt * 8:(dt + 1) * 8],
                              start=False, stop=(dt == ND - 1))
            b.bump("pe", ee)
        ctx["wbev"][0] = b.ev("pe")
        ctx["wbev"][1] = b.ev("pe")
        @b.blk.vector
        def _(v, tt=tt):
            b.wait(v, "dve", b.ev("pe"))
            s8 = arena[:, 1280:1288]
            b.bump("dve", v.tensor_copy(s8, pD[:, 0:8]))
            b.fence(v)
            b.bump("dve", v.reduce_max(arena[:, 1288:1289], s8, axis=AX.X))
            b.fence(v)
            b.bump("dve", v.tensor_scalar(arena[:, 1289:1297], s8, arena[:, 1288:1289],
                                          None, op0=ALU.is_equal))
            b.fence(v)
            b.bump("dve", v.scalar_tensor_tensor(arena[:, 1297:1305], arena[:, 1289:1297],
                                                 -1e30, s8, op0=ALU.mult, op1=ALU.add))
            b.fence(v)
            b.bump("dve", v.reduce_max(arena[:, 1305:1306], arena[:, 1297:1305], axis=AX.X))
            b.fence(v)
            v.tensor_scalar(arena[:, 1306:1314], s8, arena[:, 1305:1306], None,
                            op0=ALU.is_ge)
            v.tensor_scalar(arena[:, 1314:1322], s8, arena[:, 1288:1289], None,
                            op0=ALU.subtract)
            ee = v.tensor_tensor(arena[:, 1322:1323], arena[:, 1305:1306],
                                 arena[:, 1288:1289], op=ALU.subtract)
            b.bump("dve", ee)
        @b.blk.scalar
        def _(s, tt=tt):
            b.wait(s, "act", b.ev("dve"))
            s.activation(arena[:, 1314:1322], arena[:, 1314:1322], AF.Exp,
                         bias=vc(ctx, "k_zero"), scale=1.0)
            ee = s.activation(arena[:, 1322:1323], arena[:, 1322:1323], AF.Exp,
                              bias=vc(ctx, "k_zero"), scale=1.0)
            b.bump("act", ee)
        @b.blk.vector
        def _(v, tt=tt):
            b.wait(v, "dve", b.ev("act"))
            b.bump("dve", v.tensor_scalar(arena[:, 1322:1323], arena[:, 1322:1323], 1.0,
                                          None, op0=ALU.add))
            b.fence(v)
            b.bump("dve", v.reciprocal(arena[:, 1323:1324], arena[:, 1322:1323]))
            b.fence(v)
            b.bump("dve", v.tensor_tensor(arena[:, 1314:1322], arena[:, 1314:1322],
                                          arena[:, 1306:1314], op=ALU.mult))
            b.fence(v)
            ee = v.tensor_scalar(arena[:, 1024 + tt * 8:1024 + tt * 8 + 8],
                                 arena[:, 1314:1322], arena[:, 1323:1324], None,
                                 op0=ALU.mult)
            b.bump("dve", ee)
    return b.ev("dve")



def emit_l0f(ctx, src, dst, scale):
    b = ctx["b"]
    SA, SAf, arena, zT = ctx["SA"], ctx["SAf"], ctx["arena"], ctx["zT"]
    wb, pA, pB = ctx["wb"], ctx["pA"], ctx["pB"]
    emit_ln(ctx, src, rms=False, want_zlo=True)
    gatev = emit_top2_gate(ctx, "l0f_gwT")
    BANKS = ["pA", "pB", "pC", "pD"]
    for e in range(8):
        for fchunk in range(8):
            ci = e * 8 + fchunk
            wev = [None]
            @b.blk.gpsimd
            def _(gp, e=e, fchunk=fchunk, ci=ci):
                b.wait(gp, "pool", ctx["wbev"][ci % 2])
                wev[0] = b.bump("dG0" if ci % 2 == 0 else "dG1", gp.dma_start(
                    out=wb[ci % 2][:, :].rearrange("p (a c) -> p a c", a=8),
                    in_=ctx["IN"]["l0f_w12T"].rearrange("e (a p) f -> e p a f", p=128)[e][:, :, fchunk * 512:(fchunk + 1) * 512]), 16)
            for fi in range(4):
                ft = fchunk * 4 + fi
                pQ = ctx["pQ"]
                @b.blk.tensor
                def _(t, e=e, ci=ci, fi=fi, ft=ft):
                    b.wait(t, "pe", wev[0], gatev)
                    for _bk in ("pQ", "pA", "pB", "pC", "pD"):
                        pk_wait(b, t, ctx, _bk)
                    for dt in range(ND):
                        for tc in range(4):
                            ee = t.matmul(pQ[:, tc * 512:(tc + 1) * 512],
                                          wb[ci % 2][:, dt * 512 + fi * 128: dt * 512 + fi * 128 + 128],
                                          zT[:, zslc(dt, tc)], start=(dt == 0), stop=(dt == ND - 1))
                    b.bump("pe", ee)
                ctx["wbev"][ci % 2] = b.ev("pe")
                if ft < 16:
                    @b.blk.scalar
                    def _(s, ft=ft):
                        b.wait(s, "act", b.ev("pe"))
                        b.bump("act", s.activation(
                            SA[:, ft * T: ft * T + T], pQ[:, :],
                            AF.Silu, bias=vc(ctx, "k_zero"), scale=1.0))
                    pk_set(ctx, b, "pQ", "act")
                else:
                    @b.blk.vector
                    def _(v, ft=ft):
                        b.wait(v, "dve", b.ev("pe"), b.ev("act"))
                        hs = SA[:, (ft - 16) * T: (ft - 16) * T + T]
                        b.bump("dve", v.tensor_tensor(hs, hs, pQ[:, :], op=ALU.mult))
                    pk_set(ctx, b, "pQ", "dve")
        for dc in range(2):
            wev = [None, None]
            @b.blk.gpsimd
            def _(gp, e=e, dc=dc):
                b.wait(gp, "pool", ctx["wbev"][0], ctx["wbev"][1])
                wev[0] = b.bump("dG0", gp.dma_start(
                    out=wb[0][:, :].rearrange("p (a c) -> p a c", a=8),
                    in_=ctx["IN"]["l0f_w3T"].rearrange("e (a p) f -> e p a f", p=128)[e][:, 0:8, dc * 512:(dc + 1) * 512]), 16)
                wev[1] = b.bump("dG1", gp.dma_start(
                    out=wb[1][:, :].rearrange("p (a c) -> p a c", a=8),
                    in_=ctx["IN"]["l0f_w3T"].rearrange("e (a p) f -> e p a f", p=128)[e][:, 8:16, dc * 512:(dc + 1) * 512]), 16)
            for tt in range(NT):
                bk = BANKS[tt % 4]
                p = ctx[bk]
                @b.blk.tensor
                def _(t, e=e, dc=dc, tt=tt, p=p, bk=bk):
                    b.wait(t, "pe", wev[0], wev[1], b.ev("dve") if tt == 0 and dc == 0 else None)
                    pk_wait(b, t, ctx, bk)
                    for ht in range(16):
                        ee = t.matmul(p[:], SA[:, ht * T + tt * 128: ht * T + tt * 128 + 128],
                                      wb[ht // 8][:, (ht % 8) * 512:(ht % 8) * 512 + 512],
                                      start=(ht == 0), stop=(ht == 15))
                    b.bump("pe", ee)
                ctx["wbev"][0] = b.ev("pe")
                ctx["wbev"][1] = b.ev("pe")
                @b.blk.vector
                def _(v, e=e, dc=dc, tt=tt, p=p):
                    b.wait(v, "dve", b.ev("pe"))
                    ysl = SAf[:, tt * 1024 + dc * 512: tt * 1024 + dc * 512 + 512]
                    ccol = arena[:, 1024 + tt * 8 + e:1024 + tt * 8 + e + 1]
                    if e == 0:
                        ee = v.tensor_scalar(ysl, p[:], ccol, None, op0=ALU.mult)
                    else:
                        tmp = arena[:, 1600 + (tt % 2) * 512:1600 + (tt % 2) * 512 + 512]
                        b.bump("dve", v.tensor_scalar(tmp, p[:], ccol, None, op0=ALU.mult))
                        b.fence(v)
                        ee = v.tensor_tensor(ysl, ysl, tmp, op=ALU.add)
                    b.bump("dve", ee)
                pk_set(ctx, b, "dve" if False else bk, "dve")
    emit_ytok_residual(ctx, src, dst, scale)



def emit_l3f(ctx, src, dst, scale):
    b = ctx["b"]
    SA, SAf, arena, zT = ctx["SA"], ctx["SAf"], ctx["arena"], ctx["zT"]
    wb, pA, pB = ctx["wb"], ctx["pA"], ctx["pB"]
    emit_ln(ctx, src, rms=False, want_zlo=True)
    gatev = emit_top2_gate(ctx, "l3f_rwT")
    BANKS = ["pA", "pB", "pC", "pD"]
    for e in range(8):
        for fh in range(2):
            for fchunk in range(4):
                ci = (e * 2 + fh) * 4 + fchunk
                wev = [None]
                @b.blk.gpsimd
                def _(gp, e=e, fh=fh, fchunk=fchunk, ci=ci):
                    b.wait(gp, "pool", ctx["wbev"][ci % 2])
                    wev[0] = b.bump("dG0" if ci % 2 == 0 else "dG1", gp.dma_start(
                        out=wb[ci % 2][:, :].rearrange("p (a c) -> p a c", a=8),
                        in_=ctx["IN"]["l3f_w1"].rearrange("e (a p) f -> e p a f", p=128)[e][:, :, fh * 2048 + fchunk * 512: fh * 2048 + (fchunk + 1) * 512]), 16)
                for fi in range(4):
                    ft = fchunk * 4 + fi
                    pQ = ctx["pQ"]
                    @b.blk.tensor
                    def _(t, e=e, ci=ci, fi=fi):
                        b.wait(t, "pe", wev[0], gatev)
                        for _bk in ("pQ", "pA", "pB", "pC", "pD"):
                            pk_wait(b, t, ctx, _bk)
                        for dt in range(ND):
                            for tc in range(4):
                                ee = t.matmul(pQ[:, tc * 512:(tc + 1) * 512],
                                              wb[ci % 2][:, dt * 512 + fi * 128: dt * 512 + fi * 128 + 128],
                                              zT[:, zslc(dt, tc)], start=(dt == 0), stop=(dt == ND - 1))
                        b.bump("pe", ee)
                    ctx["wbev"][ci % 2] = b.ev("pe")
                    @b.blk.scalar
                    def _(s, ft=ft):
                        b.wait(s, "act", b.ev("pe"))
                        b.bump("act", s.activation(
                            SA[:, ft * T: ft * T + T], pQ[:, :],
                            AF.Silu, bias=vc(ctx, "k_zero"), scale=1.0))
                    pk_set(ctx, b, "pQ", "act")
            for dc in range(2):
                wev = [None, None]
                @b.blk.gpsimd
                def _(gp, e=e, fh=fh, dc=dc):
                    b.wait(gp, "pool", ctx["wbev"][0], ctx["wbev"][1])
                    wev[0] = b.bump("dG0", gp.dma_start(
                        out=wb[0][:, :].rearrange("p (a c) -> p a c", a=8),
                        in_=ctx["IN"]["l3f_w2"].rearrange("e (a p) f -> e p a f", p=128)[e][:, fh * 16:fh * 16 + 8, dc * 512:(dc + 1) * 512]), 16)
                    wev[1] = b.bump("dG1", gp.dma_start(
                        out=wb[1][:, :].rearrange("p (a c) -> p a c", a=8),
                        in_=ctx["IN"]["l3f_w2"].rearrange("e (a p) f -> e p a f", p=128)[e][:, fh * 16 + 8:fh * 16 + 16, dc * 512:(dc + 1) * 512]), 16)
                for tt in range(NT):
                    bk = BANKS[tt % 4]
                    p = ctx[bk]
                    @b.blk.tensor
                    def _(t, e=e, dc=dc, tt=tt, p=p, bk=bk):
                        b.wait(t, "pe", wev[0], wev[1],
                               b.ev("act") if tt == 0 and dc == 0 else None)
                        pk_wait(b, t, ctx, bk)
                        for ht in range(16):
                            ee = t.matmul(p[:], SA[:, ht * T + tt * 128: ht * T + tt * 128 + 128],
                                          wb[ht // 8][:, (ht % 8) * 512:(ht % 8) * 512 + 512],
                                          start=(ht == 0), stop=(ht == 15))
                        b.bump("pe", ee)
                    ctx["wbev"][0] = b.ev("pe")
                    ctx["wbev"][1] = b.ev("pe")
                    @b.blk.vector
                    def _(v, e=e, fh=fh, dc=dc, tt=tt, p=p):
                        b.wait(v, "dve", b.ev("pe"))
                        ysl = SAf[:, tt * 1024 + dc * 512: tt * 1024 + dc * 512 + 512]
                        ccol = arena[:, 1024 + tt * 8 + e:1024 + tt * 8 + e + 1]
                        if e == 0 and fh == 0:
                            ee = v.tensor_scalar(ysl, p[:], ccol, None, op0=ALU.mult)
                        else:
                            tmp = arena[:, 1600 + (tt % 2) * 512:1600 + (tt % 2) * 512 + 512]
                            b.bump("dve", v.tensor_scalar(tmp, p[:], ccol, None, op0=ALU.mult))
                            b.fence(v)
                            ee = v.tensor_tensor(ysl, ysl, tmp, op=ALU.add)
                        b.bump("dve", ee)
                    pk_set(ctx, b, bk, "dve")
    emit_ytok_residual(ctx, src, dst, scale, extra_z=True)


EMITTERS = {"l2f": emit_l2f, "l1f": emit_l1f, "l0f": emit_l0f, "l3f": emit_l3f,
            "l3a": emit_l3a, "l1a": emit_l1a, "l2a": emit_l2a, "l0a": emit_l0a}



def kernel(**inputs):
    maps = host_prepare(inputs)
    nc = build_program(STEP_SEQ)
    res = run_bass_kernel_spmd(nc, maps, core_ids=list(range(8)))
    out = np.stack([np.asarray(r["yT"], dtype=np.float32).T for r in res.results])
    return np.ascontiguousarray(out)


# version 05
# speedup vs baseline: 1.1216x; 1.1216x over previous
import numpy as np
import ml_dtypes
import contextlib
import concourse.bass as bass
import concourse.mybir as mybir
from concourse.bass_utils import run_bass_kernel_spmd

F32 = mybir.dt.float32
BF16 = mybir.dt.bfloat16
AF = mybir.ActivationFunctionType
ALU = mybir.AluOpType
AX = mybir.AxisListType

T = 2048
D = 1024
NT = T // 128
ND = D // 128
ZSTRIDE = 2080


def _bf(x):
    return np.asarray(x).astype(ml_dtypes.bfloat16)

def _cols(v):
    v = np.asarray(v, dtype=np.float32)
    k = v.size // 128
    return np.ascontiguousarray(v.reshape(k, 128).T)

def _t(w):
    return np.ascontiguousarray(np.asarray(w, dtype=np.float32).T)

def build_constants():
    C = {}
    C["c_ident"] = np.eye(128, dtype=np.float32)
    dh = 64
    inv = 1.0 / (10000.0 ** (np.arange(0, dh, 2, dtype=np.float64) / dh))
    fr = np.arange(T, dtype=np.float64)[:, None] * inv[None, :]
    emb = np.concatenate([fr, fr], axis=1)
    cos1 = np.cos(emb).T.astype(np.float32)
    sin1 = np.sin(emb).T.astype(np.float32)
    C["c_ropeL1"] = np.ascontiguousarray(
        np.stack([np.tile(cos1, (2, 1)), np.tile(sin1, (2, 1))]))
    R = np.zeros((64, 64), dtype=np.float32)
    for j in range(32):
        R[j, j + 32] = -1.0
        R[j + 32, j] = 1.0
    R2 = np.zeros((128, 128), dtype=np.float32)
    R2[:64, :64] = R
    R2[64:, 64:] = R
    C["c_rotH"] = _bf(R2.T)
    def rope_int(base):
        invf = 1.0 / (base ** (np.arange(0, dh, 2, dtype=np.float64) / dh))
        f = np.arange(T, dtype=np.float64)[:, None] * invf[None, :]
        c = np.cos(f).T
        s = np.sin(f).T
        c2 = np.zeros((64, T)); s2 = np.zeros((64, T))
        c2[0::2] = c; c2[1::2] = c
        s2[0::2] = -s; s2[1::2] = s
        return (np.ascontiguousarray(np.tile(c2, (2, 1)).astype(np.float32)),
                np.ascontiguousarray(np.tile(s2, (2, 1)).astype(np.float32)))
    cg, sg = rope_int(1000000.0)
    cl, sl = rope_int(10000.0)
    C["c_ropeG"] = np.stack([cg, sg])
    C["c_ropeL"] = np.stack([cl, sl])
    P = np.zeros((128, 128), dtype=np.float32)
    for i in range(64):
        P[2 * i, 2 * i + 1] = 1.0
        P[2 * i + 1, 2 * i] = 1.0
    C["c_rotP"] = _bf(P)
    pos = np.arange(T, dtype=np.float64)[:, None]
    div = np.exp(np.arange(0, dh, 2, dtype=np.float64) * -(np.log(10000.0) / dh))
    pe = np.zeros((T, dh))
    pe[:, 0::2] = np.sin(pos * div)
    pe[:, 1::2] = np.cos(pos * div)
    C["c_posL3"] = np.ascontiguousarray(np.tile(pe.T.astype(np.float32), (2, 1)))
    NEG = np.float32(-1e30)
    def band_mask(r, lo, hi):
        k = np.arange(128)[:, None]
        q = np.arange(512)[None, :]
        d = q - (r * 128 + k)
        return np.where((d >= lo) & (d <= hi), np.float32(0), NEG).astype(np.float32)
    C["c_maskL1"] = np.stack([band_mask(r, 0, 127) for r in range(-1, 4)])
    C["c_maskL2g"] = np.stack([band_mask(r, 0, 10 ** 9) for r in range(0, 4)])
    C["c_maskL2l"] = np.stack([band_mask(r, 0, 255) for r in range(-2, 4)])
    ind = np.zeros((8, 128, 16), dtype=np.float32)
    for dt in range(8):
        ind[dt, :64, 2 * dt] = 1.0
        ind[dt, 64:, 2 * dt + 1] = 1.0
    C["c_hind"] = ind
    C["c_hindT"] = np.ascontiguousarray(ind.transpose(0, 2, 1))
    ind2 = np.zeros((128, 2), dtype=np.float32)
    ind2[:64, 0] = 1.0
    ind2[64:, 1] = 1.0
    C["c_ind2"] = ind2
    C["c_ind2T"] = np.ascontiguousarray(ind2.T)
    return C

_CONSTS = build_constants()

VECCOLS = {}
def _register_veccols():
    off = 0
    for name, n in [
        ("l0a_Dskip", 8), ("l0a_ssmw", 8), ("l0a_ffnw", 8), ("l0a_loglam", 2),
        ("l1a_bo", 8), ("l2f_mur", 8), ("l2f_muk", 8), ("l2a_n1", 8),
        ("l2a_n2", 8), ("k_zero", 1), ("k_one", 1), ("k_shift", 1),
        ("k_eps5", 1), ("k_eps6", 1), ("k_neg1e30", 1), ("k_shift8", 1),
        ("l3a_bout", 8),
    ]:
        VECCOLS[name] = (off, n)
        off += n
    return off
NVC = _register_veccols()

def host_prepare(inputs):
    x = np.asarray(inputs["L0_attn_x"], dtype=np.float32)
    shared = {}
    shared["l0a_BwT"] = _t(inputs["L0_attn_B_w"])
    shared["l0a_CwT"] = _t(inputs["L0_attn_C_w"])
    shared["l0a_w1T"] = _t(inputs["L0_attn_w1"])
    w2t = _t(inputs["L0_attn_w2"])
    w2tp = np.zeros((2816, 1024), dtype=np.float32)
    w2tp[:2736] = w2t
    shared["l0a_w2T"] = w2tp
    shared["l0a_w3T"] = _t(inputs["L0_attn_w3"])
    shared["l0f_w12T"] = np.ascontiguousarray(
        np.asarray(inputs["L0_ffn_w12"], dtype=np.float32).transpose(0, 2, 1))
    shared["l0f_w3T"] = np.ascontiguousarray(
        np.asarray(inputs["L0_ffn_w3"], dtype=np.float32).transpose(0, 2, 1))
    shared["l0f_gwT"] = _t(inputs["L0_ffn_gate_w"])
    shared["l1a_WqT"] = _t(inputs["L1_attn_Wq"])
    shared["l1a_WkT"] = _t(inputs["L1_attn_Wk"])
    shared["l1a_WvT"] = _t(inputs["L1_attn_Wv"])
    shared["l1a_WoT"] = _t(inputs["L1_attn_Wo"])
    shared["l1f_Wg"] = np.ascontiguousarray(np.asarray(inputs["L1_ffn_Wg"], dtype=np.float32))
    shared["l1f_We"] = np.ascontiguousarray(np.asarray(inputs["L1_ffn_We"], dtype=np.float32))
    for k_src, k_dst in [("L2_attn_g_wq", "l2a_gwq"), ("L2_attn_g_kv_down", "l2a_gkvd"),
                         ("L2_attn_g_k_up", "l2a_gkup"), ("L2_attn_g_v_up", "l2a_gvup"),
                         ("L2_attn_g_wo", "l2a_gwo"), ("L2_attn_l_wq", "l2a_lwq"),
                         ("L2_attn_l_wk", "l2a_lwk"), ("L2_attn_l_wv", "l2a_lwv"),
                         ("L2_attn_l_wo", "l2a_lwo")]:
        shared[k_dst] = np.ascontiguousarray(np.asarray(inputs[k_src], dtype=np.float32))
    shared["l2f_WrT"] = _t(inputs["L2_ffn_W_r"])
    shared["l2f_WkT"] = _t(inputs["L2_ffn_W_k"])
    shared["l2f_WvT"] = _t(inputs["L2_ffn_W_v"])
    shared["l3a_Wqkv"] = np.ascontiguousarray(np.asarray(inputs["L3_attn_W_qkv"], dtype=np.float32))
    shared["l3a_Wout"] = np.ascontiguousarray(np.asarray(inputs["L3_attn_W_out"], dtype=np.float32))
    shared["l3f_w1"] = np.ascontiguousarray(np.asarray(inputs["L3_ffn_w1"], dtype=np.float32))
    shared["l3f_w2"] = np.ascontiguousarray(np.asarray(inputs["L3_ffn_w2"], dtype=np.float32))
    shared["l3f_rwT"] = _t(inputs["L3_ffn_router_w"])
    vcv = np.zeros((128, NVC), dtype=np.float32)
    def put(name, v):
        o, n = VECCOLS[name]
        vcv[:, o:o + n] = _cols(v)
    put("l0a_Dskip", inputs["L0_attn_D_skip"])
    put("l0a_ssmw", inputs["L0_attn_ssm_norm_w"])
    put("l0a_ffnw", inputs["L0_attn_ffn_norm_w"])
    put("l0a_loglam", inputs["L0_attn_log_lambda"])
    put("l1a_bo", inputs["L1_attn_bo"])
    put("l2f_mur", inputs["L2_ffn_mu_r"])
    put("l2f_muk", inputs["L2_ffn_mu_k"])
    put("l2a_n1", inputs["L2_attn_norm1_w"])
    put("l2a_n2", inputs["L2_attn_norm2_w"])
    vcv[:, VECCOLS["k_zero"][0]] = 0.0
    vcv[:, VECCOLS["k_one"][0]] = 1.0
    vcv[:, VECCOLS["k_shift"][0]] = -30.0
    vcv[:, VECCOLS["k_eps5"][0]] = 1e-5
    vcv[:, VECCOLS["k_eps6"][0]] = 1e-6
    vcv[:, VECCOLS["k_neg1e30"][0]] = -1e30
    vcv[:, VECCOLS["k_shift8"][0]] = -8.0
    put("l3a_bout", inputs["L3_attn_b_out"])
    shared["veccols"] = vcv
    vr = np.zeros((13, 1024), dtype=np.float32)
    vr[0] = np.asarray(inputs["L1_ffn_bg"], np.float32)
    vr[1:9] = np.asarray(inputs["L1_ffn_be"], np.float32)
    vr[9] = np.asarray(inputs["L3_attn_b_out"], np.float32)
    vr[10:13] = np.asarray(inputs["L3_attn_b_qkv"], np.float32).reshape(3, 1024)
    shared["vecrows"] = vr
    shared.update(_CONSTS)
    maps = []
    for c in range(8):
        m = dict(shared)
        m["xT"] = np.ascontiguousarray(x[c].T)
        maps.append(m)
    return maps



class Sem:
    __slots__ = ("h", "v", "hwm")
    def __init__(self, h):
        self.h = h
        self.v = 0
        self.hwm = {}

class B:
    def __init__(self, nc, blk, st):
        self.nc = nc
        self.blk = blk
        self.st = st
        self.sems = {}
        for name in ["dS", "dG", "dG0", "dG1", "pe", "act", "dve", "pool"]:
            self.sems[name] = Sem(st.enter_context(nc.semaphore(name)))

    def wait(self, eng, key, *evs):
        for ev in evs:
            if ev is None:
                continue
            s, v = ev
            if s.hwm.get(key, 0) < v:
                eng.wait_ge(s.h, v)
                s.hwm[key] = v

    def bump(self, sem_name, instr, n=1):
        s = self.sems[sem_name]
        instr.then_inc(s.h, n)
        s.v += n
        return (s, s.v)

    def ev(self, sem_name):
        s = self.sems[sem_name]
        return (s, s.v)

    def fence(self, eng, sem_name="dve"):
        s = self.sems[sem_name]
        eng.wait_ge(s.h, s.v)

    def all_compute_evs(self):
        return [self.ev(k) for k in ["pe", "act", "dve", "pool"]]

    def all_evs(self):
        return [self.ev(k) for k in ["pe", "act", "dve", "pool", "dS", "dG"]]


def sbuf(st, nc, name, shape, dtype=F32):
    return st.enter_context(nc.sbuf_tensor(name, shape, dtype))



STEP_SEQ = ["l0f", "l0a", "l0f", "l1f", "l1a", "l1f",
            "l2f", "l2a", "l2f", "l3f", "l3a", "l3f"]

def build_program(steps):
    nc = bass.Bass("TRN2", target_bir_lowering=False, debug=False)
    IN = {}
    def di(name, shape, dtype=F32):
        IN[name] = nc.dram_tensor(name, list(shape), dtype, kind="ExternalInput").ap()
        return IN[name]
    di("xT", (D, T))
    di("veccols", (128, NVC)); di("vecrows", (13, 1024))
    di("c_ident", (128, 128)); di("c_ropeL1", (2, 128, T)); di("c_rotH", (128, 128), BF16)
    di("c_ropeG", (2, 128, T)); di("c_ropeL", (2, 128, T)); di("c_rotP", (128, 128), BF16)
    di("c_posL3", (128, T)); di("c_maskL1", (5, 128, 512)); di("c_maskL2g", (4, 128, 512))
    di("c_maskL2l", (6, 128, 512)); di("c_hind", (8, 128, 16)); di("c_hindT", (8, 16, 128))
    di("c_ind2", (128, 2)); di("c_ind2T", (2, 128))
    di("l0a_BwT", (D, 256)); di("l0a_CwT", (256, D)); di("l0a_w1T", (D, 2736))
    di("l0a_w2T", (2816, D)); di("l0a_w3T", (D, 2736))
    di("l0f_w12T", (8, D, 4096)); di("l0f_w3T", (8, 2048, D)); di("l0f_gwT", (D, 8))
    for n in ["l1a_WqT", "l1a_WkT", "l1a_WvT", "l1a_WoT", "l1f_Wg",
              "l2a_gwq", "l2a_gwo", "l2a_lwq", "l2a_lwo", "l2f_WrT",
              "l2f_WkT", "l2f_WvT", "l3a_Wout"]:
        di(n, (D, D))
    di("l1f_We", (8, D, D))
    di("l2a_gkvd", (D, 256)); di("l2a_gkup", (256, 256)); di("l2a_gvup", (256, 256))
    di("l2a_lwk", (D, 256)); di("l2a_lwv", (D, 256))
    di("l3a_Wqkv", (D, 3072)); di("l3f_w1", (8, D, 4096)); di("l3f_w2", (8, 4096, D))
    di("l3f_rwT", (D, 8))
    yT = nc.dram_tensor("yT", [D, T], F32, kind="ExternalOutput").ap()
    hx = [nc.dram_tensor(f"hx{i}", [D, T], F32, kind="Internal").ap() for i in range(2)]
    hxi = nc.dram_tensor("hxi", [D, T], F32, kind="Internal").ap()

    st = contextlib.ExitStack()
    with st:
        ctx = dict(nc=nc, IN=IN, yT=yT, hx=hx, hxi=hxi)
        ctx["SA"] = sbuf(st, nc, "SA", [128, NT * T], BF16)
        ctx["SAf"] = sbuf(st, nc, "SAf", [128, ND * T], F32)
        ctx["zT"] = sbuf(st, nc, "zTb", [128, ND * ZSTRIDE], BF16)
        ctx["wb"] = [sbuf(st, nc, f"wb{i}", [128, 4096], BF16) for i in range(2)]
        ctx["arena"] = sbuf(st, nc, "AR", [128, 6144], F32)
        ctx["vcol"] = sbuf(st, nc, "vcolb", [128, NVC], F32)
        ctx["ident"] = sbuf(st, nc, "identb_", [128, 128], F32)
        ctx["identb"] = sbuf(st, nc, "identbb", [128, 128], BF16)
        ctx["onesf"] = sbuf(st, nc, "onesf", [1, 512], F32)
        ctx["onesb"] = sbuf(st, nc, "onesb", [1, 512], BF16)
        ctx["onescf"] = sbuf(st, nc, "onescf", [128, 1], F32)
        ctx["onescb"] = sbuf(st, nc, "onescb", [128, 1], BF16)
        ctx["touch"] = sbuf(st, nc, "tchb", [1, 8], F32)
        ctx["SAb"] = ctx["SAf"].ap().bitcast(BF16)
        for p in ["pA", "pB", "pC", "pD", "pE", "pF"]:
            ctx[p] = st.enter_context(nc.psum_tensor(p, [128, 512], F32))
        ctx["pStat"] = st.enter_context(nc.psum_tensor("pStat", [64, 512], F32))
        blk = st.enter_context(nc.Block())
        b = B(nc, blk, st)
        ctx["b"] = b
        ctx["wbev"] = [None, None]
        ctx["pkev"] = {}
        emit_prologue(ctx)
        n = len(steps)
        for i, stp in enumerate(steps):
            name, scale = stp if isinstance(stp, tuple) else (stp, 0.5 if stp.endswith("f") else 1.0)
            src = IN["xT"] if i == 0 else hx[(i - 1) % 2]
            dst = yT if i == n - 1 else hx[i % 2]
            EMITTERS[name](ctx, src, dst, scale)
        emit_epilogue(ctx)
    return nc


def emit_prologue(ctx):
    b = ctx["b"]
    @b.blk.sync
    def _(sy):
        b.bump("dS", sy.dma_start(out=ctx["vcol"][:], in_=ctx["IN"]["veccols"]), 16)
        b.bump("dS", sy.dma_start(out=ctx["ident"][:], in_=ctx["IN"]["c_ident"]), 16)
    @b.blk.vector
    def _(v):
        b.wait(v, "dve", b.ev("dS"))
        b.bump("dve", v.tensor_copy(ctx["identb"][:], ctx["ident"][:]))
        b.bump("dve", v.memset(ctx["onesf"][:], 1.0))
        b.bump("dve", v.memset(ctx["onesb"][:], 1.0))
        b.bump("dve", v.memset(ctx["onescf"][:], 1.0))
        b.bump("dve", v.memset(ctx["onescb"][:], 1.0))
        b.fence(v)
        b.bump("dve", v.tensor_copy(ctx["touch"][:], ctx["identb"][0:1, 0:8]))
    ctx["prologue_ev"] = b.ev("dve")


def emit_epilogue(ctx):
    b = ctx["b"]
    @b.blk.scalar
    def _(s):
        b.wait(s, "act", *b.all_evs())


def pk_wait(b, t, ctx, bank):
    evs = ctx["pkev"].get(bank)
    if evs:
        b.wait(t, "pe", *evs)

def pk_set(ctx, b, bank, *sems):
    ctx["pkev"][bank] = tuple(b.ev(s) for s in sems)


def vc(ctx, name, j=0):
    o, _n = VECCOLS[name]
    return ctx["vcol"][:, o + j:o + j + 1]


def zslc(dt, tc=None, w=512):
    if tc is None:
        return slice(dt * ZSTRIDE + 1, dt * ZSTRIDE + 1 + T)
    return slice(dt * ZSTRIDE + 1 + tc * w, dt * ZSTRIDE + 1 + tc * w + w)



def emit_ln(ctx, src_hbm, rms=False, w_col_name=None, eps_name="k_eps5",
            barrier=True, want_zlo=False, resident=False):
    b = ctx["b"]
    SAf, arena = ctx["SAf"], ctx["arena"]
    SA = ctx["SA"]
    pStat, zT = ctx["pStat"], ctx["zT"]
    m_row = lambda c0, c1: arena[0:1, 4096 + c0:4096 + c1]
    v_row = lambda c0, c1: arena[0:1, c0:c1]
    s_row = lambda c0, c1: arena[0:1, 2048 + c0:2048 + c1]
    xevs = []
    if not resident:
        @b.blk.sync
        def _(sy):
            if barrier:
                b.wait(sy, "dS", *b.all_compute_evs())
            b.wait(sy, "dS", b.ev("dS"))
            for dt in range(ND):
                xevs.append(b.bump("dS", sy.dma_start(
                    out=SAf[:, dt * T:(dt + 1) * T],
                    in_=src_hbm[dt * 128:(dt + 1) * 128, :]), 16))
    else:
        xevs.append(b.ev("dve"))
    for tc in range(4):
        sq_evs = []
        @b.blk.scalar
        def _(s, tc=tc):
            b.wait(s, "act", xevs[-1], b.ev("pe"))
            for dt in range(ND):
                sq_evs.append(b.bump("act", s.activation(
                    SA[:, dt * 512:dt * 512 + 512],
                    SAf[:, dt * T + tc * 512: dt * T + tc * 512 + 512],
                    AF.Square, bias=vc(ctx, "k_zero"), scale=1.0)))
        @b.blk.tensor
        def _(t, tc=tc):
            b.wait(t, "pe", xevs[-1], ctx["prologue_ev"], b.ev("dve"))
            for dt in range(ND):
                t.matmul(pStat[0:1, :], ctx["onescf"][:, :],
                         SAf[:, dt * T + tc * 512: dt * T + tc * 512 + 512],
                         start=(dt == 0), stop=(dt == ND - 1))
                b.wait(t, "pe", sq_evs[dt])
                ee = t.matmul(pStat[32:33, :], ctx["onescb"][:, :],
                              SA[:, dt * 512:dt * 512 + 512],
                              start=(dt == 0), stop=(dt == ND - 1))
            b.bump("pe", ee)
        @b.blk.vector
        def _(v, tc=tc):
            b.wait(v, "dve", b.ev("pe"))
            v.tensor_copy(m_row(tc * 512, (tc + 1) * 512), pStat[0:1, :])
            b.bump("dve", v.tensor_copy(v_row(tc * 512, (tc + 1) * 512), pStat[32:33, :]))
    @b.blk.vector
    def _(v):
        b.wait(v, "dve", b.ev("dve"))
        if not rms:
            b.bump("dve", v.tensor_scalar(m_row(0, T), m_row(0, T), 1.0 / D, None, op0=ALU.mult))
            b.fence(v)
            b.bump("dve", v.tensor_tensor(s_row(0, T), m_row(0, T), m_row(0, T), op=ALU.mult))
            b.bump("dve", v.tensor_scalar(v_row(0, T), v_row(0, T), 1.0 / D, None, op0=ALU.mult))
            b.fence(v)
            b.bump("dve", v.tensor_tensor(v_row(0, T), v_row(0, T), s_row(0, T), op=ALU.subtract))
        else:
            b.bump("dve", v.memset(m_row(0, T), 0.0))
            b.bump("dve", v.tensor_scalar(v_row(0, T), v_row(0, T), 1.0 / D, None, op0=ALU.mult))
        b.fence(v)
        b.bump("dve", v.tensor_copy(ctx["touch"][:], v_row(0, 8)))
    @b.blk.scalar
    def _(s):
        b.wait(s, "act", b.ev("dve"))
        o_eps = VECCOLS[eps_name][0]
        b.bump("act", s.activation(v_row(0, T), v_row(0, T), AF.Sqrt,
                                   bias=ctx["vcol"][0:1, o_eps:o_eps + 1], scale=1.0))
    @b.blk.vector
    def _(v):
        b.wait(v, "dve", b.ev("act"))
        b.bump("dve", v.reciprocal(v_row(0, T), v_row(0, T)))
        b.fence(v)
        b.bump("dve", v.tensor_copy(ctx["touch"][:], v_row(0, 8)))
    rowev = b.ev("dve")
    for tc in range(4):
        @b.blk.tensor
        def _(t, tc=tc):
            b.wait(t, "pe", rowev, b.ev("dve"))
            t.matmul(ctx["pD"][:, :], ctx["onesf"][:, 0:128], m_row(tc * 512, (tc + 1) * 512),
                     start=True, stop=True)
            b.bump("pe", t.matmul(ctx["pE"][:, :], ctx["onesf"][:, 0:128],
                                  v_row(tc * 512, (tc + 1) * 512), start=True, stop=True))
        @b.blk.vector
        def _(v, tc=tc):
            b.wait(v, "dve", b.ev("pe"))
            v.tensor_copy(arena[:, tc * 512:(tc + 1) * 512], ctx["pD"][:])
            b.bump("dve", v.tensor_copy(arena[:, 2048 + tc * 512:2048 + (tc + 1) * 512],
                                        ctx["pE"][:]))
    @b.blk.vector
    def _(v):
        b.wait(v, "dve", b.ev("dve"))
        for dt in range(ND):
            if not rms:
                b.bump("dve", v.tensor_tensor(SAf[:, dt * T:(dt + 1) * T],
                                              SAf[:, dt * T:(dt + 1) * T],
                                              arena[:, 0:T], op=ALU.subtract))
                b.fence(v)
                if want_zlo:
                    b.bump("dve", v.tensor_tensor(SAf[:, dt * T:(dt + 1) * T],
                                                  SAf[:, dt * T:(dt + 1) * T],
                                                  arena[:, 2048:2048 + T], op=ALU.mult))
                    b.fence(v)
                    b.bump("dve", v.tensor_copy(zT[:, zslc(dt)], SAf[:, dt * T:(dt + 1) * T]))
                    b.fence(v)
                    b.bump("dve", v.tensor_tensor(SA[:, dt * T:(dt + 1) * T], SAf[:, dt * T:(dt + 1) * T],
                                    zT[:, zslc(dt)], op=ALU.subtract))
                else:
                    b.bump("dve", v.tensor_tensor(zT[:, zslc(dt)], SAf[:, dt * T:(dt + 1) * T],
                                    arena[:, 2048:2048 + T], op=ALU.mult))
            else:
                b.bump("dve", v.scalar_tensor_tensor(zT[:, zslc(dt)], SAf[:, dt * T:(dt + 1) * T],
                                       vc(ctx, w_col_name, dt), arena[:, 2048:2048 + T],
                                       op0=ALU.mult, op1=ALU.mult))
        b.fence(v)
        b.bump("dve", v.tensor_copy(ctx["touch"][:], zT[0:1, 1:9]))
    return b.ev("dve")



def emit_l2f(ctx, src, dst, scale):
    b = ctx["b"]
    SA, SAf, arena, zT = ctx["SA"], ctx["SAf"], ctx["arena"], ctx["zT"]
    wb, pA, pB = ctx["wb"], ctx["pA"], ctx["pB"]
    zev = emit_ln(ctx, src, rms=False)
    @b.blk.vector
    def _(v):
        b.wait(v, "dve", zev)
        for dt in range(ND):
            b.bump("dve", v.memset(zT[:, dt * ZSTRIDE:dt * ZSTRIDE + 1], 0.0))
        b.fence(v)
        for dt in range(ND):
            zc = zT[:, dt * ZSTRIDE + 1: dt * ZSTRIDE + 1 + T]
            zp = zT[:, dt * ZSTRIDE: dt * ZSTRIDE + T]
            b.fence(v)
            b.bump("dve", v.tensor_tensor(arena[:, 0:T], zc, zp, op=ALU.subtract))
            b.fence(v)
            v.scalar_tensor_tensor(SA[:, dt * T:(dt + 1) * T], arena[:, 0:T],
                                   vc(ctx, "l2f_mur", dt), zp, op0=ALU.mult, op1=ALU.add)
            b.bump("dve", v.scalar_tensor_tensor(SA[:, (8 + dt) * T:(9 + dt) * T], arena[:, 0:T],
                                   vc(ctx, "l2f_muk", dt), zp, op0=ALU.mult, op1=ALU.add))
        b.bump("dve", v.tensor_copy(ctx["touch"][:], SA[0:1, 0:8]))
    mixev = b.ev("dve")
    for wname, src_off in [("l2f_WrT", 0), ("l2f_WkT", 8 * T)]:
        for do in range(ND):
            wev = [None]
            @b.blk.gpsimd
            def _(gp, wname=wname, do=do):
                b.wait(gp, "pool", ctx["wbev"][do % 2])
                wev[0] = b.bump("dG0" if do % 2 == 0 else "dG1", gp.dma_start(
                    out=wb[do % 2][:, 0:1024].rearrange("p (a c) -> p a c", a=8),
                    in_=ctx["IN"][wname].rearrange("(a p) f -> p a f", p=128)[:, :, do * 128:(do + 1) * 128]), 16)
            for tc in range(4):
                p = pA if tc % 2 == 0 else pB
                @b.blk.tensor
                def _(t, wname=wname, src_off=src_off, do=do, tc=tc, p=p):
                    b.wait(t, "pe", wev[0], mixev, b.ev("act"), b.ev("dve"))
                    for dt in range(ND):
                        ee = t.matmul(p[:, :], wb[do % 2][:, dt * 128:(dt + 1) * 128],
                                      SA[:, src_off + dt * T + tc * 512: src_off + dt * T + tc * 512 + 512],
                                      start=(dt == 0), stop=(dt == ND - 1))
                    b.bump("pe", ee)
                ctx["wbev"][do % 2] = b.ev("pe")
                @b.blk.scalar
                def _(s, wname=wname, do=do, tc=tc, p=p):
                    b.wait(s, "act", b.ev("pe"))
                    if wname == "l2f_WrT":
                        ee = s.activation(SAf[:, do * T + tc * 512: do * T + tc * 512 + 512],
                                          p[:], AF.Sigmoid, bias=vc(ctx, "k_zero"), scale=1.0)
                    else:
                        ee = s.activation(arena[:, (tc % 2) * 512:(tc % 2) * 512 + 512],
                                          p[:], AF.Relu, bias=vc(ctx, "k_zero"), scale=1.0)
                    b.bump("act", ee)
                if wname == "l2f_WkT":
                    @b.blk.vector
                    def _(v, do=do, tc=tc):
                        b.wait(v, "dve", b.ev("act"))
                        sq = arena[:, (tc % 2) * 512:(tc % 2) * 512 + 512]
                        b.bump("dve", v.tensor_tensor(zT[:, zslc(do, tc)], sq, sq, op=ALU.mult))
    k2ev = b.ev("dve")
    rev = b.ev("act")
    for do in range(ND):
        wev = [None]
        xev = [None]
        guard = b.ev("pe")
        @b.blk.gpsimd
        def _(gp, do=do, guard=guard):
            b.wait(gp, "pool", guard)
            wev[0] = b.bump("dG0" if do % 2 == 0 else "dG1", gp.dma_start(
                out=wb[do % 2][:, 0:1024].rearrange("p (a c) -> p a c", a=8),
                in_=ctx["IN"]["l2f_WvT"].rearrange("(a p) f -> p a f", p=128)[:, :, do * 128:(do + 1) * 128]), 16)
        dguard = b.ev("dve")
        @b.blk.sync
        def _(sy, do=do, dguard=dguard):
            b.wait(sy, "dS", dguard, b.ev("dS"))
            xev[0] = b.bump("dS", sy.dma_start(
                out=arena[:, 2048 + (do % 2) * 2048: 2048 + (do % 2) * 2048 + T],
                in_=src[do * 128:(do + 1) * 128, :]), 16)
        for tc in range(4):
            p = pA if tc % 2 == 0 else pB
            @b.blk.tensor
            def _(t, do=do, tc=tc, p=p):
                b.wait(t, "pe", wev[0], k2ev, b.ev("dve"))
                for dt in range(ND):
                    ee = t.matmul(p[:, :], wb[do % 2][:, dt * 128:(dt + 1) * 128],
                                  zT[:, zslc(dt, tc)], start=(dt == 0), stop=(dt == ND - 1))
                b.bump("pe", ee)
            @b.blk.vector
            def _(v, do=do, tc=tc, p=p):
                b.wait(v, "dve", b.ev("pe"), xev[0], rev)
                b.bump("dve", v.tensor_tensor(arena[:, 1024:1536], p[:],
                                SAf[:, do * T + tc * 512: do * T + tc * 512 + 512], op=ALU.mult))
                b.fence(v)
                xs = arena[:, 2048 + (do % 2) * 2048 + tc * 512: 2048 + (do % 2) * 2048 + tc * 512 + 512]
                b.bump("dve", v.scalar_tensor_tensor(xs, arena[:, 1024:1536], scale, xs,
                                                     op0=ALU.mult, op1=ALU.add))
                b.fence(v)
                b.bump("dve", v.tensor_copy(ctx["touch"][:], xs[0:1, 0:8]))
        @b.blk.sync
        def _(sy, do=do):
            b.wait(sy, "dS", b.ev("dve"))
            b.bump("dS", sy.dma_start(out=dst[do * 128:(do + 1) * 128, :],
                                      in_=arena[:, 2048 + (do % 2) * 2048: 2048 + (do % 2) * 2048 + T]), 16)



def emit_l1f(ctx, src, dst, scale):
    b = ctx["b"]
    SA, SAf, arena, zT = ctx["SA"], ctx["SAf"], ctx["arena"], ctx["zT"]
    wb, pA, pB, pC = ctx["wb"], ctx["pA"], ctx["pB"], ctx["pC"]
    zev = emit_ln(ctx, src, rms=False, want_zlo=True)
    @b.blk.gpsimd
    def _(gp):
        b.wait(gp, "pool", zev)
        b.bump("dG", gp.dma_start(out=SA[0:1, 20480:21504],
                                  in_=ctx["IN"]["vecrows"][0:1, :]), 16)
        for e in range(8):
            b.bump("dG", gp.dma_start(out=SA[0:1, 22528 + e * 1024: 22528 + (e + 1) * 1024],
                                      in_=ctx["IN"]["vecrows"][1 + e:2 + e, :]), 16)
    @b.blk.sync
    def _(sy):
        b.wait(sy, "dS", zev)
        b.bump("dS", sy.dma_start(out=arena[0:1, 0:1024],
                                  in_=ctx["IN"]["vecrows"][0:1, :]), 16)
    @b.blk.vector
    def _(v):
        b.wait(v, "dve", b.ev("dS"), b.ev("dG"))
        b.bump("dve", v.tensor_tensor(SA[0:1, 21504:22528], arena[0:1, 0:1024],
                                      SA[0:1, 20480:21504], op=ALU.subtract))
        b.fence(v)
        b.bump("dve", v.tensor_copy(ctx["touch"][:], SA[0:1, 21504:21512]))
    rowev = b.ev("dve")
    for fc in range(2):
        hlev = [None]
        for dt in range(ND):
            wev = [None]
            dguard = b.ev("dve")
            @b.blk.sync
            def _(sy, fc=fc, dt=dt, dguard=dguard):
                b.wait(sy, "dS", dguard, b.ev("pe"))
                wev[0] = b.bump("dS", sy.dma_start(
                    out=arena[:, 1536 + (dt % 2) * 512:1536 + (dt % 2) * 512 + 512],
                    in_=ctx["IN"]["l1f_Wg"][dt * 128:(dt + 1) * 128, fc * 512:(fc + 1) * 512]), 16)
            @b.blk.vector
            def _(v, fc=fc, dt=dt):
                b.wait(v, "dve", wev[0], ctx["wbev"][0], ctx["wbev"][1])
                wsl = arena[:, 1536 + (dt % 2) * 512:1536 + (dt % 2) * 512 + 512]
                b.bump("dve", v.tensor_copy(wb[0][:, dt * 512:(dt + 1) * 512], wsl))
                b.fence(v)
                b.bump("dve", v.tensor_tensor(wb[1][:, dt * 512:(dt + 1) * 512], wsl,
                                               wb[0][:, dt * 512:(dt + 1) * 512], op=ALU.subtract))
                if dt == ND - 1:
                    b.fence(v)
                    hlev[0] = b.bump("dve", v.tensor_copy(ctx["touch"][:], wb[1][0:1, 0:8]))
        for tt in range(NT):
            bk = ["pA", "pB", "pC", "pD"][tt % 4]
            @b.blk.tensor
            def _(t, fc=fc, tt=tt, bk=bk):
                b.wait(t, "pe", hlev[0])
                pk_wait(b, t, ctx, bk)
                p = ctx[bk]
                for dt in range(ND):
                    zhi = zT[:, dt * ZSTRIDE + 1 + tt * 128: dt * ZSTRIDE + 1 + tt * 128 + 128]
                    zlo = SA[:, dt * T + tt * 128: dt * T + tt * 128 + 128]
                    t.matmul(p[:], zhi, wb[0][:, dt * 512:(dt + 1) * 512],
                             start=(dt == 0), stop=False)
                    t.matmul(p[:], zhi, wb[1][:, dt * 512:(dt + 1) * 512],
                             start=False, stop=False)
                    t.matmul(p[:], zlo, wb[0][:, dt * 512:(dt + 1) * 512],
                             start=False, stop=False)
                t.matmul(p[:], ctx["onesb"][:, 0:128], SA[0:1, 20480 + fc * 512:20480 + fc * 512 + 512],
                         start=False, stop=False)
                b.bump("pe", t.matmul(p[:], ctx["onesb"][:, 0:128],
                                      SA[0:1, 21504 + fc * 512:21504 + fc * 512 + 512],
                                      start=False, stop=True))
            @b.blk.vector
            def _(v, fc=fc, tt=tt, bk=bk):
                b.wait(v, "dve", b.ev("pe"))
                p = ctx[bk]
                b.bump("dve", v.tensor_copy(SAf[:, tt * 1024 + fc * 512: tt * 1024 + fc * 512 + 512], p[:]))
            pk_set(ctx, b, bk, "dve")
            ctx["wbev"][0] = b.ev("pe")
            ctx["wbev"][1] = b.ev("pe")
    AXX = AX.X
    for tt in range(NT):
        @b.blk.vector
        def _(v, tt=tt):
            b.wait(v, "dve", b.ev("dve"))
            sc = SAf[:, tt * 1024: (tt + 1) * 1024]
            scr = sc.rearrange("p (a b) -> p a b", b=8)
            for e in range(8):
                b.bump("dve", v.reduce_max(arena[:, 1280 + e:1281 + e], scr[:, :, e], axis=AXX))
            b.fence(v)
            b.bump("dve", v.reduce_max(arena[:, 1288:1289], arena[:, 1280:1288], axis=AXX))
            b.fence(v)
            b.bump("dve", v.tensor_scalar(arena[:, 1289:1297], arena[:, 1280:1288], arena[:, 1288:1289],
                                          None, op0=ALU.is_equal))
            ee = v.tensor_scalar(arena[:, 1297:1298], arena[:, 1288:1289], -1.0, None, op0=ALU.mult)
            b.bump("dve", ee)
        @b.blk.scalar
        def _(s, tt=tt):
            b.wait(s, "act", b.ev("dve"))
            b.bump("act", s.activation(SA[:, 16384:17408], SAf[:, tt * 1024:(tt + 1) * 1024],
                                       AF.Exp, bias=arena[:, 1297:1298], scale=1.0,
                                       accum_out=arena[:, 1298:1299]))
        @b.blk.vector
        def _(v, tt=tt):
            b.wait(v, "dve", b.ev("act"))
            b.bump("dve", v.reciprocal(arena[:, 1299:1300], arena[:, 1298:1299]))
            b.fence(v)
            ee = v.tensor_scalar(arena[:, 1024 + tt * 8:1024 + tt * 8 + 8],
                                 arena[:, 1289:1297], arena[:, 1299:1300], None, op0=ALU.mult)
            b.bump("dve", ee)
    gatev = b.ev("dve")
    BANKS = ["pA", "pB", "pC", "pD"]
    for e in range(8):
        for fc in range(2):
            ci = e * 2 + fc
            wev = [None]
            @b.blk.gpsimd
            def _(gp, e=e, fc=fc, ci=ci):
                b.wait(gp, "pool", ctx["wbev"][ci % 2])
                wev[0] = b.bump("dG0" if ci % 2 == 0 else "dG1", gp.dma_start(
                    out=wb[ci % 2][:, :].rearrange("p (a c) -> p a c", a=8),
                    in_=ctx["IN"]["l1f_We"].rearrange("e (a p) f -> e p a f", p=128)[e][:, :, fc * 512:(fc + 1) * 512]), 16)
            for tt in range(NT):
                bk = BANKS[tt % 4]
                p = ctx[bk]
                @b.blk.tensor
                def _(t, e=e, ci=ci, tt=tt, p=p, bk=bk):
                    b.wait(t, "pe", wev[0], gatev)
                    pk_wait(b, t, ctx, bk)
                    for dt in range(ND):
                        zhi = zT[:, dt * ZSTRIDE + 1 + tt * 128: dt * ZSTRIDE + 1 + tt * 128 + 128]
                        t.matmul(p[:], zhi, wb[ci % 2][:, dt * 512:(dt + 1) * 512],
                                 start=(dt == 0), stop=False)
                    b.bump("pe", t.matmul(p[:], ctx["onesb"][:, 0:128],
                                          SA[0:1, 22528 + e * 1024 + fc * 512: 22528 + e * 1024 + fc * 512 + 512],
                                          start=False, stop=True))
                ctx["wbev"][ci % 2] = b.ev("pe")
                @b.blk.vector
                def _(v, e=e, fc=fc, tt=tt, p=p):
                    b.wait(v, "dve", b.ev("pe"))
                    ysl = SAf[:, tt * 1024 + fc * 512: tt * 1024 + fc * 512 + 512]
                    ccol = arena[:, 1024 + tt * 8 + e:1024 + tt * 8 + e + 1]
                    if e == 0:
                        ee = v.tensor_scalar(ysl, p[:], 0.0, ccol, op0=ALU.max, op1=ALU.mult)
                    else:
                        tmp = SA[:, 16384 + (tt % 2) * 512:16384 + (tt % 2) * 512 + 512]
                        b.bump("dve", v.tensor_scalar(tmp, p[:], 0.0, ccol,
                                                      op0=ALU.max, op1=ALU.mult))
                        b.fence(v)
                        ee = v.tensor_tensor(ysl, ysl, tmp, op=ALU.add)
                    b.bump("dve", ee)
                pk_set(ctx, b, bk, "dve")
    emit_ytok_residual(ctx, src, dst, scale)


def emit_ytok_residual(ctx, src, dst, scale, extra_z=False):
    b = ctx["b"]
    SAf, arena = ctx["SAf"], ctx["arena"]
    pC = ctx["pC"]
    for dt in range(ND):
        xev = [None]
        dguard = b.ev("dve")
        @b.blk.sync
        def _(sy, dt=dt, dguard=dguard):
            b.wait(sy, "dS", dguard, b.ev("dS"))
            xev[0] = b.bump("dS", sy.dma_start(
                out=arena[:, (dt % 2) * 2048:(dt % 2) * 2048 + T],
                in_=src[dt * 128:(dt + 1) * 128, :]), 16)
        for tc in range(4):
            @b.blk.tensor
            def _(t, dt=dt, tc=tc):
                b.wait(t, "pe", b.ev("dve"))
                for j in range(4):
                    tt = tc * 4 + j
                    ee = t.transpose(pC[:, j * 128:(j + 1) * 128],
                                     SAf[:, tt * 1024 + dt * 128: tt * 1024 + dt * 128 + 128],
                                     ctx["ident"][:])
                b.bump("pe", ee)
            @b.blk.vector
            def _(v, dt=dt, tc=tc):
                b.wait(v, "dve", b.ev("pe"), xev[0])
                xs = arena[:, (dt % 2) * 2048 + tc * 512:(dt % 2) * 2048 + tc * 512 + 512]
                b.bump("dve", v.scalar_tensor_tensor(xs, pC[:], scale, xs,
                                                     op0=ALU.mult, op1=ALU.add))
                b.fence(v)
                if extra_z:
                    b.bump("dve", v.scalar_tensor_tensor(xs, ctx["zT"][:, zslc(dt, tc)],
                                                         scale, xs, op0=ALU.mult, op1=ALU.add))
                    b.fence(v)
                b.bump("dve", v.tensor_copy(ctx["touch"][:], xs[0:1, 0:8]))
        @b.blk.sync
        def _(sy, dt=dt):
            b.wait(sy, "dS", b.ev("dve"))
            b.bump("dS", sy.dma_start(out=dst[dt * 128:(dt + 1) * 128, :],
                                      in_=arena[:, (dt % 2) * 2048:(dt % 2) * 2048 + T]), 16)







def v65_col(cfg, tt, g):
    return cfg["v65_off"] + tt * cfg["kvh"] * 65 + g * 65

def emit_attn_core(ctx, cfg):
    b = ctx["b"]
    SA, SAb, arena, zT = ctx["SA"], ctx["SAb"], ctx["arena"], ctx["zT"]
    pC, pD, pE, pF = ctx["pC"], ctx["pD"], ctx["pE"], ctx["pF"]
    nh = cfg["nh"]
    for h in cfg.get("h_order", range(nh)):
        hb = (h % 2) * 64
        hp = h // 2
        for qc in cfg.get("qc_order", range(4)):
            kts = cfg["kt_list"](qc)
            for i, (kt, mi) in enumerate(kts):
                bk = "pE" if i % 2 == 0 else "pF"
                p = ctx[bk]
                @b.blk.tensor
                def _(t, h=h, hb=hb, qc=qc, kt=kt, p=p, bk=bk):
                    b.wait(t, "pe", cfg["ready_ev"])
                    pk_wait(b, t, ctx, bk)
                    b.bump("pe", t.matmul(
                        p[:], SA[hb:hb + 64, cfg["k_off"] + (h // 2) * T + kt * 128:
                                 cfg["k_off"] + (h // 2) * T + kt * 128 + 128],
                        SA[hb:hb + 64, cfg["q_off"] + (h // 2) * T + qc * 512:
                           cfg["q_off"] + (h // 2) * T + qc * 512 + 512],
                        start=True, stop=True))
                if mi is not None:
                    @b.blk.vector
                    def _(v, mi=mi, p=p):
                        b.wait(v, "dve", b.ev("pe"), b.ev("act"))
                        b.bump("dve", v.tensor_tensor(
                            arena[:, 5632:6144], p[:],
                            arena[:, cfg["mask_arena_off"] + mi * 512:
                                  cfg["mask_arena_off"] + (mi + 1) * 512], op=ALU.add))
                    pk_set(ctx, b, bk, "dve")
                    @b.blk.scalar
                    def _(s, i=i, kt=kt):
                        b.wait(s, "act", b.ev("dve"))
                        b.bump("act", s.activation(
                            SAb[:, cfg["pT_off"] + i * 512: cfg["pT_off"] + (i + 1) * 512],
                            arena[:, 5632:6144], AF.Exp,
                            bias=vc(ctx, cfg["shift"]), scale=1.0))
                else:
                    @b.blk.scalar
                    def _(s, i=i, p=p):
                        b.wait(s, "act", b.ev("pe"))
                        b.bump("act", s.activation(
                            SAb[:, cfg["pT_off"] + i * 512: cfg["pT_off"] + (i + 1) * 512],
                            p[:], AF.Exp, bias=vc(ctx, cfg["shift"]), scale=1.0))
                    pk_set(ctx, b, bk, "act")
            @b.blk.tensor
            def _(t, h=h, qc=qc, kts=kts):
                b.wait(t, "pe", b.ev("act"), b.ev("dve"))
                g = h // (cfg["nh"] // cfg["kvh"]) if cfg["kvh"] < cfg["nh"] else h
                for i, (kt, mi) in enumerate(kts):
                    ee = t.matmul(pC[0:65, :],
                                  SAb[:, v65_col(cfg, kt, g):v65_col(cfg, kt, g) + 65],
                                  SAb[:, cfg["pT_off"] + i * 512: cfg["pT_off"] + (i + 1) * 512],
                                  start=(i == 0), stop=(i == len(kts) - 1))
                b.bump("pe", ee)
            @b.blk.vector
            def _(v):
                b.wait(v, "dve", b.ev("pe"))
                b.bump("dve", v.reciprocal(arena[0:1, 5632:6144], pC[64:65, :]))
                b.fence(v)
                b.bump("dve", v.tensor_copy(ctx["touch"][:], arena[0:1, 5632:5640]))
            @b.blk.tensor
            def _(t):
                b.wait(t, "pe", b.ev("dve"))
                b.bump("pe", t.matmul(pD[0:64, :], ctx["onesf"][:, 0:64],
                                      arena[0:1, 5632:6144], start=True, stop=True))
            @b.blk.vector
            def _(v, h=h, hb=hb, hp=hp, qc=qc):
                b.wait(v, "dve", b.ev("pe"))
                b.bump("dve", v.tensor_copy(arena[0:64, 5120:5632], pD[0:64, :]))
                b.fence(v)
                b.bump("dve", v.tensor_tensor(
                    zT[hb:hb + 64, hp * ZSTRIDE + 1 + qc * 512: hp * ZSTRIDE + 1 + qc * 512 + 512],
                    pC[0:64, :], arena[0:64, 5120:5632], op=ALU.mult))


def emit_outproj_residual(ctx, srcs, dst, w_name, bias_col_name=None):
    b = ctx["b"]
    arena, zT, wb = ctx["arena"], ctx["zT"], ctx["wb"]
    pA, pB = ctx["pA"], ctx["pB"]
    for do in range(ND):
        wev = [None]
        xev = [None]
        @b.blk.gpsimd
        def _(gp, do=do):
            b.wait(gp, "pool", ctx["wbev"][do % 2])
            wev[0] = b.bump("dG0" if do % 2 == 0 else "dG1", gp.dma_start(
                out=wb[do % 2][:, 0:1024].rearrange("p (a c) -> p a c", a=8),
                in_=ctx["IN"][w_name].rearrange("(a p) f -> p a f", p=128)[:, :, do * 128:(do + 1) * 128]), 16)
        dguard = b.ev("dve")
        @b.blk.sync
        def _(sy, do=do, dguard=dguard):
            b.wait(sy, "dS", dguard, b.ev("dS"))
            for si, s_ in enumerate(srcs):
                xev[0] = b.bump("dS", sy.dma_start(
                    out=arena[:, si * 2048:si * 2048 + T],
                    in_=s_[do * 128:(do + 1) * 128, :]), 16)
        for tc in range(4):
            p = pA if tc % 2 == 0 else pB
            @b.blk.tensor
            def _(t, do=do, tc=tc, p=p):
                b.wait(t, "pe", wev[0], b.ev("dve"))
                for ft in range(ND):
                    ee = t.matmul(p[:], wb[do % 2][:, ft * 128:(ft + 1) * 128],
                                  zT[:, zslc(ft, tc)], start=(ft == 0), stop=(ft == ND - 1))
                b.bump("pe", ee)
            ctx["wbev"][do % 2] = b.ev("pe")
            @b.blk.vector
            def _(v, do=do, tc=tc, p=p, nsrc=len(srcs)):
                b.wait(v, "dve", b.ev("pe"), xev[0])
                xs = arena[:, tc * 512:tc * 512 + 512]
                if nsrc > 1:
                    b.bump("dve", v.tensor_tensor(xs, xs,
                                                  arena[:, 2048 + tc * 512:2048 + tc * 512 + 512],
                                                  op=ALU.add))
                    b.fence(v)
                if bias_col_name is not None:
                    b.bump("dve", v.scalar_tensor_tensor(xs, p[:], vc(ctx, bias_col_name, do),
                                                         xs, op0=ALU.add, op1=ALU.add))
                else:
                    b.bump("dve", v.scalar_tensor_tensor(xs, p[:], 0.0, xs,
                                                         op0=ALU.add, op1=ALU.add))
                b.fence(v)
                b.bump("dve", v.tensor_copy(ctx["touch"][:], xs[0:1, 0:8]))
        @b.blk.sync
        def _(sy, do=do):
            b.wait(sy, "dS", b.ev("dve"))
            b.bump("dS", sy.dma_start(out=dst[do * 128:(do + 1) * 128, :],
                                      in_=arena[:, 0:T]), 16)


def proj_fm(ctx, w_name, fcols, out_sa_off, n_f, ready_evs=(), bias_row=None,
            add_tile=None, evac_scale=None, src_tiles=None):
    b = ctx["b"]
    SA, SAb, arena, zT, wb = ctx["SA"], ctx["SAb"], ctx["arena"], ctx["zT"], ctx["wb"]
    pA, pB = ctx["pA"], ctx["pB"]
    if src_tiles is None:
        srcbuf, sbase, sstride, nk = zT, 1, ZSTRIDE, ND
    else:
        srcbuf, sbase, sstride, nk = src_tiles
    ready_evs = tuple(ready_evs) + (b.ev("dve"),)
    nchunk = (n_f + 511) // 512
    BANKS = ["pA", "pB", "pC", "pD"]
    for c in range(nchunk):
        cw = min(512, n_f - c * 512)
        wev = [None]
        ci = ctx.get("wb_ci", 0); ctx["wb_ci"] = ci + 1
        @b.blk.gpsimd
        def _(gp, c=c, cw=cw, ci=ci):
            b.wait(gp, "pool", ctx["wbev"][ci % 2])
            wev[0] = b.bump("dG0" if ci % 2 == 0 else "dG1", gp.dma_start(
                out=wb[ci % 2][:, 0:nk * cw].rearrange("p (a q) -> p a q", a=nk),
                in_=ctx["IN"][w_name].rearrange("(a p) f -> p a f", p=128)[:, :, fcols + c * 512:fcols + c * 512 + cw]), 16)
        nf_t = (cw + 127) // 128
        for fi in range(nf_t):
            fw = min(128, cw - fi * 128)
            ft = c * 4 + fi
            for tc in range(4):
                bk = BANKS[(fi * 4 + tc) % 4]
                p = ctx[bk]
                @b.blk.tensor
                def _(t, c=c, ci=ci, cw=cw, fi=fi, fw=fw, ft=ft, tc=tc, p=p, bk=bk):
                    b.wait(t, "pe", wev[0], *ready_evs)
                    pk_wait(b, t, ctx, bk)
                    for dt in range(nk):
                        ee = t.matmul(p[0:fw, 0:512],
                                      wb[ci % 2][:, dt * cw + fi * 128: dt * cw + fi * 128 + fw],
                                      srcbuf[:, sbase + dt * sstride + tc * 512:
                                             sbase + dt * sstride + tc * 512 + 512],
                                      start=(dt == 0), stop=(dt == nk - 1 and bias_row is None))
                    if bias_row is not None:
                        ee = t.matmul(p[0:fw, 0:512],
                                      SAb[0:1, bias_row + c * 512 + fi * 128:
                                          bias_row + c * 512 + fi * 128 + fw],
                                      ctx["onesb"][:, 0:512], start=False, stop=True)
                    b.bump("pe", ee)
                ctx["wbev"][ci % 2] = b.ev("pe")
                @b.blk.vector
                def _(v, c=c, fi=fi, fw=fw, ft=ft, tc=tc, p=p):
                    b.wait(v, "dve", b.ev("pe"))
                    dstsl = SA[0:fw, out_sa_off + ft * T + tc * 512: out_sa_off + ft * T + tc * 512 + 512]
                    if add_tile is not None:
                        b.bump("dve", v.tensor_tensor(dstsl, p[0:fw, 0:512],
                                                      arena[0:fw, add_tile + tc * 512:add_tile + tc * 512 + 512],
                                                      op=ALU.add))
                    elif evac_scale is not None:
                        b.bump("dve", v.tensor_scalar(dstsl, p[0:fw, 0:512], evac_scale, None,
                                                      op0=ALU.mult))
                    else:
                        b.bump("dve", v.tensor_copy(dstsl, p[0:fw, 0:512]))
                pk_set(ctx, b, bk, "dve")




def proj_v65(ctx, w_name, fcols, nh_v, cfg, ready_evs=(), bias_row=None,
             src_tiles=None):
    b = ctx["b"]
    SA, SAb, zT, wb = ctx["SA"], ctx["SAb"], ctx["zT"], ctx["wb"]
    pA, pB = ctx["pA"], ctx["pB"]
    if src_tiles is None:
        srcbuf, sbase, sstride, nk = zT, 1, ZSTRIDE, ND
    else:
        srcbuf, sbase, sstride, nk = src_tiles
    ready_evs = tuple(ready_evs) + (b.ev("dve"),)
    @b.blk.vector
    def _(v):
        b.wait(v, "dve", *ready_evs)
        for tt in range(NT):
            for h in range(nh_v):
                ee = v.memset(SAb[:, v65_col(cfg, tt, h) + 64: v65_col(cfg, tt, h) + 65], 1.0)
        b.bump("dve", ee)
        b.fence(v)
        b.bump("dve", v.tensor_copy(ctx["touch"][:], SAb[0:1, v65_col(cfg, 0, 0) + 64:v65_col(cfg, 0, 0) + 72]))
    nchunk = (nh_v * 64 + 511) // 512
    BANKS = ["pA", "pB", "pC", "pD"]
    for c in range(nchunk):
        cw = min(512, nh_v * 64 - c * 512)
        wev = [None]
        ci = ctx.get("wb_ci", 0); ctx["wb_ci"] = ci + 1
        @b.blk.gpsimd
        def _(gp, c=c, cw=cw, ci=ci):
            b.wait(gp, "pool", ctx["wbev"][ci % 2])
            wev[0] = b.bump("dG0" if ci % 2 == 0 else "dG1", gp.dma_start(
                out=wb[ci % 2][:, 0:nk * cw].rearrange("p (a q) -> p a q", a=nk),
                in_=ctx["IN"][w_name].rearrange("(a p) f -> p a f", p=128)[:, :, fcols + c * 512:fcols + c * 512 + cw]), 16)
        for tt in range(NT):
            bk = BANKS[tt % 4]
            p = ctx[bk]
            @b.blk.tensor
            def _(t, c=c, ci=ci, cw=cw, tt=tt, p=p, bk=bk):
                b.wait(t, "pe", wev[0], b.ev("dve") if tt == 0 and c == 0 else None,
                       *ready_evs)
                pk_wait(b, t, ctx, bk)
                for dt in range(nk):
                    ee = t.matmul(p[:, 0:cw],
                                  srcbuf[:, sbase + dt * sstride + tt * 128: sbase + dt * sstride + tt * 128 + 128],
                                  wb[ci % 2][:, dt * cw: dt * cw + cw],
                                  start=(dt == 0), stop=(dt == nk - 1 and bias_row is None))
                if bias_row is not None:
                    ee = t.matmul(p[:, 0:cw], ctx["onesb"][:, 0:128],
                                  SAb[0:1, bias_row + c * 512: bias_row + c * 512 + cw],
                                  start=False, stop=True)
                b.bump("pe", ee)
            ctx["wbev"][ci % 2] = b.ev("pe")
            @b.blk.vector
            def _(v, c=c, cw=cw, tt=tt, p=p):
                b.wait(v, "dve", b.ev("pe"))
                for hh in range(cw // 64):
                    h = c * 8 + hh
                    ee = v.tensor_copy(SAb[:, v65_col(cfg, tt, h):v65_col(cfg, tt, h) + 64],
                                       p[:, hh * 64:(hh + 1) * 64])
                b.bump("dve", ee)
            pk_set(ctx, b, bk, "dve")



def emit_l3a(ctx, src, dst, scale):
    b = ctx["b"]
    SA, SAb, arena = ctx["SA"], ctx["SAb"], ctx["arena"]
    zev = emit_ln(ctx, src, rms=False)
    BROW = 24832
    @b.blk.gpsimd
    def _(gp):
        b.wait(gp, "pool", zev)
        for r in range(3):
            b.bump("dG", gp.dma_start(out=SAb[0:1, BROW + r * 1024: BROW + (r + 1) * 1024],
                                      in_=ctx["IN"]["vecrows"][10 + r:11 + r, :]), 16)
    @b.blk.sync
    def _(sy):
        b.wait(sy, "dS", zev)
        b.bump("dS", sy.dma_start(out=arena[:, 2048:4096], in_=ctx["IN"]["c_posL3"]), 16)
    rdy = (b.ev("dG"), b.ev("dS"))
    cfg = dict(nh=16, kvh=16, q_off=0, k_off=16384, v65_off=0, pT_off=16640,
               shift="k_shift", mask_arena_off=None,
               kt_list=lambda qc: [(kt, None) for kt in range(16)])
    proj_fm(ctx, "l3a_Wqkv", 0, 0, 1024, ready_evs=rdy, bias_row=BROW,
            evac_scale=0.125)
    proj_fm(ctx, "l3a_Wqkv", 1024, 16384, 1024, ready_evs=rdy, bias_row=BROW + 1024,
            add_tile=2048)
    proj_v65(ctx, "l3a_Wqkv", 2048, 16, cfg, ready_evs=rdy, bias_row=BROW + 2048)
    cfg["ready_ev"] = b.ev("dve")
    emit_attn_core(ctx, cfg)
    emit_outproj_residual(ctx, [src], dst, "l3a_Wout", bias_col_name="l3a_bout")




def emit_rope(ctx, targets, cos_off, sin_off, rot_wb_off, ready_evs=()):
    b = ctx["b"]
    SA, SAb, arena, wb = ctx["SA"], ctx["SAb"], ctx["arena"], ctx["wb"]
    pD = ctx["pD"]
    TMP1, TMP2 = 24832, 25344
    for (off, ntiles) in targets:
        for tl in range(ntiles):
            for tc in range(4):
                sl = SA[:, off + tl * T + tc * 512: off + tl * T + tc * 512 + 512]
                @b.blk.tensor
                def _(t, sl=sl):
                    b.wait(t, "pe", *ready_evs)
                    b.wait(t, "pe", b.ev("dve"))
                    b.bump("pe", t.matmul(pD[:], wb[0][:, rot_wb_off:rot_wb_off + 128],
                                          sl, start=True, stop=True))
                @b.blk.vector
                def _(v, sl=sl, tc=tc):
                    b.wait(v, "dve", b.ev("pe"))
                    b.bump("dve", v.tensor_tensor(
                        SAb[:, TMP1:TMP1 + 512], sl,
                        arena[:, cos_off + tc * 512:cos_off + tc * 512 + 512], op=ALU.mult))
                    b.bump("dve", v.tensor_tensor(
                        SAb[:, TMP2:TMP2 + 512], pD[:],
                        arena[:, sin_off + tc * 512:sin_off + tc * 512 + 512], op=ALU.mult))
                    b.fence(v)
                    b.bump("dve", v.tensor_tensor(sl, SAb[:, TMP1:TMP1 + 512],
                                                  SAb[:, TMP2:TMP2 + 512], op=ALU.add))



def emit_l1a(ctx, src, dst, scale):
    b = ctx["b"]
    SA, SAb, arena, wb = ctx["SA"], ctx["SAb"], ctx["arena"], ctx["wb"]
    zev = emit_ln(ctx, src, rms=False)
    @b.blk.sync
    def _(sy):
        b.wait(sy, "dS", zev)
        b.bump("dS", sy.dma_start(out=arena[:, 2048:4096], in_=ctx["IN"]["c_ropeL1"][0]), 16)
        b.bump("dS", sy.dma_start(out=arena[:, 4096:6144], in_=ctx["IN"]["c_ropeL1"][1]), 16)
    rdy = (b.ev("dS"),)
    cfg = dict(nh=16, kvh=16, q_off=0, k_off=16384, v65_off=0, pT_off=16640,
               shift="k_shift", mask_arena_off=2048,
               kt_list=lambda qc: [(kt, kt - qc * 4 + 1)
                                   for kt in range(max(0, qc * 4 - 1), qc * 4 + 4)])
    proj_fm(ctx, "l1a_WqT", 0, 0, 1024, ready_evs=rdy, evac_scale=0.125)
    proj_fm(ctx, "l1a_WkT", 0, 16384, 1024, ready_evs=rdy)
    proj_v65(ctx, "l1a_WvT", 0, 16, cfg, ready_evs=rdy)
    rev = [None]
    @b.blk.gpsimd
    def _(gp):
        b.wait(gp, "pool", ctx["wbev"][0])
        rev[0] = b.bump("dG0", gp.dma_start(out=wb[0][:, 0:128], in_=ctx["IN"]["c_rotH"]), 16)
    emit_rope(ctx, [(0, 8), (16384, 8)], 2048, 4096, 0, ready_evs=(rev[0],))
    ctx["wbev"][0] = b.ev("pe")
    mev = [None]
    dguard = b.ev("dve")
    @b.blk.sync
    def _(sy, dguard=dguard):
        b.wait(sy, "dS", dguard)
        mev[0] = b.bump("dS", sy.dma_start(
            out=arena[:, 2048:2048 + 5 * 512].rearrange("p (a c) -> p a c", a=5),
            in_=ctx["IN"]["c_maskL1"].rearrange("a p c -> p a c")), 16)
    cfg["ready_ev"] = mev[0]
    emit_attn_core(ctx, cfg)
    emit_outproj_residual(ctx, [src], dst, "l1a_WoT", bias_col_name="l1a_bo")




def emit_qknorm(ctx, targets, qscale=None, ready_evs=()):
    b = ctx["b"]
    SA, arena = ctx["SA"], ctx["arena"]
    pD, pE = ctx["pD"], ctx["pE"]
    for (off, ntiles) in targets:
        for tl in range(ntiles):
            for tc in range(4):
                sl = SA[:, off + tl * T + tc * 512: off + tl * T + tc * 512 + 512]
                @b.blk.scalar
                def _(s, sl=sl):
                    b.wait(s, "act", b.ev("dve"), b.ev("pe"))
                    b.bump("act", s.activation(arena[:, 2048:2560], sl, AF.Square,
                                               bias=vc(ctx, "k_zero"), scale=1.0))
                @b.blk.tensor
                def _(t):
                    b.wait(t, "pe", b.ev("act"), *ready_evs)
                    b.bump("pe", t.matmul(pD[0:2, :], arena[:, 2600:2602],
                                          arena[:, 2048:2560], start=True, stop=True))
                @b.blk.vector
                def _(v):
                    b.wait(v, "dve", b.ev("pe"))
                    b.bump("dve", v.tensor_scalar(arena[0:2, 5120:5632], pD[0:2, :],
                                                  1.0 / 64, None, op0=ALU.mult))
                    b.fence(v)
                    b.bump("dve", v.tensor_copy(ctx["touch"][:], arena[0:2, 5120:5128][0:1, :]))
                @b.blk.scalar
                def _(s):
                    b.wait(s, "act", b.ev("dve"))
                    b.bump("act", s.activation(arena[0:2, 5120:5632], arena[0:2, 5120:5632],
                                               AF.Sqrt, bias=ctx["vcol"][0:2, VECCOLS["k_eps6"][0]:VECCOLS["k_eps6"][0] + 1],
                                               scale=1.0))
                @b.blk.vector
                def _(v, qscale=qscale):
                    b.wait(v, "dve", b.ev("act"))
                    b.bump("dve", v.reciprocal(arena[0:2, 5120:5632], arena[0:2, 5120:5632]))
                    b.fence(v)
                    if qscale is not None:
                        b.bump("dve", v.tensor_scalar(arena[0:2, 5120:5632], arena[0:2, 5120:5632],
                                                      qscale, None, op0=ALU.mult))
                        b.fence(v)
                    b.bump("dve", v.tensor_copy(ctx["touch"][:], arena[0:2, 5120:5128][0:1, :]))
                @b.blk.tensor
                def _(t):
                    b.wait(t, "pe", b.ev("dve"))
                    b.bump("pe", t.matmul(pE[:, :], arena[0:2, 2604:2732],
                                          arena[0:2, 5120:5632], start=True, stop=True))
                @b.blk.vector
                def _(v, sl=sl):
                    b.wait(v, "dve", b.ev("pe"))
                    b.bump("dve", v.tensor_tensor(sl, sl, pE[:], op=ALU.mult))



def emit_l2a(ctx, src, dst, scale, halves=(0, 1)):
    b = ctx["b"]
    SA, SAb, SAf, arena, zT, wb = (ctx["SA"], ctx["SAb"], ctx["SAf"],
                                   ctx["arena"], ctx["zT"], ctx["wb"])
    hxi = ctx["hxi"]
    cfgs = [("l2a_gwq", "l2a_gkvd", "l2a_gkup", "l2a_gvup", "l2a_gwo", "l2a_n1",
             "c_ropeG", None, None),
            ("l2a_lwq", "l2a_lwk", None, "l2a_lwv", "l2a_lwo", "l2a_n2",
             "c_ropeL", None, None)]
    for half in halves:
        (wq, wkd, wku, wvu, wo, nw, ropec, src_h, dsts) = cfgs[half]
        glob = (half == 0)
        if glob:
            emit_ln(ctx, src, rms=False, want_zlo=True)
            @b.blk.sync
            def _(sy):
                b.wait(sy, "dS", b.ev("dve"))
                for dt in range(ND):
                    b.bump("dS", sy.dma_start(out=hxi[dt * 128:(dt + 1) * 128, :],
                                              in_=SAf[:, dt * T:(dt + 1) * T]), 16)
            emit_ln(ctx, None, rms=True, w_col_name="l2a_n1", eps_name="k_eps6",
                    resident=True)
        else:
            emit_ln(ctx, dst, rms=True, w_col_name="l2a_n2", eps_name="k_eps6")
        cev = [None]
        @b.blk.sync
        def _(sy):
            b.wait(sy, "dS", b.ev("dve"), b.ev("pe"))
            b.bump("dS", sy.dma_start(out=arena[:, 2048:4096], in_=ctx["IN"][ropec][0]), 16)
            cev[0] = b.bump("dS", sy.dma_start(out=arena[:, 4096:6144],
                                               in_=ctx["IN"][ropec][1]), 16)
        rdy = (cev[0],)
        cfg = dict(nh=16, kvh=4, q_off=0, k_off=16384, v65_off=0, pT_off=16640,
                   shift="k_shift8", mask_arena_off=2048)
        if glob:
            cfg["kt_list"] = lambda qc: [(kt, kt - qc * 4 if kt >= qc * 4 else None)
                                         for kt in range(0, qc * 4 + 4)]
        else:
            cfg["kt_list"] = lambda qc: [(kt, kt - qc * 4 + 2)
                                         for kt in range(max(0, qc * 4 - 2), qc * 4 + 4)]
        proj_fm(ctx, wq, 0, 0, 1024, ready_evs=rdy)
        if glob:
            proj_fm(ctx, wkd, 0, 16384, 256, ready_evs=rdy)
            lat_src = (SA, 16384, T, 2)
            proj_fm(ctx, wku, 0, 20480, 256, ready_evs=rdy, src_tiles=lat_src)
            proj_v65(ctx, wvu, 0, 4, cfg, ready_evs=rdy, src_tiles=lat_src)
        else:
            proj_fm(ctx, wkd, 0, 20480, 256, ready_evs=rdy)
            proj_v65(ctx, wvu, 0, 4, cfg, ready_evs=rdy)
        rev = [None]
        @b.blk.gpsimd
        def _(gp):
            b.wait(gp, "pool", ctx["wbev"][0])
            rev[0] = b.bump("dG0", gp.dma_start(out=wb[0][:, 0:128], in_=ctx["IN"]["c_rotP"]), 16)
        emit_rope(ctx, [(0, 8), (20480, 2)], 2048, 4096, 0, ready_evs=(rev[0],))
        ctx["wbev"][0] = b.ev("pe")
        iev = [None]
        rguard = b.ev("dve")
        @b.blk.sync
        def _(sy, rguard=rguard):
            b.wait(sy, "dS", rguard)
            b.bump("dS", sy.dma_start(out=arena[:, 2600:2602], in_=ctx["IN"]["c_ind2"]), 16)
            iev[0] = b.bump("dS", sy.dma_start(out=arena[0:2, 2604:2732],
                                               in_=ctx["IN"]["c_ind2T"]), 16)
        emit_qknorm(ctx, [(0, 8)], qscale=0.125, ready_evs=(iev[0],))
        emit_qknorm(ctx, [(20480, 2)], ready_evs=(iev[0],))
        @b.blk.vector
        def _(v):
            b.wait(v, "dve", b.ev("dve"), b.ev("pe"), b.ev("act"))
            b.bump("dve", v.tensor_copy(SAb[:, 4160:6208], SA[:, 20480:22528]))
            b.bump("dve", v.tensor_copy(SAb[:, 6208:8256], SA[:, 22528:24576]))
            b.fence(v)
            for h in range(16):
                g = h // 4
                ee = v.tensor_copy(SA[(h % 2) * 64:(h % 2) * 64 + 64,
                                      16384 + (h // 2) * T:16384 + (h // 2) * T + T],
                                   SAb[(g % 2) * 64:(g % 2) * 64 + 64,
                                       4160 + (g // 2) * T:4160 + (g // 2) * T + T])
            b.bump("dve", ee)
        mev = [None]
        dguard = b.ev("dve")
        @b.blk.sync
        def _(sy, glob=glob, dguard=dguard):
            b.wait(sy, "dS", dguard)
            if glob:
                mev[0] = b.bump("dS", sy.dma_start(
                    out=arena[:, 2048:2048 + 4 * 512].rearrange("p (a c) -> p a c", a=4),
                    in_=ctx["IN"]["c_maskL2g"].rearrange("a p c -> p a c")), 16)
            else:
                mev[0] = b.bump("dS", sy.dma_start(
                    out=arena[:, 2048:2048 + 6 * 512].rearrange("p (a c) -> p a c", a=6),
                    in_=ctx["IN"]["c_maskL2l"].rearrange("a p c -> p a c")), 16)
        cfg["ready_ev"] = mev[0]
        emit_attn_core(ctx, cfg)
        if glob:
            emit_outproj_residual(ctx, [hxi], dst, wo)
        else:
            emit_outproj_residual(ctx, [src, dst], dst, wo)





def emit_l0a(ctx, src, dst, scale):
    b = ctx["b"]
    SA, SAf, arena, zT, wb = ctx["SA"], ctx["SAf"], ctx["arena"], ctx["zT"], ctx["wb"]
    pA, pB, pStat = ctx["pA"], ctx["pB"], ctx["pStat"]
    hxi = ctx["hxi"]
    zev = emit_ln(ctx, src, rms=False)
    for tc in range(4):
        sq_evs = []
        @b.blk.scalar
        def _(s, tc=tc):
            b.wait(s, "act", zev, b.ev("pe"))
            for dt in range(ND):
                sq_evs.append(b.bump("act", s.activation(
                    SA[:, 16384 + dt * 512:16384 + dt * 512 + 512],
                    zT[:, zslc(dt, tc)], AF.Square, bias=vc(ctx, "k_zero"), scale=1.0)))
        @b.blk.tensor
        def _(t, tc=tc):
            b.wait(t, "pe", b.ev("dve"))
            for dt in range(ND):
                b.wait(t, "pe", sq_evs[dt])
                ee = t.matmul(pStat[32:33, :], ctx["onescb"][:, :],
                              SA[:, 16384 + dt * 512:16384 + dt * 512 + 512],
                              start=(dt == 0), stop=(dt == ND - 1))
            b.bump("pe", ee)
        @b.blk.vector
        def _(v, tc=tc):
            b.wait(v, "dve", b.ev("pe"))
            b.bump("dve", v.tensor_copy(arena[0:1, tc * 512:(tc + 1) * 512], pStat[32:33, :]))
    @b.blk.vector
    def _(v):
        b.wait(v, "dve", b.ev("dve"))
        b.bump("dve", v.tensor_scalar(arena[0:1, 0:T], arena[0:1, 0:T], 1.0 / D, None,
                                      op0=ALU.mult))
        b.fence(v)
        b.bump("dve", v.tensor_copy(ctx["touch"][:], arena[0:1, 0:8]))
    @b.blk.scalar
    def _(s):
        b.wait(s, "act", b.ev("dve"))
        o6 = VECCOLS["k_eps6"][0]
        b.bump("act", s.activation(arena[0:1, 0:T], arena[0:1, 0:T], AF.Sqrt,
                                   bias=ctx["vcol"][0:1, o6:o6 + 1], scale=1.0))
    @b.blk.vector
    def _(v):
        b.wait(v, "dve", b.ev("act"))
        b.bump("dve", v.reciprocal(arena[0:1, 0:T], arena[0:1, 0:T]))
        b.fence(v)
        b.bump("dve", v.tensor_copy(ctx["touch"][:], arena[0:1, 0:8]))
    for tc in range(4):
        @b.blk.tensor
        def _(t, tc=tc):
            b.wait(t, "pe", b.ev("dve"))
            b.bump("pe", t.matmul(ctx["pE"][:, :], ctx["onesf"][:, 0:128],
                                  arena[0:1, tc * 512:(tc + 1) * 512], start=True, stop=True))
        @b.blk.vector
        def _(v, tc=tc):
            b.wait(v, "dve", b.ev("pe"))
            b.bump("dve", v.tensor_copy(arena[:, 2048 + tc * 512:2048 + (tc + 1) * 512],
                                        ctx["pE"][:]))
    @b.blk.vector
    def _(v):
        b.wait(v, "dve", b.ev("dve"))
        for dt in range(ND):
            ee = v.scalar_tensor_tensor(SA[:, dt * T:(dt + 1) * T], zT[:, zslc(dt)],
                                        vc(ctx, "l0a_ssmw", dt), arena[:, 2048:2048 + T],
                                        op0=ALU.mult, op1=ALU.mult)
        b.bump("dve", ee)
        b.fence(v)
        b.bump("dve", v.tensor_copy(ctx["touch"][:], SA[0:1, 0:8]))
    uev = b.ev("dve")
    wev = [None]
    @b.blk.gpsimd
    def _(gp):
        b.wait(gp, "pool", ctx["wbev"][0])
        wev[0] = b.bump("dG0", gp.dma_start(
            out=wb[0][:, 0:2048].rearrange("p (a q) -> p a q", a=8),
            in_=ctx["IN"]["l0a_BwT"].rearrange("(a p) f -> p a f", p=128)), 16)
    for fi in range(2):
        for tc in range(4):
            p = pA if tc % 2 == 0 else pB
            @b.blk.tensor
            def _(t, fi=fi, tc=tc, p=p):
                b.wait(t, "pe", wev[0], uev, b.ev("dve"))
                for dt in range(ND):
                    ee = t.matmul(p[:], wb[0][:, dt * 256 + fi * 128: dt * 256 + fi * 128 + 128],
                                  SA[:, dt * T + tc * 512: dt * T + tc * 512 + 512],
                                  start=(dt == 0), stop=(dt == ND - 1))
                b.bump("pe", ee)
            @b.blk.vector
            def _(v, fi=fi, tc=tc, p=p):
                b.wait(v, "dve", b.ev("pe"))
                b.bump("dve", v.tensor_copy(SAf[:, fi * T + tc * 512: fi * T + tc * 512 + 512],
                                            p[:]))
            ctx["wbev"][0] = b.ev("pe")
    @b.blk.scalar
    def _(s):
        b.wait(s, "act", b.ev("dve"))
        o = VECCOLS["l0a_loglam"][0]
        b.bump("act", s.activation(arena[:, 0:2], ctx["vcol"][:, o:o + 2], AF.Sigmoid,
                                   bias=vc(ctx, "k_zero"), scale=1.0))
    for nt in range(2):
        @b.blk.vector
        def _(v, nt=nt):
            b.wait(v, "dve", b.ev("act"), b.ev("pe"))
            reg = arena[:, 2048 + nt * 2048:2048 + nt * 2048 + T]
            b.bump("dve", v.memset(reg, 0.0))
            b.fence(v)
            b.bump("dve", v.tensor_scalar(reg, reg, arena[:, nt:nt + 1], None, op0=ALU.add))
            b.fence(v)
            b.bump("dve", v.tensor_tensor_scan(SA[:, 20480 + nt * T:20480 + (nt + 1) * T],
                                               reg, SAf[:, nt * T:(nt + 1) * T], 0.0,
                                               op0=ALU.mult, op1=ALU.add))
    scanev = b.ev("dve")
    for c in range(2):
        wev2 = [None]
        @b.blk.gpsimd
        def _(gp, c=c):
            b.wait(gp, "pool", ctx["wbev"][1])
            wev2[0] = b.bump("dG1", gp.dma_start(
                out=wb[1][:, 0:1024].rearrange("p (a q) -> p a q", a=2),
                in_=ctx["IN"]["l0a_CwT"].rearrange("(a p) f -> p a f", p=128)[:, :, c * 512:(c + 1) * 512]), 16)
        for fi in range(4):
            ft = c * 4 + fi
            for tc in range(4):
                p = pA if (fi * 4 + tc) % 2 == 0 else pB
                @b.blk.tensor
                def _(t, c=c, fi=fi, ft=ft, tc=tc, p=p):
                    b.wait(t, "pe", wev2[0], scanev, b.ev("dve"))
                    for a in range(2):
                        ee = t.matmul(p[:], wb[1][:, a * 512 + fi * 128: a * 512 + fi * 128 + 128],
                                      SA[:, 20480 + a * T + tc * 512: 20480 + a * T + tc * 512 + 512],
                                      start=(a == 0), stop=(a == 1))
                    b.bump("pe", ee)
                @b.blk.vector
                def _(v, ft=ft, tc=tc, p=p):
                    b.wait(v, "dve", b.ev("pe"))
                    xs = SAf[:, ft * T + tc * 512: ft * T + tc * 512 + 512]
                    b.bump("dve", v.scalar_tensor_tensor(xs, SA[:, ft * T + tc * 512: ft * T + tc * 512 + 512],
                                                         vc(ctx, "l0a_Dskip", ft), p[:],
                                                         op0=ALU.mult, op1=ALU.add))
                    b.fence(v)
                    b.bump("dve", v.tensor_tensor(xs, xs, zT[:, zslc(ft, tc)], op=ALU.add))
                    b.fence(v)
                    b.bump("dve", v.tensor_copy(ctx["touch"][:], xs[0:1, 0:8]))
                ctx["wbev"][1] = b.ev("pe")
    @b.blk.sync
    def _(sy):
        b.wait(sy, "dS", b.ev("dve"), b.ev("dS"))
        for dt in range(ND):
            b.bump("dS", sy.dma_start(out=hxi[dt * 128:(dt + 1) * 128, :],
                                      in_=SAf[:, dt * T:(dt + 1) * T]), 16)
    z2ev = emit_ln(ctx, None, rms=True, w_col_name="l0a_ffnw", eps_name="k_eps6", resident=True)
    NFT = 22
    for th in range(2):
        tbase = th * 1024
        for cch in range(6):
            cw = min(512, 2736 - cch * 512)
            nf_t = (cw + 127) // 128
            wev3 = [None, None]
            @b.blk.gpsimd
            def _(gp, cch=cch, cw=cw):
                b.wait(gp, "pool", ctx["wbev"][0], ctx["wbev"][1])
                wev3[1] = b.bump("dG0", gp.dma_start(
                    out=wb[0][:, 0:8 * cw].rearrange("p (a q) -> p a q", a=8),
                    in_=ctx["IN"]["l0a_w1T"].rearrange("(a p) f -> p a f", p=128)[:, :, cch * 512:cch * 512 + cw]), 16)
                wev3[0] = b.bump("dG1", gp.dma_start(
                    out=wb[1][:, 0:8 * cw].rearrange("p (a q) -> p a q", a=8),
                    in_=ctx["IN"]["l0a_w3T"].rearrange("(a p) f -> p a f", p=128)[:, :, cch * 512:cch * 512 + cw]), 16)
            for fi in range(nf_t):
                fw = min(128, cw - fi * 128)
                ft = cch * 4 + fi
                for tc2 in range(2):
                    tsl = slice(tbase + tc2 * 512, tbase + tc2 * 512 + 512)
                    @b.blk.tensor
                    def _(t, cch=cch, cw=cw, fi=fi, fw=fw, tsl=tsl):
                        b.wait(t, "pe", wev3[0], wev3[1], z2ev)
                        pk_wait(b, t, ctx, "pA")
                        pk_wait(b, t, ctx, "pB")
                        for dt in range(ND):
                            t.matmul(pA[0:fw, :], wb[0][:, dt * cw + fi * 128: dt * cw + fi * 128 + fw],
                                     zT[:, dt * ZSTRIDE + 1 + tsl.start: dt * ZSTRIDE + 1 + tsl.start + 512],
                                     start=(dt == 0), stop=(dt == ND - 1))
                        for dt in range(ND):
                            ee = t.matmul(pB[0:fw, :], wb[1][:, dt * cw + fi * 128: dt * cw + fi * 128 + fw],
                                          zT[:, dt * ZSTRIDE + 1 + tsl.start: dt * ZSTRIDE + 1 + tsl.start + 512],
                                          start=(dt == 0), stop=(dt == ND - 1))
                        b.bump("pe", ee)
                    ctx["wbev"][0] = b.ev("pe")
                    ctx["wbev"][1] = b.ev("pe")
                    @b.blk.scalar
                    def _(s, fw=fw):
                        b.wait(s, "act", b.ev("pe"), b.ev("dve"))
                        b.bump("act", s.activation(arena[0:fw, 2048:2560], pA[0:fw, :],
                                                   AF.Silu,
                                                   bias=ctx["vcol"][0:fw, VECCOLS["k_zero"][0]:VECCOLS["k_zero"][0] + 1],
                                                   scale=1.0))
                    @b.blk.vector
                    def _(v, ft=ft, fw=fw, tc2=tc2):
                        b.wait(v, "dve", b.ev("act"))
                        b.bump("dve", v.tensor_tensor(
                            SA[0:fw, ft * 1024 + tc2 * 512: ft * 1024 + tc2 * 512 + 512],
                            arena[0:fw, 2048:2560], pB[0:fw, :], op=ALU.mult))
                    pk_set(ctx, b, "pA", "act")
                    pk_set(ctx, b, "pB", "dve")
        for do in range(ND):
            wev4 = [None]
            xev = [None]
            guard = (b.ev("pe"),)
            @b.blk.gpsimd
            def _(gp, do=do):
                b.wait(gp, "pool", ctx["wbev"][do % 2])
                wev4[0] = b.bump("dG0" if do % 2 == 0 else "dG1", gp.dma_start(
                    out=wb[do % 2][:, 0:2816].rearrange("p (a q) -> p a q", a=22),
                    in_=ctx["IN"]["l0a_w2T"].rearrange("(a p) f -> p a f", p=128)[:, :, do * 128:(do + 1) * 128]), 16)
            dguard = b.ev("dve")
            @b.blk.sync
            def _(sy, do=do, th=th, dguard=dguard):
                b.wait(sy, "dS", dguard, b.ev("dS"))
                xev[0] = b.bump("dS", sy.dma_start(
                    out=arena[:, 4096 + (do % 2) * 1024:4096 + (do % 2) * 1024 + 1024],
                    in_=src[do * 128:(do + 1) * 128, th * 1024:(th + 1) * 1024]), 16)
            for tc2 in range(2):
                bk = ["pA", "pB", "pC", "pD"][(do * 2 + tc2) % 4]
                p = ctx[bk]
                @b.blk.tensor
                def _(t, do=do, tc2=tc2, p=p, bk=bk):
                    b.wait(t, "pe", wev4[0], b.ev("dve") if do == 0 and tc2 == 0 else None)
                    pk_wait(b, t, ctx, bk)
                    for a in range(22):
                        ee = t.matmul(p[:], wb[do % 2][:, a * 128:(a + 1) * 128],
                                      SA[:, a * 1024 + tc2 * 512: a * 1024 + tc2 * 512 + 512],
                                      start=(a == 0), stop=(a == 21))
                    b.bump("pe", ee)
                ctx["wbev"][do % 2] = b.ev("pe")
                @b.blk.vector
                def _(v, do=do, th=th, tc2=tc2, p=p):
                    b.wait(v, "dve", b.ev("pe"), xev[0])
                    tmp = arena[:, 2048:2560]
                    b.bump("dve", v.tensor_tensor(tmp, p[:],
                                                  SAf[:, do * T + th * 1024 + tc2 * 512: do * T + th * 1024 + tc2 * 512 + 512],
                                                  op=ALU.add))
                    b.fence(v)
                    xs = arena[:, 4096 + (do % 2) * 1024 + tc2 * 512: 4096 + (do % 2) * 1024 + tc2 * 512 + 512]
                    b.bump("dve", v.tensor_tensor(xs, xs, tmp, op=ALU.add))
                    b.fence(v)
                    b.bump("dve", v.tensor_copy(ctx["touch"][:], xs[0:1, 0:8]))
                pk_set(ctx, b, bk, "dve")
            @b.blk.sync
            def _(sy, do=do, th=th):
                b.wait(sy, "dS", b.ev("dve"))
                b.bump("dS", sy.dma_start(
                    out=dst[do * 128:(do + 1) * 128, th * 1024:(th + 1) * 1024],
                    in_=arena[:, 4096 + (do % 2) * 1024:4096 + (do % 2) * 1024 + 1024]), 16)



def emit_top2_gate(ctx, gw_name):
    b = ctx["b"]
    SA, arena, zT, wb = ctx["SA"], ctx["arena"], ctx["zT"], ctx["wb"]
    pD = ctx["pD"]
    gev = [None]
    guard = (b.ev("pe"), b.ev("dve"))
    @b.blk.sync
    def _(sy):
        b.wait(sy, "dS", *guard)
        gev[0] = b.bump("dS", sy.dma_start(
            out=arena[:, 1536:1600].rearrange("p (a c) -> p a c", a=8),
            in_=ctx["IN"][gw_name].rearrange("(a p) f -> p a f", p=128)), 16)
    hlev = [None]
    @b.blk.vector
    def _(v):
        b.wait(v, "dve", gev[0], ctx["wbev"][0], ctx["wbev"][1])
        b.bump("dve", v.tensor_copy(wb[0][:, 0:64], arena[:, 1536:1600]))
        b.fence(v)
        b.bump("dve", v.tensor_tensor(wb[1][:, 0:64], arena[:, 1536:1600], wb[0][:, 0:64],
                                      op=ALU.subtract))
        b.fence(v)
        hlev[0] = b.bump("dve", v.tensor_copy(ctx["touch"][:], wb[1][0:1, 0:8]))
    for tt in range(NT):
        @b.blk.tensor
        def _(t, tt=tt):
            b.wait(t, "pe", hlev[0], b.ev("dve"))
            for dt in range(ND):
                zhi = zT[:, dt * ZSTRIDE + 1 + tt * 128: dt * ZSTRIDE + 1 + tt * 128 + 128]
                zlo = SA[:, dt * T + tt * 128: dt * T + tt * 128 + 128]
                t.matmul(pD[:, 0:8], zhi, wb[0][:, dt * 8:(dt + 1) * 8],
                         start=(dt == 0), stop=False)
                t.matmul(pD[:, 0:8], zhi, wb[1][:, dt * 8:(dt + 1) * 8],
                         start=False, stop=False)
                ee = t.matmul(pD[:, 0:8], zlo, wb[0][:, dt * 8:(dt + 1) * 8],
                              start=False, stop=(dt == ND - 1))
            b.bump("pe", ee)
        ctx["wbev"][0] = b.ev("pe")
        ctx["wbev"][1] = b.ev("pe")
        @b.blk.vector
        def _(v, tt=tt):
            b.wait(v, "dve", b.ev("pe"))
            s8 = arena[:, 1280:1288]
            b.bump("dve", v.tensor_copy(s8, pD[:, 0:8]))
            b.fence(v)
            b.bump("dve", v.reduce_max(arena[:, 1288:1289], s8, axis=AX.X))
            b.fence(v)
            b.bump("dve", v.tensor_scalar(arena[:, 1289:1297], s8, arena[:, 1288:1289],
                                          None, op0=ALU.is_equal))
            b.fence(v)
            b.bump("dve", v.scalar_tensor_tensor(arena[:, 1297:1305], arena[:, 1289:1297],
                                                 -1e30, s8, op0=ALU.mult, op1=ALU.add))
            b.fence(v)
            b.bump("dve", v.reduce_max(arena[:, 1305:1306], arena[:, 1297:1305], axis=AX.X))
            b.fence(v)
            v.tensor_scalar(arena[:, 1306:1314], s8, arena[:, 1305:1306], None,
                            op0=ALU.is_ge)
            v.tensor_scalar(arena[:, 1314:1322], s8, arena[:, 1288:1289], None,
                            op0=ALU.subtract)
            ee = v.tensor_tensor(arena[:, 1322:1323], arena[:, 1305:1306],
                                 arena[:, 1288:1289], op=ALU.subtract)
            b.bump("dve", ee)
        @b.blk.scalar
        def _(s, tt=tt):
            b.wait(s, "act", b.ev("dve"))
            s.activation(arena[:, 1314:1322], arena[:, 1314:1322], AF.Exp,
                         bias=vc(ctx, "k_zero"), scale=1.0)
            ee = s.activation(arena[:, 1322:1323], arena[:, 1322:1323], AF.Exp,
                              bias=vc(ctx, "k_zero"), scale=1.0)
            b.bump("act", ee)
        @b.blk.vector
        def _(v, tt=tt):
            b.wait(v, "dve", b.ev("act"))
            b.bump("dve", v.tensor_scalar(arena[:, 1322:1323], arena[:, 1322:1323], 1.0,
                                          None, op0=ALU.add))
            b.fence(v)
            b.bump("dve", v.reciprocal(arena[:, 1323:1324], arena[:, 1322:1323]))
            b.fence(v)
            b.bump("dve", v.tensor_tensor(arena[:, 1314:1322], arena[:, 1314:1322],
                                          arena[:, 1306:1314], op=ALU.mult))
            b.fence(v)
            ee = v.tensor_scalar(arena[:, 1024 + tt * 8:1024 + tt * 8 + 8],
                                 arena[:, 1314:1322], arena[:, 1323:1324], None,
                                 op0=ALU.mult)
            b.bump("dve", ee)
    return b.ev("dve")



def emit_l0f(ctx, src, dst, scale):
    b = ctx["b"]
    SA, SAf, arena, zT = ctx["SA"], ctx["SAf"], ctx["arena"], ctx["zT"]
    wb, pA, pB = ctx["wb"], ctx["pA"], ctx["pB"]
    emit_ln(ctx, src, rms=False, want_zlo=True)
    gatev = emit_top2_gate(ctx, "l0f_gwT")
    BANKS = ["pA", "pB", "pC", "pD"]
    for e in range(8):
        for fchunk in range(8):
            ci = e * 8 + fchunk
            wev = [None]
            @b.blk.gpsimd
            def _(gp, e=e, fchunk=fchunk, ci=ci):
                b.wait(gp, "pool", ctx["wbev"][ci % 2])
                wev[0] = b.bump("dG0" if ci % 2 == 0 else "dG1", gp.dma_start(
                    out=wb[ci % 2][:, :].rearrange("p (a c) -> p a c", a=8),
                    in_=ctx["IN"]["l0f_w12T"].rearrange("e (a p) f -> e p a f", p=128)[e][:, :, fchunk * 512:(fchunk + 1) * 512]), 16)
            for fi in range(4):
                ft = fchunk * 4 + fi
                for half in range(2):
                    bk0, bk1 = ("pA", "pB") if half == 0 else ("pC", "pD")
                    p0, p1 = ctx[bk0], ctx[bk1]
                    @b.blk.tensor
                    def _(t, e=e, ci=ci, fi=fi, ft=ft, half=half, p0=p0, p1=p1,
                          bk0=bk0, bk1=bk1):
                        b.wait(t, "pe", wev[0], gatev)
                        pk_wait(b, t, ctx, bk0)
                        pk_wait(b, t, ctx, bk1)
                        for dt in range(ND):
                            w_sl = wb[ci % 2][:, dt * 512 + fi * 128: dt * 512 + fi * 128 + 128]
                            t.matmul(p0[:], w_sl, zT[:, zslc(dt, 2 * half)],
                                     start=(dt == 0), stop=(dt == ND - 1))
                            ee = t.matmul(p1[:], w_sl, zT[:, zslc(dt, 2 * half + 1)],
                                          start=(dt == 0), stop=(dt == ND - 1))
                        b.bump("pe", ee)
                    ctx["wbev"][ci % 2] = b.ev("pe")
                    if ft < 16:
                        @b.blk.scalar
                        def _(s, ft=ft, half=half, p0=p0, p1=p1):
                            b.wait(s, "act", b.ev("pe"))
                            s.activation(SA[:, ft * T + half * 1024: ft * T + half * 1024 + 512],
                                         p0[:], AF.Silu, bias=vc(ctx, "k_zero"), scale=1.0)
                            b.bump("act", s.activation(
                                SA[:, ft * T + half * 1024 + 512: ft * T + half * 1024 + 1024],
                                p1[:], AF.Silu, bias=vc(ctx, "k_zero"), scale=1.0))
                        pk_set(ctx, b, bk0, "act")
                        pk_set(ctx, b, bk1, "act")
                    else:
                        @b.blk.vector
                        def _(v, ft=ft, half=half, p0=p0, p1=p1):
                            b.wait(v, "dve", b.ev("pe"), b.ev("act"))
                            hs0 = SA[:, (ft - 16) * T + half * 1024: (ft - 16) * T + half * 1024 + 512]
                            hs1 = SA[:, (ft - 16) * T + half * 1024 + 512: (ft - 16) * T + half * 1024 + 1024]
                            v.tensor_tensor(hs0, hs0, p0[:], op=ALU.mult)
                            b.bump("dve", v.tensor_tensor(hs1, hs1, p1[:], op=ALU.mult))
                        pk_set(ctx, b, bk0, "dve")
                        pk_set(ctx, b, bk1, "dve")
        for dc in range(2):
            wev = [None, None]
            @b.blk.gpsimd
            def _(gp, e=e, dc=dc):
                b.wait(gp, "pool", ctx["wbev"][0], ctx["wbev"][1])
                wev[0] = b.bump("dG0", gp.dma_start(
                    out=wb[0][:, :].rearrange("p (a c) -> p a c", a=8),
                    in_=ctx["IN"]["l0f_w3T"].rearrange("e (a p) f -> e p a f", p=128)[e][:, 0:8, dc * 512:(dc + 1) * 512]), 16)
                wev[1] = b.bump("dG1", gp.dma_start(
                    out=wb[1][:, :].rearrange("p (a c) -> p a c", a=8),
                    in_=ctx["IN"]["l0f_w3T"].rearrange("e (a p) f -> e p a f", p=128)[e][:, 8:16, dc * 512:(dc + 1) * 512]), 16)
            for tt in range(NT):
                bk = BANKS[tt % 4]
                p = ctx[bk]
                @b.blk.tensor
                def _(t, e=e, dc=dc, tt=tt, p=p, bk=bk):
                    b.wait(t, "pe", wev[0], wev[1], b.ev("dve") if tt == 0 and dc == 0 else None)
                    pk_wait(b, t, ctx, bk)
                    for ht in range(16):
                        ee = t.matmul(p[:], SA[:, ht * T + tt * 128: ht * T + tt * 128 + 128],
                                      wb[ht // 8][:, (ht % 8) * 512:(ht % 8) * 512 + 512],
                                      start=(ht == 0), stop=(ht == 15))
                    b.bump("pe", ee)
                ctx["wbev"][0] = b.ev("pe")
                ctx["wbev"][1] = b.ev("pe")
                @b.blk.vector
                def _(v, e=e, dc=dc, tt=tt, p=p):
                    b.wait(v, "dve", b.ev("pe"))
                    ysl = SAf[:, tt * 1024 + dc * 512: tt * 1024 + dc * 512 + 512]
                    ccol = arena[:, 1024 + tt * 8 + e:1024 + tt * 8 + e + 1]
                    if e == 0:
                        ee = v.tensor_scalar(ysl, p[:], ccol, None, op0=ALU.mult)
                    else:
                        tmp = arena[:, 1600 + (tt % 2) * 512:1600 + (tt % 2) * 512 + 512]
                        b.bump("dve", v.tensor_scalar(tmp, p[:], ccol, None, op0=ALU.mult))
                        b.fence(v)
                        ee = v.tensor_tensor(ysl, ysl, tmp, op=ALU.add)
                    b.bump("dve", ee)
                pk_set(ctx, b, "dve" if False else bk, "dve")
    emit_ytok_residual(ctx, src, dst, scale)



def emit_l3f(ctx, src, dst, scale):
    b = ctx["b"]
    SA, SAf, arena, zT = ctx["SA"], ctx["SAf"], ctx["arena"], ctx["zT"]
    wb, pA, pB = ctx["wb"], ctx["pA"], ctx["pB"]
    emit_ln(ctx, src, rms=False, want_zlo=True)
    gatev = emit_top2_gate(ctx, "l3f_rwT")
    BANKS = ["pA", "pB", "pC", "pD"]
    for e in range(8):
        for fh in range(2):
            for fchunk in range(4):
                ci = (e * 2 + fh) * 4 + fchunk
                wev = [None]
                @b.blk.gpsimd
                def _(gp, e=e, fh=fh, fchunk=fchunk, ci=ci):
                    b.wait(gp, "pool", ctx["wbev"][ci % 2])
                    wev[0] = b.bump("dG0" if ci % 2 == 0 else "dG1", gp.dma_start(
                        out=wb[ci % 2][:, :].rearrange("p (a c) -> p a c", a=8),
                        in_=ctx["IN"]["l3f_w1"].rearrange("e (a p) f -> e p a f", p=128)[e][:, :, fh * 2048 + fchunk * 512: fh * 2048 + (fchunk + 1) * 512]), 16)
                for fi in range(4):
                    ft = fchunk * 4 + fi
                    for half in range(2):
                        bk0, bk1 = ("pA", "pB") if half == 0 else ("pC", "pD")
                        p0, p1 = ctx[bk0], ctx[bk1]
                        @b.blk.tensor
                        def _(t, e=e, ci=ci, fi=fi, half=half, p0=p0, p1=p1,
                              bk0=bk0, bk1=bk1):
                            b.wait(t, "pe", wev[0], gatev)
                            pk_wait(b, t, ctx, bk0)
                            pk_wait(b, t, ctx, bk1)
                            for dt in range(ND):
                                w_sl = wb[ci % 2][:, dt * 512 + fi * 128: dt * 512 + fi * 128 + 128]
                                t.matmul(p0[:], w_sl, zT[:, zslc(dt, 2 * half)],
                                         start=(dt == 0), stop=(dt == ND - 1))
                                ee = t.matmul(p1[:], w_sl, zT[:, zslc(dt, 2 * half + 1)],
                                              start=(dt == 0), stop=(dt == ND - 1))
                            b.bump("pe", ee)
                        ctx["wbev"][ci % 2] = b.ev("pe")
                        @b.blk.scalar
                        def _(s, ft=ft, half=half, p0=p0, p1=p1):
                            b.wait(s, "act", b.ev("pe"))
                            s.activation(SA[:, ft * T + half * 1024: ft * T + half * 1024 + 512],
                                         p0[:], AF.Silu, bias=vc(ctx, "k_zero"), scale=1.0)
                            b.bump("act", s.activation(
                                SA[:, ft * T + half * 1024 + 512: ft * T + half * 1024 + 1024],
                                p1[:], AF.Silu, bias=vc(ctx, "k_zero"), scale=1.0))
                        pk_set(ctx, b, bk0, "act")
                        pk_set(ctx, b, bk1, "act")
            for dc in range(2):
                wev = [None, None]
                @b.blk.gpsimd
                def _(gp, e=e, fh=fh, dc=dc):
                    b.wait(gp, "pool", ctx["wbev"][0], ctx["wbev"][1])
                    wev[0] = b.bump("dG0", gp.dma_start(
                        out=wb[0][:, :].rearrange("p (a c) -> p a c", a=8),
                        in_=ctx["IN"]["l3f_w2"].rearrange("e (a p) f -> e p a f", p=128)[e][:, fh * 16:fh * 16 + 8, dc * 512:(dc + 1) * 512]), 16)
                    wev[1] = b.bump("dG1", gp.dma_start(
                        out=wb[1][:, :].rearrange("p (a c) -> p a c", a=8),
                        in_=ctx["IN"]["l3f_w2"].rearrange("e (a p) f -> e p a f", p=128)[e][:, fh * 16 + 8:fh * 16 + 16, dc * 512:(dc + 1) * 512]), 16)
                for tt in range(NT):
                    bk = BANKS[tt % 4]
                    p = ctx[bk]
                    @b.blk.tensor
                    def _(t, e=e, dc=dc, tt=tt, p=p, bk=bk):
                        b.wait(t, "pe", wev[0], wev[1],
                               b.ev("act") if tt == 0 and dc == 0 else None)
                        pk_wait(b, t, ctx, bk)
                        for ht in range(16):
                            ee = t.matmul(p[:], SA[:, ht * T + tt * 128: ht * T + tt * 128 + 128],
                                          wb[ht // 8][:, (ht % 8) * 512:(ht % 8) * 512 + 512],
                                          start=(ht == 0), stop=(ht == 15))
                        b.bump("pe", ee)
                    ctx["wbev"][0] = b.ev("pe")
                    ctx["wbev"][1] = b.ev("pe")
                    @b.blk.vector
                    def _(v, e=e, fh=fh, dc=dc, tt=tt, p=p):
                        b.wait(v, "dve", b.ev("pe"))
                        ysl = SAf[:, tt * 1024 + dc * 512: tt * 1024 + dc * 512 + 512]
                        ccol = arena[:, 1024 + tt * 8 + e:1024 + tt * 8 + e + 1]
                        if e == 0 and fh == 0:
                            ee = v.tensor_scalar(ysl, p[:], ccol, None, op0=ALU.mult)
                        else:
                            tmp = arena[:, 1600 + (tt % 2) * 512:1600 + (tt % 2) * 512 + 512]
                            b.bump("dve", v.tensor_scalar(tmp, p[:], ccol, None, op0=ALU.mult))
                            b.fence(v)
                            ee = v.tensor_tensor(ysl, ysl, tmp, op=ALU.add)
                        b.bump("dve", ee)
                    pk_set(ctx, b, bk, "dve")
    emit_ytok_residual(ctx, src, dst, scale, extra_z=True)


EMITTERS = {"l2f": emit_l2f, "l1f": emit_l1f, "l0f": emit_l0f, "l3f": emit_l3f,
            "l3a": emit_l3a, "l1a": emit_l1a, "l2a": emit_l2a, "l0a": emit_l0a}



def kernel(**inputs):
    maps = host_prepare(inputs)
    nc = build_program(STEP_SEQ)
    res = run_bass_kernel_spmd(nc, maps, core_ids=list(range(8)))
    out = np.stack([np.asarray(r["yT"], dtype=np.float32).T for r in res.results])
    return np.ascontiguousarray(out)


# version 07
# speedup vs baseline: 1.2322x; 1.0987x over previous
import numpy as np
import ml_dtypes
import contextlib
import concourse.bass as bass
import concourse.mybir as mybir
from concourse.bass_utils import run_bass_kernel_spmd

F32 = mybir.dt.float32
BF16 = mybir.dt.bfloat16
AF = mybir.ActivationFunctionType
ALU = mybir.AluOpType
AX = mybir.AxisListType

T = 2048
D = 1024
NT = T // 128
ND = D // 128
ZSTRIDE = 2080


def _bf(x):
    return np.asarray(x).astype(ml_dtypes.bfloat16)

def _cols(v):
    v = np.asarray(v, dtype=np.float32)
    k = v.size // 128
    return np.ascontiguousarray(v.reshape(k, 128).T)

def _t(w):
    return np.ascontiguousarray(np.asarray(w, dtype=np.float32).T)

def build_constants():
    C = {}
    C["c_ident"] = np.eye(128, dtype=np.float32)
    dh = 64
    inv = 1.0 / (10000.0 ** (np.arange(0, dh, 2, dtype=np.float64) / dh))
    fr = np.arange(T, dtype=np.float64)[:, None] * inv[None, :]
    emb = np.concatenate([fr, fr], axis=1)
    cos1 = np.cos(emb).T.astype(np.float32)
    sin1 = np.sin(emb).T.astype(np.float32)
    C["c_ropeL1"] = np.ascontiguousarray(
        np.stack([np.tile(cos1, (2, 1)), np.tile(sin1, (2, 1))]))
    R = np.zeros((64, 64), dtype=np.float32)
    for j in range(32):
        R[j, j + 32] = -1.0
        R[j + 32, j] = 1.0
    R2 = np.zeros((128, 128), dtype=np.float32)
    R2[:64, :64] = R
    R2[64:, 64:] = R
    C["c_rotH"] = _bf(R2.T)
    def rope_int(base):
        invf = 1.0 / (base ** (np.arange(0, dh, 2, dtype=np.float64) / dh))
        f = np.arange(T, dtype=np.float64)[:, None] * invf[None, :]
        c = np.cos(f).T
        s = np.sin(f).T
        c2 = np.zeros((64, T)); s2 = np.zeros((64, T))
        c2[0::2] = c; c2[1::2] = c
        s2[0::2] = -s; s2[1::2] = s
        return (np.ascontiguousarray(np.tile(c2, (2, 1)).astype(np.float32)),
                np.ascontiguousarray(np.tile(s2, (2, 1)).astype(np.float32)))
    cg, sg = rope_int(1000000.0)
    cl, sl = rope_int(10000.0)
    C["c_ropeG"] = np.stack([cg, sg])
    C["c_ropeL"] = np.stack([cl, sl])
    P = np.zeros((128, 128), dtype=np.float32)
    for i in range(64):
        P[2 * i, 2 * i + 1] = 1.0
        P[2 * i + 1, 2 * i] = 1.0
    C["c_rotP"] = _bf(P)
    pos = np.arange(T, dtype=np.float64)[:, None]
    div = np.exp(np.arange(0, dh, 2, dtype=np.float64) * -(np.log(10000.0) / dh))
    pe = np.zeros((T, dh))
    pe[:, 0::2] = np.sin(pos * div)
    pe[:, 1::2] = np.cos(pos * div)
    C["c_posL3"] = np.ascontiguousarray(np.tile(pe.T.astype(np.float32), (2, 1)))
    NEG = np.float32(-1e30)
    def band_mask(r, lo, hi):
        k = np.arange(128)[:, None]
        q = np.arange(512)[None, :]
        d = q - (r * 128 + k)
        return np.where((d >= lo) & (d <= hi), np.float32(0), NEG).astype(np.float32)
    C["c_maskL1"] = np.stack([band_mask(r, 0, 127) for r in range(-1, 4)])
    C["c_maskL2g"] = np.stack([band_mask(r, 0, 10 ** 9) for r in range(0, 4)])
    C["c_maskL2l"] = np.stack([band_mask(r, 0, 255) for r in range(-2, 4)])
    ind = np.zeros((8, 128, 16), dtype=np.float32)
    for dt in range(8):
        ind[dt, :64, 2 * dt] = 1.0
        ind[dt, 64:, 2 * dt + 1] = 1.0
    C["c_hind"] = ind
    C["c_hindT"] = np.ascontiguousarray(ind.transpose(0, 2, 1))
    C["c_iota"] = np.arange(128, dtype=np.float32).reshape(128, 1)
    CAP = 768
    C["c_eoff"] = np.tile((np.arange(8, dtype=np.float32) * CAP - 1.0)[None, :], (128, 1))
    ind2 = np.zeros((128, 2), dtype=np.float32)
    ind2[:64, 0] = 1.0
    ind2[64:, 1] = 1.0
    C["c_ind2"] = ind2
    C["c_ind2T"] = np.ascontiguousarray(ind2.T)
    return C

_CONSTS = build_constants()

VECCOLS = {}
def _register_veccols():
    off = 0
    for name, n in [
        ("l0a_Dskip", 8), ("l0a_ssmw", 8), ("l0a_ffnw", 8), ("l0a_loglam", 2),
        ("l1a_bo", 8), ("l2f_mur", 8), ("l2f_muk", 8), ("l2a_n1", 8),
        ("l2a_n2", 8), ("k_zero", 1), ("k_one", 1), ("k_shift", 1),
        ("k_eps5", 1), ("k_eps6", 1), ("k_neg1e30", 1), ("k_shift8", 1),
        ("l3a_bout", 8),
    ]:
        VECCOLS[name] = (off, n)
        off += n
    return off
NVC = _register_veccols()

def host_prepare(inputs):
    x = np.asarray(inputs["L0_attn_x"], dtype=np.float32)
    shared = {}
    shared["l0a_BwT"] = _t(inputs["L0_attn_B_w"])
    shared["l0a_CwT"] = _t(inputs["L0_attn_C_w"])
    shared["l0a_w1T"] = _t(inputs["L0_attn_w1"])
    w2t = _t(inputs["L0_attn_w2"])
    w2tp = np.zeros((2816, 1024), dtype=np.float32)
    w2tp[:2736] = w2t
    shared["l0a_w2T"] = w2tp
    shared["l0a_w3T"] = _t(inputs["L0_attn_w3"])
    shared["l0f_w12T"] = np.ascontiguousarray(
        np.asarray(inputs["L0_ffn_w12"], dtype=np.float32).transpose(0, 2, 1))
    shared["l0f_w3T"] = np.ascontiguousarray(
        np.asarray(inputs["L0_ffn_w3"], dtype=np.float32).transpose(0, 2, 1))
    shared["l0f_gwT"] = _t(inputs["L0_ffn_gate_w"])
    shared["l1a_WqT"] = _t(inputs["L1_attn_Wq"])
    shared["l1a_WkT"] = _t(inputs["L1_attn_Wk"])
    shared["l1a_WvT"] = _t(inputs["L1_attn_Wv"])
    shared["l1a_WoT"] = _t(inputs["L1_attn_Wo"])
    shared["l1f_Wg"] = np.ascontiguousarray(np.asarray(inputs["L1_ffn_Wg"], dtype=np.float32))
    shared["l1f_We"] = np.ascontiguousarray(np.asarray(inputs["L1_ffn_We"], dtype=np.float32))
    for k_src, k_dst in [("L2_attn_g_wq", "l2a_gwq"), ("L2_attn_g_kv_down", "l2a_gkvd"),
                         ("L2_attn_g_k_up", "l2a_gkup"), ("L2_attn_g_v_up", "l2a_gvup"),
                         ("L2_attn_g_wo", "l2a_gwo"), ("L2_attn_l_wq", "l2a_lwq"),
                         ("L2_attn_l_wk", "l2a_lwk"), ("L2_attn_l_wv", "l2a_lwv"),
                         ("L2_attn_l_wo", "l2a_lwo")]:
        shared[k_dst] = np.ascontiguousarray(np.asarray(inputs[k_src], dtype=np.float32))
    shared["l2f_WrT"] = _t(inputs["L2_ffn_W_r"])
    shared["l2f_WkT"] = _t(inputs["L2_ffn_W_k"])
    shared["l2f_WvT"] = _t(inputs["L2_ffn_W_v"])
    shared["l3a_Wqkv"] = np.ascontiguousarray(np.asarray(inputs["L3_attn_W_qkv"], dtype=np.float32))
    shared["l3a_Wout"] = np.ascontiguousarray(np.asarray(inputs["L3_attn_W_out"], dtype=np.float32))
    shared["l3f_w1"] = np.ascontiguousarray(np.asarray(inputs["L3_ffn_w1"], dtype=np.float32))
    shared["l3f_w2"] = np.ascontiguousarray(np.asarray(inputs["L3_ffn_w2"], dtype=np.float32))
    shared["l3f_rwT"] = _t(inputs["L3_ffn_router_w"])
    vcv = np.zeros((128, NVC), dtype=np.float32)
    def put(name, v):
        o, n = VECCOLS[name]
        vcv[:, o:o + n] = _cols(v)
    put("l0a_Dskip", inputs["L0_attn_D_skip"])
    put("l0a_ssmw", inputs["L0_attn_ssm_norm_w"])
    put("l0a_ffnw", inputs["L0_attn_ffn_norm_w"])
    put("l0a_loglam", inputs["L0_attn_log_lambda"])
    put("l1a_bo", inputs["L1_attn_bo"])
    put("l2f_mur", inputs["L2_ffn_mu_r"])
    put("l2f_muk", inputs["L2_ffn_mu_k"])
    put("l2a_n1", inputs["L2_attn_norm1_w"])
    put("l2a_n2", inputs["L2_attn_norm2_w"])
    vcv[:, VECCOLS["k_zero"][0]] = 0.0
    vcv[:, VECCOLS["k_one"][0]] = 1.0
    vcv[:, VECCOLS["k_shift"][0]] = -30.0
    vcv[:, VECCOLS["k_eps5"][0]] = 1e-5
    vcv[:, VECCOLS["k_eps6"][0]] = 1e-6
    vcv[:, VECCOLS["k_neg1e30"][0]] = -1e30
    vcv[:, VECCOLS["k_shift8"][0]] = -8.0
    put("l3a_bout", inputs["L3_attn_b_out"])
    shared["veccols"] = vcv
    vr = np.zeros((13, 1024), dtype=np.float32)
    vr[0] = np.asarray(inputs["L1_ffn_bg"], np.float32)
    vr[1:9] = np.asarray(inputs["L1_ffn_be"], np.float32)
    vr[9] = np.asarray(inputs["L3_attn_b_out"], np.float32)
    vr[10:13] = np.asarray(inputs["L3_attn_b_qkv"], np.float32).reshape(3, 1024)
    shared["vecrows"] = vr
    shared.update(_CONSTS)
    maps = []
    for c in range(8):
        m = dict(shared)
        m["xT"] = np.ascontiguousarray(x[c].T)
        maps.append(m)
    return maps



class Sem:
    __slots__ = ("h", "v", "hwm")
    def __init__(self, h):
        self.h = h
        self.v = 0
        self.hwm = {}

class B:
    def __init__(self, nc, blk, st):
        self.nc = nc
        self.blk = blk
        self.st = st
        self.sems = {}
        for name in ["dS", "dG", "dG0", "dG1", "pe", "act", "dve", "pool"]:
            self.sems[name] = Sem(st.enter_context(nc.semaphore(name)))

    def wait(self, eng, key, *evs):
        for ev in evs:
            if ev is None:
                continue
            s, v = ev
            if s.hwm.get(key, 0) < v:
                eng.wait_ge(s.h, v)
                s.hwm[key] = v

    def bump(self, sem_name, instr, n=1):
        s = self.sems[sem_name]
        instr.then_inc(s.h, n)
        s.v += n
        return (s, s.v)

    def ev(self, sem_name):
        s = self.sems[sem_name]
        return (s, s.v)

    def fence(self, eng, sem_name="dve"):
        s = self.sems[sem_name]
        eng.wait_ge(s.h, s.v)

    def all_compute_evs(self):
        return [self.ev(k) for k in ["pe", "act", "dve", "pool"]]

    def all_evs(self):
        return [self.ev(k) for k in ["pe", "act", "dve", "pool", "dS", "dG"]]


def sbuf(st, nc, name, shape, dtype=F32):
    return st.enter_context(nc.sbuf_tensor(name, shape, dtype))



STEP_SEQ = ["l0f", "l0a", "l0f", "l1f", "l1a", "l1f",
            "l2f", "l2a", "l2f", "l3f", "l3a", "l3f"]

def build_program(steps):
    nc = bass.Bass("TRN2", target_bir_lowering=False, debug=False)
    IN = {}
    def di(name, shape, dtype=F32):
        IN[name] = nc.dram_tensor(name, list(shape), dtype, kind="ExternalInput").ap()
        return IN[name]
    di("xT", (D, T))
    di("veccols", (128, NVC)); di("vecrows", (13, 1024))
    di("c_ident", (128, 128)); di("c_ropeL1", (2, 128, T)); di("c_rotH", (128, 128), BF16)
    di("c_ropeG", (2, 128, T)); di("c_ropeL", (2, 128, T)); di("c_rotP", (128, 128), BF16)
    di("c_posL3", (128, T)); di("c_maskL1", (5, 128, 512)); di("c_maskL2g", (4, 128, 512))
    di("c_maskL2l", (6, 128, 512)); di("c_hind", (8, 128, 16)); di("c_hindT", (8, 16, 128))
    di("c_ind2", (128, 2)); di("c_ind2T", (2, 128))
    di("c_iota", (128, 1)); di("c_eoff", (128, 8))
    di("l0a_BwT", (D, 256)); di("l0a_CwT", (256, D)); di("l0a_w1T", (D, 2736))
    di("l0a_w2T", (2816, D)); di("l0a_w3T", (D, 2736))
    di("l0f_w12T", (8, D, 4096)); di("l0f_w3T", (8, 2048, D)); di("l0f_gwT", (D, 8))
    for n in ["l1a_WqT", "l1a_WkT", "l1a_WvT", "l1a_WoT", "l1f_Wg",
              "l2a_gwq", "l2a_gwo", "l2a_lwq", "l2a_lwo", "l2f_WrT",
              "l2f_WkT", "l2f_WvT", "l3a_Wout"]:
        di(n, (D, D))
    di("l1f_We", (8, D, D))
    di("l2a_gkvd", (D, 256)); di("l2a_gkup", (256, 256)); di("l2a_gvup", (256, 256))
    di("l2a_lwk", (D, 256)); di("l2a_lwv", (D, 256))
    di("l3a_Wqkv", (D, 3072)); di("l3f_w1", (8, D, 4096)); di("l3f_w2", (8, 4096, D))
    di("l3f_rwT", (D, 8))
    yT = nc.dram_tensor("yT", [D, T], F32, kind="ExternalOutput").ap()
    hx = [nc.dram_tensor(f"hx{i}", [D, T], F32, kind="Internal").ap() for i in range(2)]
    hxi = nc.dram_tensor("hxi", [D, T], F32, kind="Internal").ap()
    ctx_extra = {}
    ctx_extra["l3tab"] = nc.dram_tensor("l3tab", [6144, 2], F32, kind="Internal").ap()
    ctx_extra["zrows"] = nc.dram_tensor("zrows", [T, D], BF16, kind="Internal").ap()
    ctx_extra["ctab"] = nc.dram_tensor("ctab", [T, 8], F32, kind="Internal").ap()
    ctx_extra["yhbm"] = nc.dram_tensor("yhbm", [T, D], F32, kind="Internal").ap()

    st = contextlib.ExitStack()
    with st:
        ctx = dict(nc=nc, IN=IN, yT=yT, hx=hx, hxi=hxi)
        ctx.update(ctx_extra)
        ctx["SA"] = sbuf(st, nc, "SA", [128, NT * T], BF16)
        ctx["SAf"] = sbuf(st, nc, "SAf", [128, ND * T], F32)
        ctx["zT"] = sbuf(st, nc, "zTb", [128, ND * ZSTRIDE], BF16)
        ctx["wb"] = [sbuf(st, nc, f"wb{i}", [128, 4096], BF16) for i in range(2)]
        ctx["arena"] = sbuf(st, nc, "AR", [128, 6144], F32)
        ctx["vcol"] = sbuf(st, nc, "vcolb", [128, NVC], F32)
        ctx["ident"] = sbuf(st, nc, "identb_", [128, 128], F32)
        ctx["identb"] = sbuf(st, nc, "identbb", [128, 128], BF16)
        ctx["onesf"] = sbuf(st, nc, "onesf", [1, 512], F32)
        ctx["onesb"] = sbuf(st, nc, "onesb", [1, 512], BF16)
        ctx["onescf"] = sbuf(st, nc, "onescf", [128, 1], F32)
        ctx["onescb"] = sbuf(st, nc, "onescb", [128, 1], BF16)
        ctx["touch"] = sbuf(st, nc, "tchb", [1, 8], F32)
        ctx["su32"] = st.enter_context(nc.sbuf_tensor("su32", [128, 16], mybir.dt.uint32))
        ctx["SAb"] = ctx["SAf"].ap().bitcast(BF16)
        for p in ["pA", "pB", "pC", "pD", "pE", "pF"]:
            ctx[p] = st.enter_context(nc.psum_tensor(p, [128, 512], F32))
        ctx["pStat"] = st.enter_context(nc.psum_tensor("pStat", [64, 512], F32))
        ctx["pTb"] = st.enter_context(nc.psum_tensor("pTb", [128, 1024], BF16))
        blk = st.enter_context(nc.Block())
        b = B(nc, blk, st)
        ctx["b"] = b
        ctx["wbev"] = [None, None]
        ctx["pkev"] = {}
        emit_prologue(ctx)
        n = len(steps)
        for i, stp in enumerate(steps):
            name, scale = stp if isinstance(stp, tuple) else (stp, 0.5 if stp.endswith("f") else 1.0)
            src = IN["xT"] if i == 0 else hx[(i - 1) % 2]
            dst = yT if i == n - 1 else hx[i % 2]
            EMITTERS[name](ctx, src, dst, scale)
        emit_epilogue(ctx)
    return nc


def emit_prologue(ctx):
    b = ctx["b"]
    @b.blk.sync
    def _(sy):
        b.bump("dS", sy.dma_start(out=ctx["vcol"][:], in_=ctx["IN"]["veccols"]), 16)
        b.bump("dS", sy.dma_start(out=ctx["ident"][:], in_=ctx["IN"]["c_ident"]), 16)
    @b.blk.gpsimd
    def _(gp):
        ctx["rb6143"] = gp.to_reg(6143)
    @b.blk.vector
    def _(v):
        b.wait(v, "dve", b.ev("dS"))
        b.bump("dve", v.tensor_copy(ctx["identb"][:], ctx["ident"][:]))
        b.bump("dve", v.memset(ctx["onesf"][:], 1.0))
        b.bump("dve", v.memset(ctx["onesb"][:], 1.0))
        b.bump("dve", v.memset(ctx["onescf"][:], 1.0))
        b.bump("dve", v.memset(ctx["onescb"][:], 1.0))
        b.fence(v)
        b.bump("dve", v.tensor_copy(ctx["touch"][:], ctx["identb"][0:1, 0:8]))
    ctx["prologue_ev"] = b.ev("dve")


def emit_epilogue(ctx):
    b = ctx["b"]
    @b.blk.scalar
    def _(s):
        b.wait(s, "act", *b.all_evs())


def pk_wait(b, t, ctx, bank):
    evs = ctx["pkev"].get(bank)
    if evs:
        b.wait(t, "pe", *evs)

def pk_set(ctx, b, bank, *sems):
    ctx["pkev"][bank] = tuple(b.ev(s) for s in sems)


def vc(ctx, name, j=0):
    o, _n = VECCOLS[name]
    return ctx["vcol"][:, o + j:o + j + 1]


def zslc(dt, tc=None, w=512):
    if tc is None:
        return slice(dt * ZSTRIDE + 1, dt * ZSTRIDE + 1 + T)
    return slice(dt * ZSTRIDE + 1 + tc * w, dt * ZSTRIDE + 1 + tc * w + w)



def emit_ln(ctx, src_hbm, rms=False, w_col_name=None, eps_name="k_eps5",
            barrier=True, want_zlo=False, resident=False):
    b = ctx["b"]
    SAf, arena = ctx["SAf"], ctx["arena"]
    SA = ctx["SA"]
    pStat, zT = ctx["pStat"], ctx["zT"]
    m_row = lambda c0, c1: arena[0:1, 4096 + c0:4096 + c1]
    v_row = lambda c0, c1: arena[0:1, c0:c1]
    s_row = lambda c0, c1: arena[0:1, 2048 + c0:2048 + c1]
    xevs = []
    if not resident:
        @b.blk.sync
        def _(sy):
            if barrier:
                b.wait(sy, "dS", *b.all_compute_evs())
            b.wait(sy, "dS", b.ev("dS"))
            for dt in range(ND):
                xevs.append(b.bump("dS", sy.dma_start(
                    out=SAf[:, dt * T:(dt + 1) * T],
                    in_=src_hbm[dt * 128:(dt + 1) * 128, :]), 16))
    else:
        xevs.append(b.ev("dve"))
    for tc in range(4):
        sq_evs = []
        @b.blk.scalar
        def _(s, tc=tc):
            b.wait(s, "act", xevs[-1], b.ev("pe"))
            for dt in range(ND):
                sq_evs.append(b.bump("act", s.activation(
                    SA[:, dt * 512:dt * 512 + 512],
                    SAf[:, dt * T + tc * 512: dt * T + tc * 512 + 512],
                    AF.Square, bias=vc(ctx, "k_zero"), scale=1.0)))
        @b.blk.tensor
        def _(t, tc=tc):
            b.wait(t, "pe", xevs[-1], ctx["prologue_ev"], b.ev("dve"))
            for dt in range(ND):
                t.matmul(pStat[0:1, :], ctx["onescf"][:, :],
                         SAf[:, dt * T + tc * 512: dt * T + tc * 512 + 512],
                         start=(dt == 0), stop=(dt == ND - 1))
                b.wait(t, "pe", sq_evs[dt])
                ee = t.matmul(pStat[32:33, :], ctx["onescb"][:, :],
                              SA[:, dt * 512:dt * 512 + 512],
                              start=(dt == 0), stop=(dt == ND - 1))
            b.bump("pe", ee)
        @b.blk.vector
        def _(v, tc=tc):
            b.wait(v, "dve", b.ev("pe"))
            v.tensor_copy(m_row(tc * 512, (tc + 1) * 512), pStat[0:1, :])
            b.bump("dve", v.tensor_copy(v_row(tc * 512, (tc + 1) * 512), pStat[32:33, :]))
    @b.blk.vector
    def _(v):
        b.wait(v, "dve", b.ev("dve"))
        if not rms:
            b.bump("dve", v.tensor_scalar(m_row(0, T), m_row(0, T), 1.0 / D, None, op0=ALU.mult))
            b.fence(v)
            b.bump("dve", v.tensor_tensor(s_row(0, T), m_row(0, T), m_row(0, T), op=ALU.mult))
            b.bump("dve", v.tensor_scalar(v_row(0, T), v_row(0, T), 1.0 / D, None, op0=ALU.mult))
            b.fence(v)
            b.bump("dve", v.tensor_tensor(v_row(0, T), v_row(0, T), s_row(0, T), op=ALU.subtract))
        else:
            b.bump("dve", v.memset(m_row(0, T), 0.0))
            b.bump("dve", v.tensor_scalar(v_row(0, T), v_row(0, T), 1.0 / D, None, op0=ALU.mult))
        b.fence(v)
        b.bump("dve", v.tensor_copy(ctx["touch"][:], v_row(0, 8)))
    @b.blk.scalar
    def _(s):
        b.wait(s, "act", b.ev("dve"))
        o_eps = VECCOLS[eps_name][0]
        b.bump("act", s.activation(v_row(0, T), v_row(0, T), AF.Sqrt,
                                   bias=ctx["vcol"][0:1, o_eps:o_eps + 1], scale=1.0))
    @b.blk.vector
    def _(v):
        b.wait(v, "dve", b.ev("act"))
        b.bump("dve", v.reciprocal(v_row(0, T), v_row(0, T)))
        b.fence(v)
        b.bump("dve", v.tensor_copy(ctx["touch"][:], v_row(0, 8)))
    rowev = b.ev("dve")
    for tc in range(4):
        @b.blk.tensor
        def _(t, tc=tc):
            b.wait(t, "pe", rowev, b.ev("dve"))
            t.matmul(ctx["pD"][:, :], ctx["onesf"][:, 0:128], m_row(tc * 512, (tc + 1) * 512),
                     start=True, stop=True)
            b.bump("pe", t.matmul(ctx["pE"][:, :], ctx["onesf"][:, 0:128],
                                  v_row(tc * 512, (tc + 1) * 512), start=True, stop=True))
        @b.blk.vector
        def _(v, tc=tc):
            b.wait(v, "dve", b.ev("pe"))
            v.tensor_copy(arena[:, tc * 512:(tc + 1) * 512], ctx["pD"][:])
            b.bump("dve", v.tensor_copy(arena[:, 2048 + tc * 512:2048 + (tc + 1) * 512],
                                        ctx["pE"][:]))
    @b.blk.vector
    def _(v):
        b.wait(v, "dve", b.ev("dve"))
        for dt in range(ND):
            if not rms:
                b.bump("dve", v.tensor_tensor(SAf[:, dt * T:(dt + 1) * T],
                                              SAf[:, dt * T:(dt + 1) * T],
                                              arena[:, 0:T], op=ALU.subtract))
                b.fence(v)
                if want_zlo:
                    b.bump("dve", v.tensor_tensor(SAf[:, dt * T:(dt + 1) * T],
                                                  SAf[:, dt * T:(dt + 1) * T],
                                                  arena[:, 2048:2048 + T], op=ALU.mult))
                    b.fence(v)
                    b.bump("dve", v.tensor_copy(zT[:, zslc(dt)], SAf[:, dt * T:(dt + 1) * T]))
                    b.fence(v)
                    b.bump("dve", v.tensor_tensor(SA[:, dt * T:(dt + 1) * T], SAf[:, dt * T:(dt + 1) * T],
                                    zT[:, zslc(dt)], op=ALU.subtract))
                else:
                    b.bump("dve", v.tensor_tensor(zT[:, zslc(dt)], SAf[:, dt * T:(dt + 1) * T],
                                    arena[:, 2048:2048 + T], op=ALU.mult))
            else:
                b.bump("dve", v.scalar_tensor_tensor(zT[:, zslc(dt)], SAf[:, dt * T:(dt + 1) * T],
                                       vc(ctx, w_col_name, dt), arena[:, 2048:2048 + T],
                                       op0=ALU.mult, op1=ALU.mult))
        b.fence(v)
        b.bump("dve", v.tensor_copy(ctx["touch"][:], zT[0:1, 1:9]))
    return b.ev("dve")



def emit_l2f(ctx, src, dst, scale):
    b = ctx["b"]
    SA, SAf, arena, zT = ctx["SA"], ctx["SAf"], ctx["arena"], ctx["zT"]
    wb, pA, pB = ctx["wb"], ctx["pA"], ctx["pB"]
    zev = emit_ln(ctx, src, rms=False)
    @b.blk.vector
    def _(v):
        b.wait(v, "dve", zev)
        for dt in range(ND):
            b.bump("dve", v.memset(zT[:, dt * ZSTRIDE:dt * ZSTRIDE + 1], 0.0))
        b.fence(v)
        for dt in range(ND):
            zc = zT[:, dt * ZSTRIDE + 1: dt * ZSTRIDE + 1 + T]
            zp = zT[:, dt * ZSTRIDE: dt * ZSTRIDE + T]
            b.fence(v)
            b.bump("dve", v.tensor_tensor(arena[:, 0:T], zc, zp, op=ALU.subtract))
            b.fence(v)
            v.scalar_tensor_tensor(SA[:, dt * T:(dt + 1) * T], arena[:, 0:T],
                                   vc(ctx, "l2f_mur", dt), zp, op0=ALU.mult, op1=ALU.add)
            b.bump("dve", v.scalar_tensor_tensor(SA[:, (8 + dt) * T:(9 + dt) * T], arena[:, 0:T],
                                   vc(ctx, "l2f_muk", dt), zp, op0=ALU.mult, op1=ALU.add))
        b.bump("dve", v.tensor_copy(ctx["touch"][:], SA[0:1, 0:8]))
    mixev = b.ev("dve")
    for wname, src_off in [("l2f_WrT", 0), ("l2f_WkT", 8 * T)]:
        for do in range(ND):
            wev = [None]
            @b.blk.gpsimd
            def _(gp, wname=wname, do=do):
                b.wait(gp, "pool", ctx["wbev"][do % 2])
                wev[0] = b.bump("dG0" if do % 2 == 0 else "dG1", gp.dma_start(
                    out=wb[do % 2][:, 0:1024].rearrange("p (a c) -> p a c", a=8),
                    in_=ctx["IN"][wname].rearrange("(a p) f -> p a f", p=128)[:, :, do * 128:(do + 1) * 128]), 16)
            for tc in range(4):
                p = pA if tc % 2 == 0 else pB
                @b.blk.tensor
                def _(t, wname=wname, src_off=src_off, do=do, tc=tc, p=p):
                    b.wait(t, "pe", wev[0], mixev, b.ev("act"), b.ev("dve"))
                    for dt in range(ND):
                        ee = t.matmul(p[:, :], wb[do % 2][:, dt * 128:(dt + 1) * 128],
                                      SA[:, src_off + dt * T + tc * 512: src_off + dt * T + tc * 512 + 512],
                                      start=(dt == 0), stop=(dt == ND - 1))
                    b.bump("pe", ee)
                ctx["wbev"][do % 2] = b.ev("pe")
                @b.blk.scalar
                def _(s, wname=wname, do=do, tc=tc, p=p):
                    b.wait(s, "act", b.ev("pe"))
                    if wname == "l2f_WrT":
                        ee = s.activation(SAf[:, do * T + tc * 512: do * T + tc * 512 + 512],
                                          p[:], AF.Sigmoid, bias=vc(ctx, "k_zero"), scale=1.0)
                    else:
                        ee = s.activation(arena[:, (tc % 2) * 512:(tc % 2) * 512 + 512],
                                          p[:], AF.Relu, bias=vc(ctx, "k_zero"), scale=1.0)
                    b.bump("act", ee)
                if wname == "l2f_WkT":
                    @b.blk.vector
                    def _(v, do=do, tc=tc):
                        b.wait(v, "dve", b.ev("act"))
                        sq = arena[:, (tc % 2) * 512:(tc % 2) * 512 + 512]
                        b.bump("dve", v.tensor_tensor(zT[:, zslc(do, tc)], sq, sq, op=ALU.mult))
    k2ev = b.ev("dve")
    rev = b.ev("act")
    for do in range(ND):
        wev = [None]
        xev = [None]
        guard = b.ev("pe")
        @b.blk.gpsimd
        def _(gp, do=do, guard=guard):
            b.wait(gp, "pool", guard)
            wev[0] = b.bump("dG0" if do % 2 == 0 else "dG1", gp.dma_start(
                out=wb[do % 2][:, 0:1024].rearrange("p (a c) -> p a c", a=8),
                in_=ctx["IN"]["l2f_WvT"].rearrange("(a p) f -> p a f", p=128)[:, :, do * 128:(do + 1) * 128]), 16)
        dguard = b.ev("dve")
        @b.blk.sync
        def _(sy, do=do, dguard=dguard):
            b.wait(sy, "dS", dguard, b.ev("dS"))
            xev[0] = b.bump("dS", sy.dma_start(
                out=arena[:, 2048 + (do % 2) * 2048: 2048 + (do % 2) * 2048 + T],
                in_=src[do * 128:(do + 1) * 128, :]), 16)
        for tc in range(4):
            p = pA if tc % 2 == 0 else pB
            @b.blk.tensor
            def _(t, do=do, tc=tc, p=p):
                b.wait(t, "pe", wev[0], k2ev, b.ev("dve"))
                for dt in range(ND):
                    ee = t.matmul(p[:, :], wb[do % 2][:, dt * 128:(dt + 1) * 128],
                                  zT[:, zslc(dt, tc)], start=(dt == 0), stop=(dt == ND - 1))
                b.bump("pe", ee)
            @b.blk.vector
            def _(v, do=do, tc=tc, p=p):
                b.wait(v, "dve", b.ev("pe"), xev[0], rev)
                b.bump("dve", v.tensor_tensor(arena[:, 1024:1536], p[:],
                                SAf[:, do * T + tc * 512: do * T + tc * 512 + 512], op=ALU.mult))
                b.fence(v)
                xs = arena[:, 2048 + (do % 2) * 2048 + tc * 512: 2048 + (do % 2) * 2048 + tc * 512 + 512]
                b.bump("dve", v.scalar_tensor_tensor(xs, arena[:, 1024:1536], scale, xs,
                                                     op0=ALU.mult, op1=ALU.add))
                b.fence(v)
                b.bump("dve", v.tensor_copy(ctx["touch"][:], xs[0:1, 0:8]))
        @b.blk.sync
        def _(sy, do=do):
            b.wait(sy, "dS", b.ev("dve"))
            b.bump("dS", sy.dma_start(out=dst[do * 128:(do + 1) * 128, :],
                                      in_=arena[:, 2048 + (do % 2) * 2048: 2048 + (do % 2) * 2048 + T]), 16)



def emit_l1f(ctx, src, dst, scale):
    b = ctx["b"]
    SA, SAf, arena, zT = ctx["SA"], ctx["SAf"], ctx["arena"], ctx["zT"]
    wb, pA, pB, pC = ctx["wb"], ctx["pA"], ctx["pB"], ctx["pC"]
    zev = emit_ln(ctx, src, rms=False, want_zlo=True)
    @b.blk.gpsimd
    def _(gp):
        b.wait(gp, "pool", zev)
        b.bump("dG", gp.dma_start(out=SA[0:1, 20480:21504],
                                  in_=ctx["IN"]["vecrows"][0:1, :]), 16)
        for e in range(8):
            b.bump("dG", gp.dma_start(out=SA[0:1, 22528 + e * 1024: 22528 + (e + 1) * 1024],
                                      in_=ctx["IN"]["vecrows"][1 + e:2 + e, :]), 16)
    @b.blk.sync
    def _(sy):
        b.wait(sy, "dS", zev)
        b.bump("dS", sy.dma_start(out=arena[0:1, 0:1024],
                                  in_=ctx["IN"]["vecrows"][0:1, :]), 16)
    @b.blk.vector
    def _(v):
        b.wait(v, "dve", b.ev("dS"), b.ev("dG"))
        b.bump("dve", v.tensor_tensor(SA[0:1, 21504:22528], arena[0:1, 0:1024],
                                      SA[0:1, 20480:21504], op=ALU.subtract))
        b.fence(v)
        b.bump("dve", v.tensor_copy(ctx["touch"][:], SA[0:1, 21504:21512]))
    rowev = b.ev("dve")
    for fc in range(2):
        hlev = [None]
        for dt in range(ND):
            wev = [None]
            dguard = b.ev("dve")
            @b.blk.sync
            def _(sy, fc=fc, dt=dt, dguard=dguard):
                b.wait(sy, "dS", dguard, b.ev("pe"))
                wev[0] = b.bump("dS", sy.dma_start(
                    out=arena[:, 1536 + (dt % 2) * 512:1536 + (dt % 2) * 512 + 512],
                    in_=ctx["IN"]["l1f_Wg"][dt * 128:(dt + 1) * 128, fc * 512:(fc + 1) * 512]), 16)
            @b.blk.vector
            def _(v, fc=fc, dt=dt):
                b.wait(v, "dve", wev[0], ctx["wbev"][0], ctx["wbev"][1])
                wsl = arena[:, 1536 + (dt % 2) * 512:1536 + (dt % 2) * 512 + 512]
                b.bump("dve", v.tensor_copy(wb[0][:, dt * 512:(dt + 1) * 512], wsl))
                b.fence(v)
                b.bump("dve", v.tensor_tensor(wb[1][:, dt * 512:(dt + 1) * 512], wsl,
                                               wb[0][:, dt * 512:(dt + 1) * 512], op=ALU.subtract))
                if dt == ND - 1:
                    b.fence(v)
                    hlev[0] = b.bump("dve", v.tensor_copy(ctx["touch"][:], wb[1][0:1, 0:8]))
        for tt in range(NT):
            bk = ["pA", "pB", "pC", "pD"][tt % 4]
            @b.blk.tensor
            def _(t, fc=fc, tt=tt, bk=bk):
                b.wait(t, "pe", hlev[0])
                pk_wait(b, t, ctx, bk)
                p = ctx[bk]
                for dt in range(ND):
                    zhi = zT[:, dt * ZSTRIDE + 1 + tt * 128: dt * ZSTRIDE + 1 + tt * 128 + 128]
                    zlo = SA[:, dt * T + tt * 128: dt * T + tt * 128 + 128]
                    t.matmul(p[:], zhi, wb[0][:, dt * 512:(dt + 1) * 512],
                             start=(dt == 0), stop=False)
                    t.matmul(p[:], zhi, wb[1][:, dt * 512:(dt + 1) * 512],
                             start=False, stop=False)
                    t.matmul(p[:], zlo, wb[0][:, dt * 512:(dt + 1) * 512],
                             start=False, stop=False)
                t.matmul(p[:], ctx["onesb"][:, 0:128], SA[0:1, 20480 + fc * 512:20480 + fc * 512 + 512],
                         start=False, stop=False)
                b.bump("pe", t.matmul(p[:], ctx["onesb"][:, 0:128],
                                      SA[0:1, 21504 + fc * 512:21504 + fc * 512 + 512],
                                      start=False, stop=True))
            @b.blk.vector
            def _(v, fc=fc, tt=tt, bk=bk):
                b.wait(v, "dve", b.ev("pe"))
                p = ctx[bk]
                b.bump("dve", v.tensor_copy(SAf[:, tt * 1024 + fc * 512: tt * 1024 + fc * 512 + 512], p[:]))
            pk_set(ctx, b, bk, "dve")
            ctx["wbev"][0] = b.ev("pe")
            ctx["wbev"][1] = b.ev("pe")
    AXX = AX.X
    for tt in range(NT):
        @b.blk.vector
        def _(v, tt=tt):
            b.wait(v, "dve", b.ev("dve"))
            sc = SAf[:, tt * 1024: (tt + 1) * 1024]
            scr = sc.rearrange("p (a b) -> p a b", b=8)
            for e in range(8):
                b.bump("dve", v.reduce_max(arena[:, 1280 + e:1281 + e], scr[:, :, e], axis=AXX))
            b.fence(v)
            b.bump("dve", v.reduce_max(arena[:, 1288:1289], arena[:, 1280:1288], axis=AXX))
            b.fence(v)
            b.bump("dve", v.tensor_scalar(arena[:, 1289:1297], arena[:, 1280:1288], arena[:, 1288:1289],
                                          None, op0=ALU.is_equal))
            ee = v.tensor_scalar(arena[:, 1297:1298], arena[:, 1288:1289], -1.0, None, op0=ALU.mult)
            b.bump("dve", ee)
        @b.blk.scalar
        def _(s, tt=tt):
            b.wait(s, "act", b.ev("dve"))
            b.bump("act", s.activation(SA[:, 16384:17408], SAf[:, tt * 1024:(tt + 1) * 1024],
                                       AF.Exp, bias=arena[:, 1297:1298], scale=1.0,
                                       accum_out=arena[:, 1298:1299]))
        @b.blk.vector
        def _(v, tt=tt):
            b.wait(v, "dve", b.ev("act"))
            b.bump("dve", v.reciprocal(arena[:, 1299:1300], arena[:, 1298:1299]))
            b.fence(v)
            ee = v.tensor_scalar(arena[:, 1024 + tt * 8:1024 + tt * 8 + 8],
                                 arena[:, 1289:1297], arena[:, 1299:1300], None, op0=ALU.mult)
            b.bump("dve", ee)
    gatev = b.ev("dve")
    BANKS = ["pA", "pB", "pC", "pD"]
    for e in range(8):
        for fc in range(2):
            ci = e * 2 + fc
            wev = [None]
            @b.blk.gpsimd
            def _(gp, e=e, fc=fc, ci=ci):
                b.wait(gp, "pool", ctx["wbev"][ci % 2])
                wev[0] = b.bump("dG0" if ci % 2 == 0 else "dG1", gp.dma_start(
                    out=wb[ci % 2][:, :].rearrange("p (a c) -> p a c", a=8),
                    in_=ctx["IN"]["l1f_We"].rearrange("e (a p) f -> e p a f", p=128)[e][:, :, fc * 512:(fc + 1) * 512]), 16)
            for tt in range(NT):
                bk = BANKS[tt % 4]
                p = ctx[bk]
                @b.blk.tensor
                def _(t, e=e, ci=ci, tt=tt, p=p, bk=bk):
                    b.wait(t, "pe", wev[0], gatev)
                    pk_wait(b, t, ctx, bk)
                    for dt in range(ND):
                        zhi = zT[:, dt * ZSTRIDE + 1 + tt * 128: dt * ZSTRIDE + 1 + tt * 128 + 128]
                        t.matmul(p[:], zhi, wb[ci % 2][:, dt * 512:(dt + 1) * 512],
                                 start=(dt == 0), stop=False)
                    b.bump("pe", t.matmul(p[:], ctx["onesb"][:, 0:128],
                                          SA[0:1, 22528 + e * 1024 + fc * 512: 22528 + e * 1024 + fc * 512 + 512],
                                          start=False, stop=True))
                ctx["wbev"][ci % 2] = b.ev("pe")
                @b.blk.vector
                def _(v, e=e, fc=fc, tt=tt, p=p):
                    b.wait(v, "dve", b.ev("pe"))
                    ysl = SAf[:, tt * 1024 + fc * 512: tt * 1024 + fc * 512 + 512]
                    ccol = arena[:, 1024 + tt * 8 + e:1024 + tt * 8 + e + 1]
                    if e == 0:
                        ee = v.tensor_scalar(ysl, p[:], 0.0, ccol, op0=ALU.max, op1=ALU.mult)
                    else:
                        tmp = SA[:, 16384 + (tt % 2) * 512:16384 + (tt % 2) * 512 + 512]
                        b.bump("dve", v.tensor_scalar(tmp, p[:], 0.0, ccol,
                                                      op0=ALU.max, op1=ALU.mult))
                        b.fence(v)
                        ee = v.tensor_tensor(ysl, ysl, tmp, op=ALU.add)
                    b.bump("dve", ee)
                pk_set(ctx, b, bk, "dve")
    emit_ytok_residual(ctx, src, dst, scale)


def emit_ytok_residual(ctx, src, dst, scale, extra_z=False):
    b = ctx["b"]
    SAf, arena = ctx["SAf"], ctx["arena"]
    pC = ctx["pC"]
    for dt in range(ND):
        xev = [None]
        dguard = b.ev("dve")
        @b.blk.sync
        def _(sy, dt=dt, dguard=dguard):
            b.wait(sy, "dS", dguard, b.ev("dS"))
            xev[0] = b.bump("dS", sy.dma_start(
                out=arena[:, (dt % 2) * 2048:(dt % 2) * 2048 + T],
                in_=src[dt * 128:(dt + 1) * 128, :]), 16)
        for tc in range(4):
            @b.blk.tensor
            def _(t, dt=dt, tc=tc):
                b.wait(t, "pe", b.ev("dve"))
                for j in range(4):
                    tt = tc * 4 + j
                    ee = t.transpose(pC[:, j * 128:(j + 1) * 128],
                                     SAf[:, tt * 1024 + dt * 128: tt * 1024 + dt * 128 + 128],
                                     ctx["ident"][:])
                b.bump("pe", ee)
            @b.blk.vector
            def _(v, dt=dt, tc=tc):
                b.wait(v, "dve", b.ev("pe"), xev[0])
                xs = arena[:, (dt % 2) * 2048 + tc * 512:(dt % 2) * 2048 + tc * 512 + 512]
                b.bump("dve", v.scalar_tensor_tensor(xs, pC[:], scale, xs,
                                                     op0=ALU.mult, op1=ALU.add))
                b.fence(v)
                if extra_z:
                    b.bump("dve", v.scalar_tensor_tensor(xs, ctx["zT"][:, zslc(dt, tc)],
                                                         scale, xs, op0=ALU.mult, op1=ALU.add))
                    b.fence(v)
                b.bump("dve", v.tensor_copy(ctx["touch"][:], xs[0:1, 0:8]))
        @b.blk.sync
        def _(sy, dt=dt):
            b.wait(sy, "dS", b.ev("dve"))
            b.bump("dS", sy.dma_start(out=dst[dt * 128:(dt + 1) * 128, :],
                                      in_=arena[:, (dt % 2) * 2048:(dt % 2) * 2048 + T]), 16)







def v65_col(cfg, tt, g):
    return cfg["v65_off"] + tt * cfg["kvh"] * 65 + g * 65

def emit_attn_core(ctx, cfg):
    b = ctx["b"]
    SA, SAb, arena, zT = ctx["SA"], ctx["SAb"], ctx["arena"], ctx["zT"]
    pC, pD, pE, pF = ctx["pC"], ctx["pD"], ctx["pE"], ctx["pF"]
    nh = cfg["nh"]
    for h in cfg.get("h_order", range(nh)):
        hb = (h % 2) * 64
        hp = h // 2
        for qc in cfg.get("qc_order", range(4)):
            kts = cfg["kt_list"](qc)
            for i, (kt, mi) in enumerate(kts):
                bk = "pE" if i % 2 == 0 else "pF"
                p = ctx[bk]
                @b.blk.tensor
                def _(t, h=h, hb=hb, qc=qc, kt=kt, p=p, bk=bk):
                    b.wait(t, "pe", cfg["ready_ev"])
                    pk_wait(b, t, ctx, bk)
                    b.bump("pe", t.matmul(
                        p[:], SA[hb:hb + 64, cfg["k_off"] + (h // 2) * T + kt * 128:
                                 cfg["k_off"] + (h // 2) * T + kt * 128 + 128],
                        SA[hb:hb + 64, cfg["q_off"] + (h // 2) * T + qc * 512:
                           cfg["q_off"] + (h // 2) * T + qc * 512 + 512],
                        start=True, stop=True))
                if mi is not None:
                    @b.blk.vector
                    def _(v, mi=mi, p=p):
                        b.wait(v, "dve", b.ev("pe"), b.ev("act"))
                        b.bump("dve", v.tensor_tensor(
                            arena[:, 5632:6144], p[:],
                            arena[:, cfg["mask_arena_off"] + mi * 512:
                                  cfg["mask_arena_off"] + (mi + 1) * 512], op=ALU.add))
                    pk_set(ctx, b, bk, "dve")
                    @b.blk.scalar
                    def _(s, i=i, kt=kt):
                        b.wait(s, "act", b.ev("dve"))
                        b.bump("act", s.activation(
                            SAb[:, cfg["pT_off"] + i * 512: cfg["pT_off"] + (i + 1) * 512],
                            arena[:, 5632:6144], AF.Exp,
                            bias=vc(ctx, cfg["shift"]), scale=1.0))
                else:
                    @b.blk.scalar
                    def _(s, i=i, p=p):
                        b.wait(s, "act", b.ev("pe"))
                        b.bump("act", s.activation(
                            SAb[:, cfg["pT_off"] + i * 512: cfg["pT_off"] + (i + 1) * 512],
                            p[:], AF.Exp, bias=vc(ctx, cfg["shift"]), scale=1.0))
                    pk_set(ctx, b, bk, "act")
            @b.blk.tensor
            def _(t, h=h, qc=qc, kts=kts):
                b.wait(t, "pe", b.ev("act"), b.ev("dve"))
                g = h // (cfg["nh"] // cfg["kvh"]) if cfg["kvh"] < cfg["nh"] else h
                for i, (kt, mi) in enumerate(kts):
                    ee = t.matmul(pC[0:65, :],
                                  SAb[:, v65_col(cfg, kt, g):v65_col(cfg, kt, g) + 65],
                                  SAb[:, cfg["pT_off"] + i * 512: cfg["pT_off"] + (i + 1) * 512],
                                  start=(i == 0), stop=(i == len(kts) - 1))
                b.bump("pe", ee)
            @b.blk.vector
            def _(v):
                b.wait(v, "dve", b.ev("pe"))
                b.bump("dve", v.reciprocal(arena[0:1, 5632:6144], pC[64:65, :]))
                b.fence(v)
                b.bump("dve", v.tensor_copy(ctx["touch"][:], arena[0:1, 5632:5640]))
            @b.blk.tensor
            def _(t):
                b.wait(t, "pe", b.ev("dve"))
                b.bump("pe", t.matmul(pD[0:64, :], ctx["onesf"][:, 0:64],
                                      arena[0:1, 5632:6144], start=True, stop=True))
            @b.blk.vector
            def _(v, h=h, hb=hb, hp=hp, qc=qc):
                b.wait(v, "dve", b.ev("pe"))
                b.bump("dve", v.tensor_copy(arena[0:64, 5120:5632], pD[0:64, :]))
                b.fence(v)
                b.bump("dve", v.tensor_tensor(
                    zT[hb:hb + 64, hp * ZSTRIDE + 1 + qc * 512: hp * ZSTRIDE + 1 + qc * 512 + 512],
                    pC[0:64, :], arena[0:64, 5120:5632], op=ALU.mult))


def emit_outproj_residual(ctx, srcs, dst, w_name, bias_col_name=None):
    b = ctx["b"]
    arena, zT, wb = ctx["arena"], ctx["zT"], ctx["wb"]
    pA, pB = ctx["pA"], ctx["pB"]
    for do in range(ND):
        wev = [None]
        xev = [None]
        @b.blk.gpsimd
        def _(gp, do=do):
            b.wait(gp, "pool", ctx["wbev"][do % 2])
            wev[0] = b.bump("dG0" if do % 2 == 0 else "dG1", gp.dma_start(
                out=wb[do % 2][:, 0:1024].rearrange("p (a c) -> p a c", a=8),
                in_=ctx["IN"][w_name].rearrange("(a p) f -> p a f", p=128)[:, :, do * 128:(do + 1) * 128]), 16)
        dguard = b.ev("dve")
        @b.blk.sync
        def _(sy, do=do, dguard=dguard):
            b.wait(sy, "dS", dguard, b.ev("dS"))
            for si, s_ in enumerate(srcs):
                xev[0] = b.bump("dS", sy.dma_start(
                    out=arena[:, si * 2048:si * 2048 + T],
                    in_=s_[do * 128:(do + 1) * 128, :]), 16)
        for tc in range(4):
            p = pA if tc % 2 == 0 else pB
            @b.blk.tensor
            def _(t, do=do, tc=tc, p=p):
                b.wait(t, "pe", wev[0], b.ev("dve"))
                for ft in range(ND):
                    ee = t.matmul(p[:], wb[do % 2][:, ft * 128:(ft + 1) * 128],
                                  zT[:, zslc(ft, tc)], start=(ft == 0), stop=(ft == ND - 1))
                b.bump("pe", ee)
            ctx["wbev"][do % 2] = b.ev("pe")
            @b.blk.vector
            def _(v, do=do, tc=tc, p=p, nsrc=len(srcs)):
                b.wait(v, "dve", b.ev("pe"), xev[0])
                xs = arena[:, tc * 512:tc * 512 + 512]
                if nsrc > 1:
                    b.bump("dve", v.tensor_tensor(xs, xs,
                                                  arena[:, 2048 + tc * 512:2048 + tc * 512 + 512],
                                                  op=ALU.add))
                    b.fence(v)
                if bias_col_name is not None:
                    b.bump("dve", v.scalar_tensor_tensor(xs, p[:], vc(ctx, bias_col_name, do),
                                                         xs, op0=ALU.add, op1=ALU.add))
                else:
                    b.bump("dve", v.scalar_tensor_tensor(xs, p[:], 0.0, xs,
                                                         op0=ALU.add, op1=ALU.add))
                b.fence(v)
                b.bump("dve", v.tensor_copy(ctx["touch"][:], xs[0:1, 0:8]))
        @b.blk.sync
        def _(sy, do=do):
            b.wait(sy, "dS", b.ev("dve"))
            b.bump("dS", sy.dma_start(out=dst[do * 128:(do + 1) * 128, :],
                                      in_=arena[:, 0:T]), 16)


def proj_fm(ctx, w_name, fcols, out_sa_off, n_f, ready_evs=(), bias_row=None,
            add_tile=None, evac_scale=None, src_tiles=None):
    b = ctx["b"]
    SA, SAb, arena, zT, wb = ctx["SA"], ctx["SAb"], ctx["arena"], ctx["zT"], ctx["wb"]
    pA, pB = ctx["pA"], ctx["pB"]
    if src_tiles is None:
        srcbuf, sbase, sstride, nk = zT, 1, ZSTRIDE, ND
    else:
        srcbuf, sbase, sstride, nk = src_tiles
    ready_evs = tuple(ready_evs) + (b.ev("dve"),)
    nchunk = (n_f + 511) // 512
    BANKS = ["pA", "pB", "pC", "pD"]
    for c in range(nchunk):
        cw = min(512, n_f - c * 512)
        wev = [None]
        ci = ctx.get("wb_ci", 0); ctx["wb_ci"] = ci + 1
        @b.blk.gpsimd
        def _(gp, c=c, cw=cw, ci=ci):
            b.wait(gp, "pool", ctx["wbev"][ci % 2])
            wev[0] = b.bump("dG0" if ci % 2 == 0 else "dG1", gp.dma_start(
                out=wb[ci % 2][:, 0:nk * cw].rearrange("p (a q) -> p a q", a=nk),
                in_=ctx["IN"][w_name].rearrange("(a p) f -> p a f", p=128)[:, :, fcols + c * 512:fcols + c * 512 + cw]), 16)
        nf_t = (cw + 127) // 128
        for fi in range(nf_t):
            fw = min(128, cw - fi * 128)
            ft = c * 4 + fi
            for tc in range(4):
                bk = BANKS[(fi * 4 + tc) % 4]
                p = ctx[bk]
                @b.blk.tensor
                def _(t, c=c, ci=ci, cw=cw, fi=fi, fw=fw, ft=ft, tc=tc, p=p, bk=bk):
                    b.wait(t, "pe", wev[0], *ready_evs)
                    pk_wait(b, t, ctx, bk)
                    for dt in range(nk):
                        ee = t.matmul(p[0:fw, 0:512],
                                      wb[ci % 2][:, dt * cw + fi * 128: dt * cw + fi * 128 + fw],
                                      srcbuf[:, sbase + dt * sstride + tc * 512:
                                             sbase + dt * sstride + tc * 512 + 512],
                                      start=(dt == 0), stop=(dt == nk - 1 and bias_row is None))
                    if bias_row is not None:
                        ee = t.matmul(p[0:fw, 0:512],
                                      SAb[0:1, bias_row + c * 512 + fi * 128:
                                          bias_row + c * 512 + fi * 128 + fw],
                                      ctx["onesb"][:, 0:512], start=False, stop=True)
                    b.bump("pe", ee)
                ctx["wbev"][ci % 2] = b.ev("pe")
                @b.blk.vector
                def _(v, c=c, fi=fi, fw=fw, ft=ft, tc=tc, p=p):
                    b.wait(v, "dve", b.ev("pe"))
                    dstsl = SA[0:fw, out_sa_off + ft * T + tc * 512: out_sa_off + ft * T + tc * 512 + 512]
                    if add_tile is not None:
                        b.bump("dve", v.tensor_tensor(dstsl, p[0:fw, 0:512],
                                                      arena[0:fw, add_tile + tc * 512:add_tile + tc * 512 + 512],
                                                      op=ALU.add))
                    elif evac_scale is not None:
                        b.bump("dve", v.tensor_scalar(dstsl, p[0:fw, 0:512], evac_scale, None,
                                                      op0=ALU.mult))
                    else:
                        b.bump("dve", v.tensor_copy(dstsl, p[0:fw, 0:512]))
                pk_set(ctx, b, bk, "dve")




def proj_v65(ctx, w_name, fcols, nh_v, cfg, ready_evs=(), bias_row=None,
             src_tiles=None):
    b = ctx["b"]
    SA, SAb, zT, wb = ctx["SA"], ctx["SAb"], ctx["zT"], ctx["wb"]
    pA, pB = ctx["pA"], ctx["pB"]
    if src_tiles is None:
        srcbuf, sbase, sstride, nk = zT, 1, ZSTRIDE, ND
    else:
        srcbuf, sbase, sstride, nk = src_tiles
    ready_evs = tuple(ready_evs) + (b.ev("dve"),)
    @b.blk.vector
    def _(v):
        b.wait(v, "dve", *ready_evs)
        for tt in range(NT):
            for h in range(nh_v):
                ee = v.memset(SAb[:, v65_col(cfg, tt, h) + 64: v65_col(cfg, tt, h) + 65], 1.0)
        b.bump("dve", ee)
        b.fence(v)
        b.bump("dve", v.tensor_copy(ctx["touch"][:], SAb[0:1, v65_col(cfg, 0, 0) + 64:v65_col(cfg, 0, 0) + 72]))
    nchunk = (nh_v * 64 + 511) // 512
    BANKS = ["pA", "pB", "pC", "pD"]
    for c in range(nchunk):
        cw = min(512, nh_v * 64 - c * 512)
        wev = [None]
        ci = ctx.get("wb_ci", 0); ctx["wb_ci"] = ci + 1
        @b.blk.gpsimd
        def _(gp, c=c, cw=cw, ci=ci):
            b.wait(gp, "pool", ctx["wbev"][ci % 2])
            wev[0] = b.bump("dG0" if ci % 2 == 0 else "dG1", gp.dma_start(
                out=wb[ci % 2][:, 0:nk * cw].rearrange("p (a q) -> p a q", a=nk),
                in_=ctx["IN"][w_name].rearrange("(a p) f -> p a f", p=128)[:, :, fcols + c * 512:fcols + c * 512 + cw]), 16)
        for tt in range(NT):
            bk = BANKS[tt % 4]
            p = ctx[bk]
            @b.blk.tensor
            def _(t, c=c, ci=ci, cw=cw, tt=tt, p=p, bk=bk):
                b.wait(t, "pe", wev[0], b.ev("dve") if tt == 0 and c == 0 else None,
                       *ready_evs)
                pk_wait(b, t, ctx, bk)
                for dt in range(nk):
                    ee = t.matmul(p[:, 0:cw],
                                  srcbuf[:, sbase + dt * sstride + tt * 128: sbase + dt * sstride + tt * 128 + 128],
                                  wb[ci % 2][:, dt * cw: dt * cw + cw],
                                  start=(dt == 0), stop=(dt == nk - 1 and bias_row is None))
                if bias_row is not None:
                    ee = t.matmul(p[:, 0:cw], ctx["onesb"][:, 0:128],
                                  SAb[0:1, bias_row + c * 512: bias_row + c * 512 + cw],
                                  start=False, stop=True)
                b.bump("pe", ee)
            ctx["wbev"][ci % 2] = b.ev("pe")
            @b.blk.vector
            def _(v, c=c, cw=cw, tt=tt, p=p):
                b.wait(v, "dve", b.ev("pe"))
                for hh in range(cw // 64):
                    h = c * 8 + hh
                    ee = v.tensor_copy(SAb[:, v65_col(cfg, tt, h):v65_col(cfg, tt, h) + 64],
                                       p[:, hh * 64:(hh + 1) * 64])
                b.bump("dve", ee)
            pk_set(ctx, b, bk, "dve")



def emit_l3a(ctx, src, dst, scale):
    b = ctx["b"]
    SA, SAb, arena = ctx["SA"], ctx["SAb"], ctx["arena"]
    zev = emit_ln(ctx, src, rms=False)
    BROW = 24832
    @b.blk.gpsimd
    def _(gp):
        b.wait(gp, "pool", zev)
        for r in range(3):
            b.bump("dG", gp.dma_start(out=SAb[0:1, BROW + r * 1024: BROW + (r + 1) * 1024],
                                      in_=ctx["IN"]["vecrows"][10 + r:11 + r, :]), 16)
    @b.blk.sync
    def _(sy):
        b.wait(sy, "dS", zev)
        b.bump("dS", sy.dma_start(out=arena[:, 2048:4096], in_=ctx["IN"]["c_posL3"]), 16)
    rdy = (b.ev("dG"), b.ev("dS"))
    cfg = dict(nh=16, kvh=16, q_off=0, k_off=16384, v65_off=0, pT_off=16640,
               shift="k_shift", mask_arena_off=None,
               kt_list=lambda qc: [(kt, None) for kt in range(16)])
    proj_fm(ctx, "l3a_Wqkv", 0, 0, 1024, ready_evs=rdy, bias_row=BROW,
            evac_scale=0.125)
    proj_fm(ctx, "l3a_Wqkv", 1024, 16384, 1024, ready_evs=rdy, bias_row=BROW + 1024,
            add_tile=2048)
    proj_v65(ctx, "l3a_Wqkv", 2048, 16, cfg, ready_evs=rdy, bias_row=BROW + 2048)
    cfg["ready_ev"] = b.ev("dve")
    emit_attn_core(ctx, cfg)
    emit_outproj_residual(ctx, [src], dst, "l3a_Wout", bias_col_name="l3a_bout")




def emit_rope(ctx, targets, cos_off, sin_off, rot_wb_off, ready_evs=()):
    b = ctx["b"]
    SA, SAb, arena, wb = ctx["SA"], ctx["SAb"], ctx["arena"], ctx["wb"]
    pD = ctx["pD"]
    TMP1, TMP2 = 24832, 25344
    for (off, ntiles) in targets:
        for tl in range(ntiles):
            for tc in range(4):
                sl = SA[:, off + tl * T + tc * 512: off + tl * T + tc * 512 + 512]
                @b.blk.tensor
                def _(t, sl=sl):
                    b.wait(t, "pe", *ready_evs)
                    b.wait(t, "pe", b.ev("dve"))
                    b.bump("pe", t.matmul(pD[:], wb[0][:, rot_wb_off:rot_wb_off + 128],
                                          sl, start=True, stop=True))
                @b.blk.vector
                def _(v, sl=sl, tc=tc):
                    b.wait(v, "dve", b.ev("pe"))
                    b.bump("dve", v.tensor_tensor(
                        SAb[:, TMP1:TMP1 + 512], sl,
                        arena[:, cos_off + tc * 512:cos_off + tc * 512 + 512], op=ALU.mult))
                    b.bump("dve", v.tensor_tensor(
                        SAb[:, TMP2:TMP2 + 512], pD[:],
                        arena[:, sin_off + tc * 512:sin_off + tc * 512 + 512], op=ALU.mult))
                    b.fence(v)
                    b.bump("dve", v.tensor_tensor(sl, SAb[:, TMP1:TMP1 + 512],
                                                  SAb[:, TMP2:TMP2 + 512], op=ALU.add))



def emit_l1a(ctx, src, dst, scale):
    b = ctx["b"]
    SA, SAb, arena, wb = ctx["SA"], ctx["SAb"], ctx["arena"], ctx["wb"]
    zev = emit_ln(ctx, src, rms=False)
    @b.blk.sync
    def _(sy):
        b.wait(sy, "dS", zev)
        b.bump("dS", sy.dma_start(out=arena[:, 2048:4096], in_=ctx["IN"]["c_ropeL1"][0]), 16)
        b.bump("dS", sy.dma_start(out=arena[:, 4096:6144], in_=ctx["IN"]["c_ropeL1"][1]), 16)
    rdy = (b.ev("dS"),)
    cfg = dict(nh=16, kvh=16, q_off=0, k_off=16384, v65_off=0, pT_off=16640,
               shift="k_shift", mask_arena_off=2048,
               kt_list=lambda qc: [(kt, kt - qc * 4 + 1)
                                   for kt in range(max(0, qc * 4 - 1), qc * 4 + 4)])
    proj_fm(ctx, "l1a_WqT", 0, 0, 1024, ready_evs=rdy, evac_scale=0.125)
    proj_fm(ctx, "l1a_WkT", 0, 16384, 1024, ready_evs=rdy)
    proj_v65(ctx, "l1a_WvT", 0, 16, cfg, ready_evs=rdy)
    rev = [None]
    @b.blk.gpsimd
    def _(gp):
        b.wait(gp, "pool", ctx["wbev"][0])
        rev[0] = b.bump("dG0", gp.dma_start(out=wb[0][:, 0:128], in_=ctx["IN"]["c_rotH"]), 16)
    emit_rope(ctx, [(0, 8), (16384, 8)], 2048, 4096, 0, ready_evs=(rev[0],))
    ctx["wbev"][0] = b.ev("pe")
    mev = [None]
    dguard = b.ev("dve")
    @b.blk.sync
    def _(sy, dguard=dguard):
        b.wait(sy, "dS", dguard)
        mev[0] = b.bump("dS", sy.dma_start(
            out=arena[:, 2048:2048 + 5 * 512].rearrange("p (a c) -> p a c", a=5),
            in_=ctx["IN"]["c_maskL1"].rearrange("a p c -> p a c")), 16)
    cfg["ready_ev"] = mev[0]
    emit_attn_core(ctx, cfg)
    emit_outproj_residual(ctx, [src], dst, "l1a_WoT", bias_col_name="l1a_bo")




def emit_qknorm(ctx, targets, qscale=None, ready_evs=()):
    b = ctx["b"]
    SA, arena = ctx["SA"], ctx["arena"]
    pD, pE = ctx["pD"], ctx["pE"]
    for (off, ntiles) in targets:
        for tl in range(ntiles):
            for tc in range(4):
                sl = SA[:, off + tl * T + tc * 512: off + tl * T + tc * 512 + 512]
                @b.blk.scalar
                def _(s, sl=sl):
                    b.wait(s, "act", b.ev("dve"), b.ev("pe"))
                    b.bump("act", s.activation(arena[:, 2048:2560], sl, AF.Square,
                                               bias=vc(ctx, "k_zero"), scale=1.0))
                @b.blk.tensor
                def _(t):
                    b.wait(t, "pe", b.ev("act"), *ready_evs)
                    b.bump("pe", t.matmul(pD[0:2, :], arena[:, 2600:2602],
                                          arena[:, 2048:2560], start=True, stop=True))
                @b.blk.vector
                def _(v):
                    b.wait(v, "dve", b.ev("pe"))
                    b.bump("dve", v.tensor_scalar(arena[0:2, 5120:5632], pD[0:2, :],
                                                  1.0 / 64, None, op0=ALU.mult))
                    b.fence(v)
                    b.bump("dve", v.tensor_copy(ctx["touch"][:], arena[0:2, 5120:5128][0:1, :]))
                @b.blk.scalar
                def _(s):
                    b.wait(s, "act", b.ev("dve"))
                    b.bump("act", s.activation(arena[0:2, 5120:5632], arena[0:2, 5120:5632],
                                               AF.Sqrt, bias=ctx["vcol"][0:2, VECCOLS["k_eps6"][0]:VECCOLS["k_eps6"][0] + 1],
                                               scale=1.0))
                @b.blk.vector
                def _(v, qscale=qscale):
                    b.wait(v, "dve", b.ev("act"))
                    b.bump("dve", v.reciprocal(arena[0:2, 5120:5632], arena[0:2, 5120:5632]))
                    b.fence(v)
                    if qscale is not None:
                        b.bump("dve", v.tensor_scalar(arena[0:2, 5120:5632], arena[0:2, 5120:5632],
                                                      qscale, None, op0=ALU.mult))
                        b.fence(v)
                    b.bump("dve", v.tensor_copy(ctx["touch"][:], arena[0:2, 5120:5128][0:1, :]))
                @b.blk.tensor
                def _(t):
                    b.wait(t, "pe", b.ev("dve"))
                    b.bump("pe", t.matmul(pE[:, :], arena[0:2, 2604:2732],
                                          arena[0:2, 5120:5632], start=True, stop=True))
                @b.blk.vector
                def _(v, sl=sl):
                    b.wait(v, "dve", b.ev("pe"))
                    b.bump("dve", v.tensor_tensor(sl, sl, pE[:], op=ALU.mult))



def emit_l2a(ctx, src, dst, scale, halves=(0, 1)):
    b = ctx["b"]
    SA, SAb, SAf, arena, zT, wb = (ctx["SA"], ctx["SAb"], ctx["SAf"],
                                   ctx["arena"], ctx["zT"], ctx["wb"])
    hxi = ctx["hxi"]
    cfgs = [("l2a_gwq", "l2a_gkvd", "l2a_gkup", "l2a_gvup", "l2a_gwo", "l2a_n1",
             "c_ropeG", None, None),
            ("l2a_lwq", "l2a_lwk", None, "l2a_lwv", "l2a_lwo", "l2a_n2",
             "c_ropeL", None, None)]
    for half in halves:
        (wq, wkd, wku, wvu, wo, nw, ropec, src_h, dsts) = cfgs[half]
        glob = (half == 0)
        if glob:
            emit_ln(ctx, src, rms=False, want_zlo=True)
            @b.blk.sync
            def _(sy):
                b.wait(sy, "dS", b.ev("dve"))
                for dt in range(ND):
                    b.bump("dS", sy.dma_start(out=hxi[dt * 128:(dt + 1) * 128, :],
                                              in_=SAf[:, dt * T:(dt + 1) * T]), 16)
            emit_ln(ctx, None, rms=True, w_col_name="l2a_n1", eps_name="k_eps6",
                    resident=True)
        else:
            emit_ln(ctx, dst, rms=True, w_col_name="l2a_n2", eps_name="k_eps6")
        cev = [None]
        @b.blk.sync
        def _(sy):
            b.wait(sy, "dS", b.ev("dve"), b.ev("pe"))
            b.bump("dS", sy.dma_start(out=arena[:, 2048:4096], in_=ctx["IN"][ropec][0]), 16)
            cev[0] = b.bump("dS", sy.dma_start(out=arena[:, 4096:6144],
                                               in_=ctx["IN"][ropec][1]), 16)
        rdy = (cev[0],)
        cfg = dict(nh=16, kvh=4, q_off=0, k_off=16384, v65_off=0, pT_off=16640,
                   shift="k_shift8", mask_arena_off=2048)
        if glob:
            cfg["kt_list"] = lambda qc: [(kt, kt - qc * 4 if kt >= qc * 4 else None)
                                         for kt in range(0, qc * 4 + 4)]
        else:
            cfg["kt_list"] = lambda qc: [(kt, kt - qc * 4 + 2)
                                         for kt in range(max(0, qc * 4 - 2), qc * 4 + 4)]
        proj_fm(ctx, wq, 0, 0, 1024, ready_evs=rdy)
        if glob:
            proj_fm(ctx, wkd, 0, 16384, 256, ready_evs=rdy)
            lat_src = (SA, 16384, T, 2)
            proj_fm(ctx, wku, 0, 20480, 256, ready_evs=rdy, src_tiles=lat_src)
            proj_v65(ctx, wvu, 0, 4, cfg, ready_evs=rdy, src_tiles=lat_src)
        else:
            proj_fm(ctx, wkd, 0, 20480, 256, ready_evs=rdy)
            proj_v65(ctx, wvu, 0, 4, cfg, ready_evs=rdy)
        rev = [None]
        @b.blk.gpsimd
        def _(gp):
            b.wait(gp, "pool", ctx["wbev"][0])
            rev[0] = b.bump("dG0", gp.dma_start(out=wb[0][:, 0:128], in_=ctx["IN"]["c_rotP"]), 16)
        emit_rope(ctx, [(0, 8), (20480, 2)], 2048, 4096, 0, ready_evs=(rev[0],))
        ctx["wbev"][0] = b.ev("pe")
        iev = [None]
        rguard = b.ev("dve")
        @b.blk.sync
        def _(sy, rguard=rguard):
            b.wait(sy, "dS", rguard)
            b.bump("dS", sy.dma_start(out=arena[:, 2600:2602], in_=ctx["IN"]["c_ind2"]), 16)
            iev[0] = b.bump("dS", sy.dma_start(out=arena[0:2, 2604:2732],
                                               in_=ctx["IN"]["c_ind2T"]), 16)
        emit_qknorm(ctx, [(0, 8)], qscale=0.125, ready_evs=(iev[0],))
        emit_qknorm(ctx, [(20480, 2)], ready_evs=(iev[0],))
        @b.blk.vector
        def _(v):
            b.wait(v, "dve", b.ev("dve"), b.ev("pe"), b.ev("act"))
            b.bump("dve", v.tensor_copy(SAb[:, 4160:6208], SA[:, 20480:22528]))
            b.bump("dve", v.tensor_copy(SAb[:, 6208:8256], SA[:, 22528:24576]))
            b.fence(v)
            for h in range(16):
                g = h // 4
                ee = v.tensor_copy(SA[(h % 2) * 64:(h % 2) * 64 + 64,
                                      16384 + (h // 2) * T:16384 + (h // 2) * T + T],
                                   SAb[(g % 2) * 64:(g % 2) * 64 + 64,
                                       4160 + (g // 2) * T:4160 + (g // 2) * T + T])
            b.bump("dve", ee)
        mev = [None]
        dguard = b.ev("dve")
        @b.blk.sync
        def _(sy, glob=glob, dguard=dguard):
            b.wait(sy, "dS", dguard)
            if glob:
                mev[0] = b.bump("dS", sy.dma_start(
                    out=arena[:, 2048:2048 + 4 * 512].rearrange("p (a c) -> p a c", a=4),
                    in_=ctx["IN"]["c_maskL2g"].rearrange("a p c -> p a c")), 16)
            else:
                mev[0] = b.bump("dS", sy.dma_start(
                    out=arena[:, 2048:2048 + 6 * 512].rearrange("p (a c) -> p a c", a=6),
                    in_=ctx["IN"]["c_maskL2l"].rearrange("a p c -> p a c")), 16)
        cfg["ready_ev"] = mev[0]
        emit_attn_core(ctx, cfg)
        if glob:
            emit_outproj_residual(ctx, [hxi], dst, wo)
        else:
            emit_outproj_residual(ctx, [src, dst], dst, wo)





def emit_l0a(ctx, src, dst, scale):
    b = ctx["b"]
    SA, SAf, arena, zT, wb = ctx["SA"], ctx["SAf"], ctx["arena"], ctx["zT"], ctx["wb"]
    pA, pB, pStat = ctx["pA"], ctx["pB"], ctx["pStat"]
    hxi = ctx["hxi"]
    zev = emit_ln(ctx, src, rms=False)
    for tc in range(4):
        sq_evs = []
        @b.blk.scalar
        def _(s, tc=tc):
            b.wait(s, "act", zev, b.ev("pe"))
            for dt in range(ND):
                sq_evs.append(b.bump("act", s.activation(
                    SA[:, 16384 + dt * 512:16384 + dt * 512 + 512],
                    zT[:, zslc(dt, tc)], AF.Square, bias=vc(ctx, "k_zero"), scale=1.0)))
        @b.blk.tensor
        def _(t, tc=tc):
            b.wait(t, "pe", b.ev("dve"))
            for dt in range(ND):
                b.wait(t, "pe", sq_evs[dt])
                ee = t.matmul(pStat[32:33, :], ctx["onescb"][:, :],
                              SA[:, 16384 + dt * 512:16384 + dt * 512 + 512],
                              start=(dt == 0), stop=(dt == ND - 1))
            b.bump("pe", ee)
        @b.blk.vector
        def _(v, tc=tc):
            b.wait(v, "dve", b.ev("pe"))
            b.bump("dve", v.tensor_copy(arena[0:1, tc * 512:(tc + 1) * 512], pStat[32:33, :]))
    @b.blk.vector
    def _(v):
        b.wait(v, "dve", b.ev("dve"))
        b.bump("dve", v.tensor_scalar(arena[0:1, 0:T], arena[0:1, 0:T], 1.0 / D, None,
                                      op0=ALU.mult))
        b.fence(v)
        b.bump("dve", v.tensor_copy(ctx["touch"][:], arena[0:1, 0:8]))
    @b.blk.scalar
    def _(s):
        b.wait(s, "act", b.ev("dve"))
        o6 = VECCOLS["k_eps6"][0]
        b.bump("act", s.activation(arena[0:1, 0:T], arena[0:1, 0:T], AF.Sqrt,
                                   bias=ctx["vcol"][0:1, o6:o6 + 1], scale=1.0))
    @b.blk.vector
    def _(v):
        b.wait(v, "dve", b.ev("act"))
        b.bump("dve", v.reciprocal(arena[0:1, 0:T], arena[0:1, 0:T]))
        b.fence(v)
        b.bump("dve", v.tensor_copy(ctx["touch"][:], arena[0:1, 0:8]))
    for tc in range(4):
        @b.blk.tensor
        def _(t, tc=tc):
            b.wait(t, "pe", b.ev("dve"))
            b.bump("pe", t.matmul(ctx["pE"][:, :], ctx["onesf"][:, 0:128],
                                  arena[0:1, tc * 512:(tc + 1) * 512], start=True, stop=True))
        @b.blk.vector
        def _(v, tc=tc):
            b.wait(v, "dve", b.ev("pe"))
            b.bump("dve", v.tensor_copy(arena[:, 2048 + tc * 512:2048 + (tc + 1) * 512],
                                        ctx["pE"][:]))
    @b.blk.vector
    def _(v):
        b.wait(v, "dve", b.ev("dve"))
        for dt in range(ND):
            ee = v.scalar_tensor_tensor(SA[:, dt * T:(dt + 1) * T], zT[:, zslc(dt)],
                                        vc(ctx, "l0a_ssmw", dt), arena[:, 2048:2048 + T],
                                        op0=ALU.mult, op1=ALU.mult)
        b.bump("dve", ee)
        b.fence(v)
        b.bump("dve", v.tensor_copy(ctx["touch"][:], SA[0:1, 0:8]))
    uev = b.ev("dve")
    wev = [None]
    @b.blk.gpsimd
    def _(gp):
        b.wait(gp, "pool", ctx["wbev"][0])
        wev[0] = b.bump("dG0", gp.dma_start(
            out=wb[0][:, 0:2048].rearrange("p (a q) -> p a q", a=8),
            in_=ctx["IN"]["l0a_BwT"].rearrange("(a p) f -> p a f", p=128)), 16)
    for fi in range(2):
        for tc in range(4):
            p = pA if tc % 2 == 0 else pB
            @b.blk.tensor
            def _(t, fi=fi, tc=tc, p=p):
                b.wait(t, "pe", wev[0], uev, b.ev("dve"))
                for dt in range(ND):
                    ee = t.matmul(p[:], wb[0][:, dt * 256 + fi * 128: dt * 256 + fi * 128 + 128],
                                  SA[:, dt * T + tc * 512: dt * T + tc * 512 + 512],
                                  start=(dt == 0), stop=(dt == ND - 1))
                b.bump("pe", ee)
            @b.blk.vector
            def _(v, fi=fi, tc=tc, p=p):
                b.wait(v, "dve", b.ev("pe"))
                b.bump("dve", v.tensor_copy(SAf[:, fi * T + tc * 512: fi * T + tc * 512 + 512],
                                            p[:]))
            ctx["wbev"][0] = b.ev("pe")
    @b.blk.scalar
    def _(s):
        b.wait(s, "act", b.ev("dve"))
        o = VECCOLS["l0a_loglam"][0]
        b.bump("act", s.activation(arena[:, 0:2], ctx["vcol"][:, o:o + 2], AF.Sigmoid,
                                   bias=vc(ctx, "k_zero"), scale=1.0))
    for nt in range(2):
        @b.blk.vector
        def _(v, nt=nt):
            b.wait(v, "dve", b.ev("act"), b.ev("pe"))
            reg = arena[:, 2048 + nt * 2048:2048 + nt * 2048 + T]
            b.bump("dve", v.memset(reg, 0.0))
            b.fence(v)
            b.bump("dve", v.tensor_scalar(reg, reg, arena[:, nt:nt + 1], None, op0=ALU.add))
            b.fence(v)
            b.bump("dve", v.tensor_tensor_scan(SA[:, 20480 + nt * T:20480 + (nt + 1) * T],
                                               reg, SAf[:, nt * T:(nt + 1) * T], 0.0,
                                               op0=ALU.mult, op1=ALU.add))
    scanev = b.ev("dve")
    for c in range(2):
        wev2 = [None]
        @b.blk.gpsimd
        def _(gp, c=c):
            b.wait(gp, "pool", ctx["wbev"][1])
            wev2[0] = b.bump("dG1", gp.dma_start(
                out=wb[1][:, 0:1024].rearrange("p (a q) -> p a q", a=2),
                in_=ctx["IN"]["l0a_CwT"].rearrange("(a p) f -> p a f", p=128)[:, :, c * 512:(c + 1) * 512]), 16)
        for fi in range(4):
            ft = c * 4 + fi
            for tc in range(4):
                p = pA if (fi * 4 + tc) % 2 == 0 else pB
                @b.blk.tensor
                def _(t, c=c, fi=fi, ft=ft, tc=tc, p=p):
                    b.wait(t, "pe", wev2[0], scanev, b.ev("dve"))
                    for a in range(2):
                        ee = t.matmul(p[:], wb[1][:, a * 512 + fi * 128: a * 512 + fi * 128 + 128],
                                      SA[:, 20480 + a * T + tc * 512: 20480 + a * T + tc * 512 + 512],
                                      start=(a == 0), stop=(a == 1))
                    b.bump("pe", ee)
                @b.blk.vector
                def _(v, ft=ft, tc=tc, p=p):
                    b.wait(v, "dve", b.ev("pe"))
                    xs = SAf[:, ft * T + tc * 512: ft * T + tc * 512 + 512]
                    b.bump("dve", v.scalar_tensor_tensor(xs, SA[:, ft * T + tc * 512: ft * T + tc * 512 + 512],
                                                         vc(ctx, "l0a_Dskip", ft), p[:],
                                                         op0=ALU.mult, op1=ALU.add))
                    b.fence(v)
                    b.bump("dve", v.tensor_tensor(xs, xs, zT[:, zslc(ft, tc)], op=ALU.add))
                    b.fence(v)
                    b.bump("dve", v.tensor_copy(ctx["touch"][:], xs[0:1, 0:8]))
                ctx["wbev"][1] = b.ev("pe")
    @b.blk.sync
    def _(sy):
        b.wait(sy, "dS", b.ev("dve"), b.ev("dS"))
        for dt in range(ND):
            b.bump("dS", sy.dma_start(out=hxi[dt * 128:(dt + 1) * 128, :],
                                      in_=SAf[:, dt * T:(dt + 1) * T]), 16)
    z2ev = emit_ln(ctx, None, rms=True, w_col_name="l0a_ffnw", eps_name="k_eps6", resident=True)
    NFT = 22
    for th in range(2):
        tbase = th * 1024
        for cch in range(6):
            cw = min(512, 2736 - cch * 512)
            nf_t = (cw + 127) // 128
            wev3 = [None, None]
            @b.blk.gpsimd
            def _(gp, cch=cch, cw=cw):
                b.wait(gp, "pool", ctx["wbev"][0], ctx["wbev"][1])
                wev3[1] = b.bump("dG0", gp.dma_start(
                    out=wb[0][:, 0:8 * cw].rearrange("p (a q) -> p a q", a=8),
                    in_=ctx["IN"]["l0a_w1T"].rearrange("(a p) f -> p a f", p=128)[:, :, cch * 512:cch * 512 + cw]), 16)
                wev3[0] = b.bump("dG1", gp.dma_start(
                    out=wb[1][:, 0:8 * cw].rearrange("p (a q) -> p a q", a=8),
                    in_=ctx["IN"]["l0a_w3T"].rearrange("(a p) f -> p a f", p=128)[:, :, cch * 512:cch * 512 + cw]), 16)
            for fi in range(nf_t):
                fw = min(128, cw - fi * 128)
                ft = cch * 4 + fi
                for tc2 in range(2):
                    tsl = slice(tbase + tc2 * 512, tbase + tc2 * 512 + 512)
                    @b.blk.tensor
                    def _(t, cch=cch, cw=cw, fi=fi, fw=fw, tsl=tsl):
                        b.wait(t, "pe", wev3[0], wev3[1], z2ev)
                        pk_wait(b, t, ctx, "pA")
                        pk_wait(b, t, ctx, "pB")
                        for dt in range(ND):
                            t.matmul(pA[0:fw, :], wb[0][:, dt * cw + fi * 128: dt * cw + fi * 128 + fw],
                                     zT[:, dt * ZSTRIDE + 1 + tsl.start: dt * ZSTRIDE + 1 + tsl.start + 512],
                                     start=(dt == 0), stop=(dt == ND - 1))
                        for dt in range(ND):
                            ee = t.matmul(pB[0:fw, :], wb[1][:, dt * cw + fi * 128: dt * cw + fi * 128 + fw],
                                          zT[:, dt * ZSTRIDE + 1 + tsl.start: dt * ZSTRIDE + 1 + tsl.start + 512],
                                          start=(dt == 0), stop=(dt == ND - 1))
                        b.bump("pe", ee)
                    ctx["wbev"][0] = b.ev("pe")
                    ctx["wbev"][1] = b.ev("pe")
                    @b.blk.scalar
                    def _(s, fw=fw):
                        b.wait(s, "act", b.ev("pe"), b.ev("dve"))
                        b.bump("act", s.activation(arena[0:fw, 2048:2560], pA[0:fw, :],
                                                   AF.Silu,
                                                   bias=ctx["vcol"][0:fw, VECCOLS["k_zero"][0]:VECCOLS["k_zero"][0] + 1],
                                                   scale=1.0))
                    @b.blk.vector
                    def _(v, ft=ft, fw=fw, tc2=tc2):
                        b.wait(v, "dve", b.ev("act"))
                        b.bump("dve", v.tensor_tensor(
                            SA[0:fw, ft * 1024 + tc2 * 512: ft * 1024 + tc2 * 512 + 512],
                            arena[0:fw, 2048:2560], pB[0:fw, :], op=ALU.mult))
                    pk_set(ctx, b, "pA", "act")
                    pk_set(ctx, b, "pB", "dve")
        for do in range(ND):
            wev4 = [None]
            xev = [None]
            guard = (b.ev("pe"),)
            @b.blk.gpsimd
            def _(gp, do=do):
                b.wait(gp, "pool", ctx["wbev"][do % 2])
                wev4[0] = b.bump("dG0" if do % 2 == 0 else "dG1", gp.dma_start(
                    out=wb[do % 2][:, 0:2816].rearrange("p (a q) -> p a q", a=22),
                    in_=ctx["IN"]["l0a_w2T"].rearrange("(a p) f -> p a f", p=128)[:, :, do * 128:(do + 1) * 128]), 16)
            dguard = b.ev("dve")
            @b.blk.sync
            def _(sy, do=do, th=th, dguard=dguard):
                b.wait(sy, "dS", dguard, b.ev("dS"))
                xev[0] = b.bump("dS", sy.dma_start(
                    out=arena[:, 4096 + (do % 2) * 1024:4096 + (do % 2) * 1024 + 1024],
                    in_=src[do * 128:(do + 1) * 128, th * 1024:(th + 1) * 1024]), 16)
            for tc2 in range(2):
                bk = ["pA", "pB", "pC", "pD"][(do * 2 + tc2) % 4]
                p = ctx[bk]
                @b.blk.tensor
                def _(t, do=do, tc2=tc2, p=p, bk=bk):
                    b.wait(t, "pe", wev4[0], b.ev("dve") if do == 0 and tc2 == 0 else None)
                    pk_wait(b, t, ctx, bk)
                    for a in range(22):
                        ee = t.matmul(p[:], wb[do % 2][:, a * 128:(a + 1) * 128],
                                      SA[:, a * 1024 + tc2 * 512: a * 1024 + tc2 * 512 + 512],
                                      start=(a == 0), stop=(a == 21))
                    b.bump("pe", ee)
                ctx["wbev"][do % 2] = b.ev("pe")
                @b.blk.vector
                def _(v, do=do, th=th, tc2=tc2, p=p):
                    b.wait(v, "dve", b.ev("pe"), xev[0])
                    tmp = arena[:, 2048:2560]
                    b.bump("dve", v.tensor_tensor(tmp, p[:],
                                                  SAf[:, do * T + th * 1024 + tc2 * 512: do * T + th * 1024 + tc2 * 512 + 512],
                                                  op=ALU.add))
                    b.fence(v)
                    xs = arena[:, 4096 + (do % 2) * 1024 + tc2 * 512: 4096 + (do % 2) * 1024 + tc2 * 512 + 512]
                    b.bump("dve", v.tensor_tensor(xs, xs, tmp, op=ALU.add))
                    b.fence(v)
                    b.bump("dve", v.tensor_copy(ctx["touch"][:], xs[0:1, 0:8]))
                pk_set(ctx, b, bk, "dve")
            @b.blk.sync
            def _(sy, do=do, th=th):
                b.wait(sy, "dS", b.ev("dve"))
                b.bump("dS", sy.dma_start(
                    out=dst[do * 128:(do + 1) * 128, th * 1024:(th + 1) * 1024],
                    in_=arena[:, 4096 + (do % 2) * 1024:4096 + (do % 2) * 1024 + 1024]), 16)



def emit_top2_gate(ctx, gw_name):
    b = ctx["b"]
    SA, arena, zT, wb = ctx["SA"], ctx["arena"], ctx["zT"], ctx["wb"]
    pD = ctx["pD"]
    gev = [None]
    guard = (b.ev("pe"), b.ev("dve"))
    @b.blk.sync
    def _(sy):
        b.wait(sy, "dS", *guard)
        gev[0] = b.bump("dS", sy.dma_start(
            out=arena[:, 1536:1600].rearrange("p (a c) -> p a c", a=8),
            in_=ctx["IN"][gw_name].rearrange("(a p) f -> p a f", p=128)), 16)
    hlev = [None]
    @b.blk.vector
    def _(v):
        b.wait(v, "dve", gev[0], ctx["wbev"][0], ctx["wbev"][1])
        b.bump("dve", v.tensor_copy(wb[0][:, 0:64], arena[:, 1536:1600]))
        b.fence(v)
        b.bump("dve", v.tensor_tensor(wb[1][:, 0:64], arena[:, 1536:1600], wb[0][:, 0:64],
                                      op=ALU.subtract))
        b.fence(v)
        hlev[0] = b.bump("dve", v.tensor_copy(ctx["touch"][:], wb[1][0:1, 0:8]))
    for tt in range(NT):
        @b.blk.tensor
        def _(t, tt=tt):
            b.wait(t, "pe", hlev[0], b.ev("dve"))
            for dt in range(ND):
                zhi = zT[:, dt * ZSTRIDE + 1 + tt * 128: dt * ZSTRIDE + 1 + tt * 128 + 128]
                zlo = SA[:, dt * T + tt * 128: dt * T + tt * 128 + 128]
                t.matmul(pD[:, 0:8], zhi, wb[0][:, dt * 8:(dt + 1) * 8],
                         start=(dt == 0), stop=False)
                t.matmul(pD[:, 0:8], zhi, wb[1][:, dt * 8:(dt + 1) * 8],
                         start=False, stop=False)
                ee = t.matmul(pD[:, 0:8], zlo, wb[0][:, dt * 8:(dt + 1) * 8],
                              start=False, stop=(dt == ND - 1))
            b.bump("pe", ee)
        ctx["wbev"][0] = b.ev("pe")
        ctx["wbev"][1] = b.ev("pe")
        @b.blk.vector
        def _(v, tt=tt):
            b.wait(v, "dve", b.ev("pe"))
            s8 = arena[:, 1280:1288]
            b.bump("dve", v.tensor_copy(s8, pD[:, 0:8]))
            b.fence(v)
            b.bump("dve", v.reduce_max(arena[:, 1288:1289], s8, axis=AX.X))
            b.fence(v)
            b.bump("dve", v.tensor_scalar(arena[:, 1289:1297], s8, arena[:, 1288:1289],
                                          None, op0=ALU.is_equal))
            b.fence(v)
            b.bump("dve", v.scalar_tensor_tensor(arena[:, 1297:1305], arena[:, 1289:1297],
                                                 -1e30, s8, op0=ALU.mult, op1=ALU.add))
            b.fence(v)
            b.bump("dve", v.reduce_max(arena[:, 1305:1306], arena[:, 1297:1305], axis=AX.X))
            b.fence(v)
            v.tensor_scalar(arena[:, 1306:1314], s8, arena[:, 1305:1306], None,
                            op0=ALU.is_ge)
            v.tensor_scalar(arena[:, 1314:1322], s8, arena[:, 1288:1289], None,
                            op0=ALU.subtract)
            ee = v.tensor_tensor(arena[:, 1322:1323], arena[:, 1305:1306],
                                 arena[:, 1288:1289], op=ALU.subtract)
            b.bump("dve", ee)
        @b.blk.scalar
        def _(s, tt=tt):
            b.wait(s, "act", b.ev("dve"))
            s.activation(arena[:, 1314:1322], arena[:, 1314:1322], AF.Exp,
                         bias=vc(ctx, "k_zero"), scale=1.0)
            ee = s.activation(arena[:, 1322:1323], arena[:, 1322:1323], AF.Exp,
                              bias=vc(ctx, "k_zero"), scale=1.0)
            b.bump("act", ee)
        @b.blk.vector
        def _(v, tt=tt):
            b.wait(v, "dve", b.ev("act"))
            b.bump("dve", v.tensor_scalar(arena[:, 1322:1323], arena[:, 1322:1323], 1.0,
                                          None, op0=ALU.add))
            b.fence(v)
            b.bump("dve", v.reciprocal(arena[:, 1323:1324], arena[:, 1322:1323]))
            b.fence(v)
            b.bump("dve", v.tensor_tensor(arena[:, 1314:1322], arena[:, 1314:1322],
                                          arena[:, 1306:1314], op=ALU.mult))
            b.fence(v)
            ee = v.tensor_scalar(arena[:, 1024 + tt * 8:1024 + tt * 8 + 8],
                                 arena[:, 1314:1322], arena[:, 1323:1324], None,
                                 op0=ALU.mult)
            b.bump("dve", ee)
    return b.ev("dve")



def emit_l0f(ctx, src, dst, scale):
    b = ctx["b"]
    SA, SAf, arena, zT = ctx["SA"], ctx["SAf"], ctx["arena"], ctx["zT"]
    wb, pA, pB = ctx["wb"], ctx["pA"], ctx["pB"]
    emit_ln(ctx, src, rms=False, want_zlo=True)
    gatev = emit_top2_gate(ctx, "l0f_gwT")
    BANKS = ["pA", "pB", "pC", "pD"]
    for e in range(8):
        for fchunk in range(8):
            ci = e * 8 + fchunk
            wev = [None]
            @b.blk.gpsimd
            def _(gp, e=e, fchunk=fchunk, ci=ci):
                b.wait(gp, "pool", ctx["wbev"][ci % 2])
                wev[0] = b.bump("dG0" if ci % 2 == 0 else "dG1", gp.dma_start(
                    out=wb[ci % 2][:, :].rearrange("p (a c) -> p a c", a=8),
                    in_=ctx["IN"]["l0f_w12T"].rearrange("e (a p) f -> e p a f", p=128)[e][:, :, fchunk * 512:(fchunk + 1) * 512]), 16)
            for fi in range(4):
                ft = fchunk * 4 + fi
                for half in range(2):
                    bk0, bk1 = ("pA", "pB") if half == 0 else ("pC", "pD")
                    p0, p1 = ctx[bk0], ctx[bk1]
                    @b.blk.tensor
                    def _(t, e=e, ci=ci, fi=fi, ft=ft, half=half, p0=p0, p1=p1,
                          bk0=bk0, bk1=bk1):
                        b.wait(t, "pe", wev[0], gatev)
                        pk_wait(b, t, ctx, bk0)
                        pk_wait(b, t, ctx, bk1)
                        for dt in range(ND):
                            w_sl = wb[ci % 2][:, dt * 512 + fi * 128: dt * 512 + fi * 128 + 128]
                            t.matmul(p0[:], w_sl, zT[:, zslc(dt, 2 * half)],
                                     start=(dt == 0), stop=(dt == ND - 1))
                            ee = t.matmul(p1[:], w_sl, zT[:, zslc(dt, 2 * half + 1)],
                                          start=(dt == 0), stop=(dt == ND - 1))
                        b.bump("pe", ee)
                    ctx["wbev"][ci % 2] = b.ev("pe")
                    if ft < 16:
                        @b.blk.scalar
                        def _(s, ft=ft, half=half, p0=p0, p1=p1):
                            b.wait(s, "act", b.ev("pe"))
                            s.activation(SA[:, ft * T + half * 1024: ft * T + half * 1024 + 512],
                                         p0[:], AF.Silu, bias=vc(ctx, "k_zero"), scale=1.0)
                            b.bump("act", s.activation(
                                SA[:, ft * T + half * 1024 + 512: ft * T + half * 1024 + 1024],
                                p1[:], AF.Silu, bias=vc(ctx, "k_zero"), scale=1.0))
                        pk_set(ctx, b, bk0, "act")
                        pk_set(ctx, b, bk1, "act")
                    else:
                        @b.blk.vector
                        def _(v, ft=ft, half=half, p0=p0, p1=p1):
                            b.wait(v, "dve", b.ev("pe"), b.ev("act"))
                            hs0 = SA[:, (ft - 16) * T + half * 1024: (ft - 16) * T + half * 1024 + 512]
                            hs1 = SA[:, (ft - 16) * T + half * 1024 + 512: (ft - 16) * T + half * 1024 + 1024]
                            v.tensor_tensor(hs0, hs0, p0[:], op=ALU.mult)
                            b.bump("dve", v.tensor_tensor(hs1, hs1, p1[:], op=ALU.mult))
                        pk_set(ctx, b, bk0, "dve")
                        pk_set(ctx, b, bk1, "dve")
        for dc in range(2):
            wev = [None, None]
            @b.blk.gpsimd
            def _(gp, e=e, dc=dc):
                b.wait(gp, "pool", ctx["wbev"][0], ctx["wbev"][1])
                wev[0] = b.bump("dG0", gp.dma_start(
                    out=wb[0][:, :].rearrange("p (a c) -> p a c", a=8),
                    in_=ctx["IN"]["l0f_w3T"].rearrange("e (a p) f -> e p a f", p=128)[e][:, 0:8, dc * 512:(dc + 1) * 512]), 16)
                wev[1] = b.bump("dG1", gp.dma_start(
                    out=wb[1][:, :].rearrange("p (a c) -> p a c", a=8),
                    in_=ctx["IN"]["l0f_w3T"].rearrange("e (a p) f -> e p a f", p=128)[e][:, 8:16, dc * 512:(dc + 1) * 512]), 16)
            for tt in range(NT):
                bk = BANKS[tt % 4]
                p = ctx[bk]
                @b.blk.tensor
                def _(t, e=e, dc=dc, tt=tt, p=p, bk=bk):
                    b.wait(t, "pe", wev[0], wev[1], b.ev("dve") if tt == 0 and dc == 0 else None)
                    pk_wait(b, t, ctx, bk)
                    for ht in range(16):
                        ee = t.matmul(p[:], SA[:, ht * T + tt * 128: ht * T + tt * 128 + 128],
                                      wb[ht // 8][:, (ht % 8) * 512:(ht % 8) * 512 + 512],
                                      start=(ht == 0), stop=(ht == 15))
                    b.bump("pe", ee)
                ctx["wbev"][0] = b.ev("pe")
                ctx["wbev"][1] = b.ev("pe")
                @b.blk.vector
                def _(v, e=e, dc=dc, tt=tt, p=p):
                    b.wait(v, "dve", b.ev("pe"))
                    ysl = SAf[:, tt * 1024 + dc * 512: tt * 1024 + dc * 512 + 512]
                    ccol = arena[:, 1024 + tt * 8 + e:1024 + tt * 8 + e + 1]
                    if e == 0:
                        ee = v.tensor_scalar(ysl, p[:], ccol, None, op0=ALU.mult)
                    else:
                        tmp = arena[:, 1600 + (tt % 2) * 512:1600 + (tt % 2) * 512 + 512]
                        b.bump("dve", v.tensor_scalar(tmp, p[:], ccol, None, op0=ALU.mult))
                        b.fence(v)
                        ee = v.tensor_tensor(ysl, ysl, tmp, op=ALU.add)
                    b.bump("dve", ee)
                pk_set(ctx, b, "dve" if False else bk, "dve")
    emit_ytok_residual(ctx, src, dst, scale)



def emit_l3f(ctx, src, dst, scale):
    b = ctx["b"]
    SA, SAf, arena, zT = ctx["SA"], ctx["SAf"], ctx["arena"], ctx["zT"]
    wb, pA, pB = ctx["wb"], ctx["pA"], ctx["pB"]
    emit_ln(ctx, src, rms=False, want_zlo=True)
    gatev = emit_top2_gate(ctx, "l3f_rwT")
    BANKS = ["pA", "pB", "pC", "pD"]
    for e in range(8):
        for fh in range(2):
            for fchunk in range(4):
                ci = (e * 2 + fh) * 4 + fchunk
                wev = [None]
                @b.blk.gpsimd
                def _(gp, e=e, fh=fh, fchunk=fchunk, ci=ci):
                    b.wait(gp, "pool", ctx["wbev"][ci % 2])
                    wev[0] = b.bump("dG0" if ci % 2 == 0 else "dG1", gp.dma_start(
                        out=wb[ci % 2][:, :].rearrange("p (a c) -> p a c", a=8),
                        in_=ctx["IN"]["l3f_w1"].rearrange("e (a p) f -> e p a f", p=128)[e][:, :, fh * 2048 + fchunk * 512: fh * 2048 + (fchunk + 1) * 512]), 16)
                for fi in range(4):
                    ft = fchunk * 4 + fi
                    for half in range(2):
                        bk0, bk1 = ("pA", "pB") if half == 0 else ("pC", "pD")
                        p0, p1 = ctx[bk0], ctx[bk1]
                        @b.blk.tensor
                        def _(t, e=e, ci=ci, fi=fi, half=half, p0=p0, p1=p1,
                              bk0=bk0, bk1=bk1):
                            b.wait(t, "pe", wev[0], gatev)
                            pk_wait(b, t, ctx, bk0)
                            pk_wait(b, t, ctx, bk1)
                            for dt in range(ND):
                                w_sl = wb[ci % 2][:, dt * 512 + fi * 128: dt * 512 + fi * 128 + 128]
                                t.matmul(p0[:], w_sl, zT[:, zslc(dt, 2 * half)],
                                         start=(dt == 0), stop=(dt == ND - 1))
                                ee = t.matmul(p1[:], w_sl, zT[:, zslc(dt, 2 * half + 1)],
                                              start=(dt == 0), stop=(dt == ND - 1))
                            b.bump("pe", ee)
                        ctx["wbev"][ci % 2] = b.ev("pe")
                        @b.blk.scalar
                        def _(s, ft=ft, half=half, p0=p0, p1=p1):
                            b.wait(s, "act", b.ev("pe"))
                            s.activation(SA[:, ft * T + half * 1024: ft * T + half * 1024 + 512],
                                         p0[:], AF.Silu, bias=vc(ctx, "k_zero"), scale=1.0)
                            b.bump("act", s.activation(
                                SA[:, ft * T + half * 1024 + 512: ft * T + half * 1024 + 1024],
                                p1[:], AF.Silu, bias=vc(ctx, "k_zero"), scale=1.0))
                        pk_set(ctx, b, bk0, "act")
                        pk_set(ctx, b, bk1, "act")
            for dc in range(2):
                wev = [None, None]
                @b.blk.gpsimd
                def _(gp, e=e, fh=fh, dc=dc):
                    b.wait(gp, "pool", ctx["wbev"][0], ctx["wbev"][1])
                    wev[0] = b.bump("dG0", gp.dma_start(
                        out=wb[0][:, :].rearrange("p (a c) -> p a c", a=8),
                        in_=ctx["IN"]["l3f_w2"].rearrange("e (a p) f -> e p a f", p=128)[e][:, fh * 16:fh * 16 + 8, dc * 512:(dc + 1) * 512]), 16)
                    wev[1] = b.bump("dG1", gp.dma_start(
                        out=wb[1][:, :].rearrange("p (a c) -> p a c", a=8),
                        in_=ctx["IN"]["l3f_w2"].rearrange("e (a p) f -> e p a f", p=128)[e][:, fh * 16 + 8:fh * 16 + 16, dc * 512:(dc + 1) * 512]), 16)
                for tt in range(NT):
                    bk = BANKS[tt % 4]
                    p = ctx[bk]
                    @b.blk.tensor
                    def _(t, e=e, dc=dc, tt=tt, p=p, bk=bk):
                        b.wait(t, "pe", wev[0], wev[1],
                               b.ev("act") if tt == 0 and dc == 0 else None)
                        pk_wait(b, t, ctx, bk)
                        for ht in range(16):
                            ee = t.matmul(p[:], SA[:, ht * T + tt * 128: ht * T + tt * 128 + 128],
                                          wb[ht // 8][:, (ht % 8) * 512:(ht % 8) * 512 + 512],
                                          start=(ht == 0), stop=(ht == 15))
                        b.bump("pe", ee)
                    ctx["wbev"][0] = b.ev("pe")
                    ctx["wbev"][1] = b.ev("pe")
                    @b.blk.vector
                    def _(v, e=e, fh=fh, dc=dc, tt=tt, p=p):
                        b.wait(v, "dve", b.ev("pe"))
                        ysl = SAf[:, tt * 1024 + dc * 512: tt * 1024 + dc * 512 + 512]
                        ccol = arena[:, 1024 + tt * 8 + e:1024 + tt * 8 + e + 1]
                        if e == 0 and fh == 0:
                            ee = v.tensor_scalar(ysl, p[:], ccol, None, op0=ALU.mult)
                        else:
                            tmp = arena[:, 1600 + (tt % 2) * 512:1600 + (tt % 2) * 512 + 512]
                            b.bump("dve", v.tensor_scalar(tmp, p[:], ccol, None, op0=ALU.mult))
                            b.fence(v)
                            ee = v.tensor_tensor(ysl, ysl, tmp, op=ALU.add)
                        b.bump("dve", ee)
                    pk_set(ctx, b, bk, "dve")
    emit_ytok_residual(ctx, src, dst, scale, extra_z=True)





CAP3 = 768
BIGIDX = 4.0e9

def emit_l3f_sparse(ctx, src, dst, scale):
    b = ctx["b"]
    SA, SAf, arena, zT, wb = ctx["SA"], ctx["SAf"], ctx["arena"], ctx["zT"], ctx["wb"]
    pA, pB, pC, pD, pTb = ctx["pA"], ctx["pB"], ctx["pC"], ctx["pD"], ctx["pTb"]
    su32 = ctx["su32"]
    l3tab, zrows, ctab, yhbm = ctx["l3tab"], ctx["zrows"], ctx["ctab"], ctx["yhbm"]
    BANKS = ["pA", "pB", "pC", "pD"]
    emit_ln(ctx, src, rms=False, want_zlo=True)
    gatev = emit_top2_gate(ctx, "l3f_rwT")
    cev = [None]
    @b.blk.sync
    def _(sy):
        b.wait(sy, "dS", gatev, b.ev("dS"))
        b.bump("dS", sy.dma_start(out=arena[:, 1160:1161], in_=ctx["IN"]["c_iota"]), 16)
        b.bump("dS", sy.dma_start(out=arena[:, 1162:1170], in_=ctx["IN"]["c_eoff"]), 16)

    @b.blk.vector
    def _(v):
        b.wait(v, "dve", gatev)
        b.bump("dve", v.memset(arena[:, 1536:1632], BIGIDX))
        b.bump("dve", v.memset(arena[:, 0:1024], 0.0))
        b.bump("dve", v.memset(arena[32:40, 2048:4096], 1.0))
        b.fence(v)
        b.bump("dve", v.tensor_copy(ctx["touch"][:], arena[0:1, 1536:1544]))
    @b.blk.sync
    def _(sy):
        b.wait(sy, "dS", b.ev("dve"))
        b.bump("dS", sy.dma_start(out=l3tab.rearrange("(a c) k -> a (c k)", a=128),
                                  in_=arena[:, 1536:1632]), 16)
        for r in range(NT):
            b.bump("dS", sy.dma_start(out=yhbm[r * 128:(r + 1) * 128, :],
                                      in_=arena[:, 0:1024]), 16)
    cev[0] = b.ev("dS")
    for tt in range(NT):
        @b.blk.vector
        def _(v, tt=tt):
            b.wait(v, "dve", b.ev("dve"), b.ev("dS"))
            b.bump("dve", v.tensor_scalar(arena[:, 1170:1178],
                                          arena[:, 1024 + tt * 8:1024 + tt * 8 + 8],
                                          0.0, None, op0=ALU.is_gt))
            b.fence(v)
            b.bump("dve", v.tensor_copy(ctx["touch"][:], arena[0:1, 1170:1178]))
        @b.blk.tensor
        def _(t, tt=tt):
            b.wait(t, "pe", b.ev("dve"))
            b.bump("pe", t.transpose(pD[0:8, 0:128], arena[:, 1170:1178], ctx["ident"][:]))
        @b.blk.vector
        def _(v, tt=tt):
            b.wait(v, "dve", b.ev("pe"))
            b.bump("dve", v.tensor_copy(arena[32:40, 4096 + tt * 128:4096 + (tt + 1) * 128],
                                        pD[0:8, 0:128]))
    @b.blk.vector
    def _(v):
        b.wait(v, "dve", b.ev("dve"))
        b.bump("dve", v.tensor_tensor_scan(arena[0:8, 2048:4096], arena[32:40, 2048:4096],
                                           arena[32:40, 4096:6144], 0.0,
                                           op0=ALU.mult, op1=ALU.add))
        b.fence(v)
        b.bump("dve", v.tensor_copy(ctx["touch"][:], arena[0:1, 2048:2056]))
    rankev = b.ev("dve")
    for tt in range(NT):
        @b.blk.tensor
        def _(t, tt=tt):
            b.wait(t, "pe", b.ev("dve"), b.ev("act"))
            for dt in range(ND):
                ee = t.transpose(pTb[:, dt * 128:(dt + 1) * 128],
                                 zT[:, dt * ZSTRIDE + 1 + tt * 128: dt * ZSTRIDE + 1 + tt * 128 + 128],
                                 ctx["identb"][:])
            b.bump("pe", ee)
        @b.blk.vector
        def _(v, tt=tt):
            b.wait(v, "dve", b.ev("pe"), b.ev("dS"))
            b.bump("dve", v.tensor_copy(SA[:, 0:1024], pTb[:, :]))
            b.fence(v)
            b.bump("dve", v.tensor_copy(ctx["touch"][:], SA[0:1, 0:8]))
        @b.blk.sync
        def _(sy, tt=tt):
            b.wait(sy, "dS", b.ev("dve"))
            b.bump("dS", sy.dma_start(out=zrows[tt * 128:(tt + 1) * 128, :],
                                      in_=SA[:, 0:1024]), 16)
    for tt in range(NT):
        @b.blk.tensor
        def _(t, tt=tt):
            b.wait(t, "pe", rankev, cev[0], b.ev("dve"))
            b.bump("pe", t.transpose(pD[:, 0:8],
                                     arena[0:8, 2048 + tt * 128:2048 + (tt + 1) * 128],
                                     ctx["ident"][0:8, 0:8]))
        @b.blk.vector
        def _(v, tt=tt):
            b.wait(v, "dve", b.ev("pe"), b.ev("dG"))
            b.bump("dve", v.tensor_copy(arena[:, 1178:1186], pD[:, 0:8]))
            kb = arena[:, 1170:1178]
            b.bump("dve", v.tensor_scalar(kb, arena[:, 1024 + tt * 8:1024 + tt * 8 + 8],
                                          0.0, None, op0=ALU.is_gt))
            b.fence(v)
            b.bump("dve", v.tensor_tensor(arena[:, 1186:1194], arena[:, 1178:1186],
                                          arena[:, 1162:1170], op=ALU.add))
            b.fence(v)
            b.bump("dve", v.tensor_tensor(arena[:, 1194:1202], arena[:, 1186:1194], kb,
                                          op=ALU.mult))
            b.bump("dve", v.tensor_scalar(arena[:, 1202:1210], kb, -BIGIDX, BIGIDX,
                                          op0=ALU.mult, op1=ALU.add))
            b.fence(v)
            b.bump("dve", v.tensor_tensor(arena[:, 1194:1202], arena[:, 1194:1202],
                                          arena[:, 1202:1210], op=ALU.add))
            b.fence(v)
            b.bump("dve", v.tensor_scalar(arena[:, 1202:1210], arena[:, 1178:1186],
                                          float(CAP3), None, op0=ALU.is_gt))
            b.fence(v)
            b.bump("dve", v.scalar_tensor_tensor(arena[:, 1194:1202], arena[:, 1202:1210],
                                                 BIGIDX, arena[:, 1194:1202],
                                                 op0=ALU.mult, op1=ALU.add))
            b.fence(v)
            b.bump("dve", v.tensor_scalar(arena[:, 1194:1202], arena[:, 1194:1202],
                                          BIGIDX, None, op0=ALU.min))
            b.fence(v)
            b.bump("dve", v.tensor_copy(su32[:, 0:8], arena[:, 1194:1202]))
            b.bump("dve", v.tensor_scalar(arena[:, 1210:1211], arena[:, 1160:1161],
                                          float(tt * 128), None, op0=ALU.add))
            b.fence(v)
            for e in range(8):
                b.bump("dve", v.tensor_copy(arena[:, 1212 + 2 * e:1213 + 2 * e],
                                            arena[:, 1210:1211]))
                b.bump("dve", v.tensor_copy(arena[:, 1213 + 2 * e:1214 + 2 * e],
                                            arena[:, 1024 + tt * 8 + e:1024 + tt * 8 + e + 1]))
            b.fence(v)
            b.bump("dve", v.tensor_copy(ctx["touch"][:], arena[0:1, 1194:1202]))
        @b.blk.gpsimd
        def _(gp, tt=tt):
            b.wait(gp, "pool", b.ev("dve"), b.ev("dS"))
            rb = ctx["rb6143"]
            for e in range(8):
                b.bump("dG", gp.indirect_dma_start(
                    l3tab, bass.IndirectOffsetOnAxis(su32[:, e:e + 1], 0),
                    arena[:, 1212 + 2 * e:1214 + 2 * e], None,
                    bounds_check=rb, oob_is_err=False), 16)
    scatev = b.ev("dG")
    for e in range(8):
        tev = [None]
        @b.blk.sync
        def _(sy, e=e):
            b.wait(sy, "dS", scatev, b.ev("dve") if e == 0 else b.ev("dG"))
            tev[0] = b.bump("dS", sy.dma_start(
                out=arena[0:6, 1280:1536].rearrange("p (a k) -> p a k", k=2),
                in_=l3tab.rearrange("(q s p) k -> q s p k", q=8, s=6)[e]), 16)
        @b.blk.tensor
        def _(t, e=e):
            b.wait(t, "pe", tev[0], b.ev("dve"))
            tidv = arena[0:6, 1280:1536].rearrange("p (a k) -> p a k", k=2)
            t.transpose(pD[:, 0:6], tidv[:, :, 0], ctx["ident"][0:6, 0:6])
            b.bump("pe", t.transpose(pD[:, 8:14], tidv[:, :, 1], ctx["ident"][0:6, 0:6]))
        @b.blk.vector
        def _(v, e=e):
            b.wait(v, "dve", b.ev("pe"))
            b.bump("dve", v.tensor_copy(arena[:, 1408:1414], pD[:, 0:6]))
            b.bump("dve", v.tensor_copy(arena[:, 1416:1422], pD[:, 8:14]))
            b.fence(v)
            b.bump("dve", v.tensor_copy(su32[:, 8:14], arena[:, 1408:1414]))
            b.fence(v)
            b.bump("dve", v.tensor_copy(ctx["touch"][:], arena[0:1, 1408:1416]))
        @b.blk.gpsimd
        def _(gp, e=e):
            b.wait(gp, "pool", b.ev("dve"), b.ev("pe"))
            rb = ctx["rb6143"]
            for s in range(6):
                b.bump("dG", gp.indirect_dma_start(
                    SA[:, s * 1024:(s + 1) * 1024], None,
                    zrows, bass.IndirectOffsetOnAxis(su32[:, 8 + s:9 + s], 0),
                    bounds_check=rb, oob_is_err=False), 16)
        gthev = b.ev("dG")
        for s in range(6):
            @b.blk.tensor
            def _(t, e=e, s=s):
                b.wait(t, "pe", gthev, b.ev("dve"))
                for dt in range(ND):
                    ee = t.transpose(pTb[:, dt * 128:(dt + 1) * 128],
                                     SA[:, s * 1024 + dt * 128: s * 1024 + dt * 128 + 128],
                                     ctx["identb"][:])
                b.bump("pe", ee)
            @b.blk.vector
            def _(v, e=e, s=s):
                b.wait(v, "dve", b.ev("pe"))
                for dt in range(ND):
                    ee = v.tensor_copy(SA[:, 8192 + dt * 768 + s * 128: 8192 + dt * 768 + s * 128 + 128],
                                       pTb[:, dt * 128:(dt + 1) * 128])
                b.bump("dve", ee)
        zgev = b.ev("dve")
        for fh in range(2):
            for fchunk in range(4):
                ci = ctx.get("wb_ci", 0); ctx["wb_ci"] = ci + 1
                wev = [None]
                @b.blk.gpsimd
                def _(gp, e=e, fh=fh, fchunk=fchunk, ci=ci):
                    b.wait(gp, "pool", ctx["wbev"][ci % 2])
                    wev[0] = b.bump("dG0" if ci % 2 == 0 else "dG1", gp.dma_start(
                        out=wb[ci % 2][:, :].rearrange("p (a c) -> p a c", a=8),
                        in_=ctx["IN"]["l3f_w1"].rearrange("e (a p) f -> e p a f", p=128)[e][:, :, fh * 2048 + fchunk * 512: fh * 2048 + (fchunk + 1) * 512]), 16)
                for fi in range(4):
                    ft = fchunk * 4 + fi
                    for tq, tw in ((0, 512), (512, 256)):
                        bk = BANKS[(fi * 2 + (0 if tq == 0 else 1)) % 4]
                        p = ctx[bk]
                        @b.blk.tensor
                        def _(t, e=e, ci=ci, fi=fi, tq=tq, tw=tw, p=p, bk=bk):
                            b.wait(t, "pe", wev[0], zgev)
                            pk_wait(b, t, ctx, bk)
                            for dt in range(ND):
                                ee = t.matmul(p[:, 0:tw],
                                              wb[ci % 2][:, dt * 512 + fi * 128: dt * 512 + fi * 128 + 128],
                                              SA[:, 8192 + dt * 768 + tq: 8192 + dt * 768 + tq + tw],
                                              start=(dt == 0), stop=(dt == ND - 1))
                            b.bump("pe", ee)
                        ctx["wbev"][ci % 2] = b.ev("pe")
                        @b.blk.scalar
                        def _(s_, ft=ft, tq=tq, tw=tw, p=p):
                            b.wait(s_, "act", b.ev("pe"))
                            b.bump("act", s_.activation(
                                SA[:, 16384 + ft * 768 + tq: 16384 + ft * 768 + tq + tw],
                                p[:, 0:tw], AF.Silu, bias=vc(ctx, "k_zero"), scale=1.0))
                        pk_set(ctx, b, bk, "act")
            for dc in range(2):
                wev = [None, None]
                @b.blk.gpsimd
                def _(gp, e=e, fh=fh, dc=dc):
                    b.wait(gp, "pool", ctx["wbev"][0], ctx["wbev"][1])
                    wev[0] = b.bump("dG0", gp.dma_start(
                        out=wb[0][:, :].rearrange("p (a c) -> p a c", a=8),
                        in_=ctx["IN"]["l3f_w2"].rearrange("e (a p) f -> e p a f", p=128)[e][:, fh * 16:fh * 16 + 8, dc * 512:(dc + 1) * 512]), 16)
                    wev[1] = b.bump("dG1", gp.dma_start(
                        out=wb[1][:, :].rearrange("p (a c) -> p a c", a=8),
                        in_=ctx["IN"]["l3f_w2"].rearrange("e (a p) f -> e p a f", p=128)[e][:, fh * 16 + 8:fh * 16 + 16, dc * 512:(dc + 1) * 512]), 16)
                for s in range(6):
                    bk = BANKS[(dc * 6 + s) % 4]
                    p = ctx[bk]
                    @b.blk.tensor
                    def _(t, e=e, fh=fh, dc=dc, s=s, p=p, bk=bk):
                        b.wait(t, "pe", wev[0], wev[1],
                               b.ev("act") if s == 0 and dc == 0 else None)
                        pk_wait(b, t, ctx, bk)
                        for ht in range(16):
                            ee = t.matmul(p[:], SA[:, 16384 + ht * 768 + s * 128: 16384 + ht * 768 + s * 128 + 128],
                                          wb[ht // 8][:, (ht % 8) * 512:(ht % 8) * 512 + 512],
                                          start=(ht == 0), stop=(ht == 15))
                        b.bump("pe", ee)
                    ctx["wbev"][0] = b.ev("pe")
                    ctx["wbev"][1] = b.ev("pe")
                    @b.blk.vector
                    def _(v, e=e, fh=fh, dc=dc, s=s, p=p):
                        b.wait(v, "dve", b.ev("pe"), gthev)
                        ysl = SAf[:, s * 1024 + dc * 512: s * 1024 + dc * 512 + 512]
                        ccol = arena[:, 1416 + s:1417 + s]
                        if fh == 0:
                            ee = v.tensor_scalar(ysl, p[:], ccol, None, op0=ALU.mult)
                        else:
                            tmp = arena[:, 1464 + (s % 2) * 512:1464 + (s % 2) * 512 + 512]
                            b.bump("dve", v.tensor_scalar(tmp, p[:], ccol, None, op0=ALU.mult))
                            b.fence(v)
                            ee = v.tensor_tensor(ysl, ysl, tmp, op=ALU.add)
                        b.bump("dve", ee)
                    pk_set(ctx, b, bk, "dve")
        @b.blk.gpsimd
        def _(gp, e=e):
            b.wait(gp, "pool", b.ev("dve"))
            rb = ctx["rb6143"]
            for s in range(6):
                b.bump("dG", gp.indirect_dma_start(
                    yhbm, bass.IndirectOffsetOnAxis(su32[:, 8 + s:9 + s], 0),
                    SAf[:, s * 1024:(s + 1) * 1024], None,
                    bounds_check=rb, oob_is_err=False, compute_op=ALU.add), 16)
    @b.blk.sync
    def _(sy):
        b.wait(sy, "dS", b.ev("dG"), b.ev("dve"), b.ev("dS"))
        for tt in range(NT):
            b.bump("dS", sy.dma_start(out=SAf[:, tt * 1024:(tt + 1) * 1024],
                                      in_=yhbm[tt * 128:(tt + 1) * 128, :]), 16)
    @b.blk.vector
    def _(v):
        b.wait(v, "dve", b.ev("dS"))
        b.bump("dve", v.tensor_copy(ctx["touch"][:], SAf[0:1, 0:8]))
    emit_ytok_residual(ctx, src, dst, scale, extra_z=True)


EMITTERS = {"l2f": emit_l2f, "l1f": emit_l1f, "l0f": emit_l0f, "l3f": emit_l3f_sparse,
            "l3a": emit_l3a, "l1a": emit_l1a, "l2a": emit_l2a, "l0a": emit_l0a}



def kernel(**inputs):
    maps = host_prepare(inputs)
    nc = build_program(STEP_SEQ)
    res = run_bass_kernel_spmd(nc, maps, core_ids=list(range(8)))
    out = np.stack([np.asarray(r["yT"], dtype=np.float32).T for r in res.results])
    return np.ascontiguousarray(out)
